# version 14
# baseline (speedup 1.0000x reference)
from contextlib import ExitStack

import numpy as np
import concourse.bass as bass
import concourse.mybir as mybir
from concourse.bass_utils import run_bass_kernel_spmd

F32 = mybir.dt.float32
BF16 = mybir.dt.bfloat16
I16 = mybir.dt.int16
U8 = mybir.dt.uint8
AF = mybir.ActivationFunctionType
ALU = mybir.AluOpType
AX = mybir.AxisListType

D = 1024
INNER = 4096
H = 16
DH = 64
SEQ = 2048
NSEQ = 2
NTOK = NSEQ * SEQ
TT = 512
NTILE = NTOK // TT
TC = TT // 128
EPS = 1e-5
NKV = 32
C32M = 12582912.0
C52M = 6755399441055744.0

ENGS = ("sync", "vector", "scalar", "tensor", "gpsimd")

STAGE_FFN1 = True
STAGE_ATTN = True
STAGE_FFN2 = True


class Builder:
    def __init__(self):
        self.nc = bass.Bass("TRN2", target_bir_lowering=False)
        self.ctx = ExitStack()
        self.q = {e: [] for e in ENGS}
        self.sems = {}
        self.semctr = {}
        self.last_tick = {e: None for e in ENGS}

    def sem(self, name):
        if name not in self.sems:
            self.sems[name] = self.ctx.enter_context(self.nc.semaphore(name))
            self.semctr[name] = 0
        return self.sems[name]

    def semval(self, name):
        self.sem(name)
        return self.semctr[name]

    def emit(self, eng, fn, tick=None, tickn=1):
        if tick is None:
            self.q[eng].append(fn)
            self.last_tick[eng] = None
            return None
        h = self.sem(tick)
        self.semctr[tick] += tickn
        val = self.semctr[tick]
        self.q[eng].append(lambda E, f=fn, h=h, n=tickn: f(E).then_inc(h, n))
        self.last_tick[eng] = (tick, val)
        return val

    def tick_last(self, eng, name, n=1):
        if self.last_tick[eng] is not None:
            return self.emit(eng, lambda E: E.nop(), tick=name, tickn=n)
        h = self.sem(name)
        self.semctr[name] += n
        val = self.semctr[name]
        last = self.q[eng][-1]
        assert not isinstance(last, tuple)
        self.q[eng][-1] = (lambda E, f=last, h=h, m=n: f(E).then_inc(h, m))
        self.last_tick[eng] = (name, val)
        return val

    def wait(self, eng, name, val):
        if val <= 0:
            self.sem(name)
            return
        h = self.sem(name)
        self.q[eng].append(("w", h, val))
        self.last_tick[eng] = None

    def link(self, eng):
        if self.last_tick[eng] is not None:
            name, v = self.last_tick[eng]
            self.wait(eng, name, v)
            return
        name = f"_lk_{eng}"
        v = self.tick_last(eng, name)
        self.wait(eng, name, v)

    def dma(self, out, in_, tick, eng="sync"):
        return self.emit(eng, lambda E, o=out, i=in_: E.dma_start(out=o, in_=i),
                         tick=tick, tickn=16)

    def run_block(self):
        block = self.ctx.enter_context(self.nc.Block())

        def mk(engname):
            def body(E):
                for item in self.q[engname]:
                    if isinstance(item, tuple):
                        _, h, val = item
                        E.wait_ge(h, val)
                    else:
                        item(E)
            return body

        block.sync(mk("sync"))
        block.vector(mk("vector"))
        block.scalar(mk("scalar"))
        block.tensor(mk("tensor"))
        block.gpsimd(mk("gpsimd"))


def build_program():
    b = Builder()
    nc = b.nc
    e = b.ctx.enter_context

    x0 = nc.dram_tensor("x0", [NTOK, D], F32, kind="ExternalInput").ap()
    wq_f = nc.dram_tensor("wq", [D, D], F32, kind="ExternalInput").ap()
    wk_f = nc.dram_tensor("wk", [D, D], F32, kind="ExternalInput").ap()
    wv_f = nc.dram_tensor("wv", [D, D], F32, kind="ExternalInput").ap()
    wo_f = nc.dram_tensor("wo", [D, D], F32, kind="ExternalInput").ap()
    wgT = nc.dram_tensor("wgT", [D, INNER], F32, kind="ExternalInput").ap()
    wuT = nc.dram_tensor("wuT", [D, INNER], F32, kind="ExternalInput").ap()
    wdT = nc.dram_tensor("wdT", [INNER, D], F32, kind="ExternalInput").ap()
    consts = nc.dram_tensor("consts", [128, 8], F32, kind="ExternalInput").ap()
    indic = nc.dram_tensor("indic", [64, SEQ], BF16, kind="ExternalInput").ap()
    ident = nc.dram_tensor("ident", [128, 128], BF16, kind="ExternalInput").ap()
    identf = nc.dram_tensor("identf", [128, 128], F32, kind="ExternalInput").ap()
    y = nc.dram_tensor("y", [NTOK, D], F32, kind="ExternalOutput").ap()

    x1d = nc.dram_tensor("x1d", [NTOK, D], F32, kind="Internal").ap()
    x2d = nc.dram_tensor("x2d", [NTOK, D], F32, kind="Internal").ap()
    wg_q = nc.dram_tensor("wg_q", [D, INNER], BF16, kind="Internal").ap()
    wu_q = nc.dram_tensor("wu_q", [D, INNER], BF16, kind="Internal").ap()
    wd_q = nc.dram_tensor("wd_q", [INNER, D], BF16, kind="Internal").ap()
    wq_b = nc.dram_tensor("wq_b", [D, D], BF16, kind="Internal").ap()
    wk_b = nc.dram_tensor("wk_b", [D, D], BF16, kind="Internal").ap()
    wv_b = nc.dram_tensor("wv_b", [D, D], BF16, kind="Internal").ap()
    wo_b = nc.dram_tensor("wo_b", [D, D], BF16, kind="Internal").ap()

    cst = e(nc.sbuf_tensor("cst", [128, 8], F32))
    nc.const_aps.register(F32, 256.0, cst[:, 0:1])
    nc.const_aps.register(F32, EPS, cst[:, 1:2])
    nc.const_aps.register(F32, 0.125, cst[:, 2:3])
    ones_r = e(nc.sbuf_tensor("ones_r", [1, 128], F32))
    ones_c = e(nc.sbuf_tensor("ones_c", [128, 1], F32))
    mu = e(nc.sbuf_tensor("mu", [128, 4], F32))
    idt = e(nc.sbuf_tensor("idt", [128, 128], BF16))
    idtf = e(nc.sbuf_tensor("idtf", [128, 128], F32))
    stats = e(nc.sbuf_tensor("stats", [128, 72], F32))
    dump = e(nc.sbuf_tensor("dump", [128, D], F32))

    from types import SimpleNamespace

    def mkscope():
        return ExitStack()

    _alloc_n = [0]

    def alloc(scope, shape, dt):
        _alloc_n[0] += 1
        t = scope.enter_context(
            nc.sbuf_tensor(f"t{_alloc_n[0]}", [128] + list(shape), dt))
        return t[:]

    def barrier(done=()):
        vals = {}
        for eng in ("sync", "vector", "scalar", "tensor", "gpsimd"):
            vals[eng] = b.emit(eng, lambda E: E.nop(), tick=f"ph_{eng}")
        for eng in ("sync", "vector", "scalar", "tensor", "gpsimd"):
            for oeng, v in vals.items():
                if oeng != eng:
                    b.wait(eng, f"ph_{oeng}", v)
            for nm, v in done:
                b.wait(eng, nm, v)

    ps = [e(nc.psum_tensor(f"psbank{i}", [128, 512], F32)) for i in range(8)]
    psb = [p[:].bitcast(BF16) for p in ps]

    bank_mm = [0] * 8
    bank_ev = [0] * 8

    def mm_group(matmuls, bank):
        b.wait("tensor", f"bev{bank}", bank_ev[bank])
        for m in matmuls[:-1]:
            b.emit("tensor", m)
        v = b.emit("tensor", matmuls[-1], tick=f"bmm{bank}")
        bank_mm[bank] = v
        return v

    def evac(eng, fn, bank):
        b.wait(eng, f"bmm{bank}", bank_mm[bank])
        v = b.emit(eng, fn, tick=f"bev{bank}")
        bank_ev[bank] = v
        return v

    b.dma(cst[:], consts[:], "cld")
    b.dma(idt[:], ident[:], "cld")
    b.dma(idtf[:], identf[:], "cld")
    for eng in ("vector", "scalar", "tensor", "gpsimd"):
        b.wait(eng, "cld", 48)
    b.emit("vector", lambda E: E.memset(ones_r[:], 1.0))
    vo = b.emit("vector", lambda E: E.memset(ones_c[:], 1.0), tick="vones")
    b.wait("tensor", "vones", vo)
    b.emit("vector", lambda E: E.memset(cst[:, 3:4], 0.75))
    b.link("vector")
    b.emit("vector", lambda E: E.tensor_scalar(
        cst[:, 4:5], cst[:, 3:4], C52M, None, ALU.add))
    b.link("vector")
    b.emit("vector", lambda E: E.tensor_scalar(
        cst[:, 4:5], cst[:, 4:5], -C52M, None, ALU.add))
    b.link("vector")
    b.emit("vector", lambda E: E.tensor_scalar(
        cst[:, 5:6], cst[:, 4:5], -1.0, 1.0, ALU.mult, ALU.add))
    b.link("vector")
    b.emit("vector", lambda E: E.tensor_scalar(
        cst[:, 6:7], cst[:, 4:5], -500.0, None, ALU.mult), tick="wcst")

    wq_scope = mkscope()
    awbuf = alloc(wq_scope, [4, D], BF16)
    for (src, dst) in ((wq_f, wq_b), (wk_f, wk_b), (wv_f, wv_b), (wo_f, wo_b)):
        for ch in range(2):
            seg = src[ch * 512:(ch + 1) * 512, :].rearrange("(c p) f -> p c f", p=128)
            lv = b.dma(awbuf, seg, "awld", eng="gpsimd")
            b.wait("sync", "awld", lv)
            dseg = dst[ch * 512:(ch + 1) * 512, :].rearrange("(c p) f -> p c f", p=128)
            sv = b.dma(dseg, awbuf, "awst")
            b.wait("gpsimd", "awst", sv)

    wqin8 = alloc(wq_scope, [8, INNER], F32)
    wqin32 = wqin8.rearrange("p a b -> p (a b)").rearrange(
        "p (a b) -> p a b", a=32)
    wqv = alloc(wq_scope, [INNER], F32)
    wqq0 = alloc(wq_scope, [INNER], I16)
    wqr = alloc(wq_scope, [INNER], BF16)
    wqhi = alloc(wq_scope, [INNER], F32)
    wqleaf = alloc(wq_scope, [1024], F32)

    def wquant(w_src, w_dst, nrow, ncol, mucol):
        nch = nrow // 128
        wqin = wqin8 if nch == 8 else wqin32
        b.wait("vector", "bmm0", bank_mm[0])
        b.wait("vector", "bmm1", bank_mm[1])
        b.wait("sync", "awst", b.semval("awst"))
        b.wait("sync", "wq_v", b.semval("wq_v"))
        b.wait("sync", "wq_s", b.semval("wq_s"))
        lv = b.dma(wqin[:, 0:nch, 0:ncol],
                   w_src[:].rearrange("(c p) f -> p c f", p=128), "wqld")
        b.wait("vector", "wqld", lv)
        b.wait("scalar", "wqld", lv)
        MC = float((1 << 23) + (1 << 22))
        nl = ncol // 128
        b.emit("vector", lambda E: E.memset(stats[:, 43:44], 0.0))
        b.link("vector")
        for c in range(nch):
            b.wait("scalar", "wq_v", b.semval("wq_v"))
            b.emit("scalar", lambda E, c=c, n=ncol: E.activation(
                wqv[:, 0:n], wqin[:, c, 0:n], AF.Abs), tick="wq_s")
            b.wait("vector", "wq_s", b.semval("wq_s"))
            b.emit("vector", lambda E, n=ncol: E.tensor_scalar(
                wqhi[:, 0:n], wqv[:, 0:n], float(2 ** 20), MC, ALU.mult,
                ALU.add))
            b.link("vector")
            b.emit("vector", lambda E, n=ncol: E.tensor_scalar(
                wqhi[:, 0:n], wqhi[:, 0:n], -MC, None, ALU.add))
            b.link("vector")
            b.emit("vector", lambda E, n=ncol: E.scalar_tensor_tensor(
                wqv[:, 0:n], wqv[:, 0:n], float(2 ** 20), wqhi[:, 0:n],
                ALU.mult, ALU.subtract))
            b.link("vector")
            b.emit("vector", lambda E, n=ncol: E.tensor_reduce(
                stats[:, 42:43], wqv[:, 0:n], AX.X, ALU.add))
            b.link("vector")
            b.emit("vector", lambda E: E.tensor_add(
                stats[:, 43:44], stats[:, 43:44], stats[:, 42:43]))
            b.link("vector")
            b.emit("vector", lambda E, c=c, n=ncol, nl=nl: E.tensor_reduce(
                wqleaf[:, c * nl:(c + 1) * nl],
                wqhi[:, 0:n].rearrange("p (a x) -> p a x", x=128),
                AX.X, ALU.add), tick="wq_v")
        nlf = nch * nl
        b.wait("vector", "wq_v", b.semval("wq_v"))
        b.emit("vector", lambda E, m=nlf: E.tensor_scalar(
            wqhi[:, 0:m], wqleaf[:, 0:m], float(2 ** -10), MC, ALU.mult,
            ALU.add))
        b.link("vector")
        b.emit("vector", lambda E, m=nlf: E.tensor_scalar(
            wqhi[:, 0:m], wqhi[:, 0:m], -MC, None, ALU.add))
        b.link("vector")
        b.emit("vector", lambda E, m=nlf: E.scalar_tensor_tensor(
            wqleaf[:, 0:m], wqhi[:, 0:m], float(-(2 ** 10)), wqleaf[:, 0:m],
            ALU.mult, ALU.add))
        b.link("vector")
        b.emit("vector", lambda E, m=nlf: E.tensor_reduce(
            stats[:, 36:37], wqhi[:, 0:m], AX.X, ALU.add))
        b.emit("vector", lambda E, m=nlf: E.tensor_reduce(
            stats[:, 37:38], wqleaf[:, 0:m], AX.X, ALU.add))
        b.link("vector")
        b.emit("vector", lambda E: E.tensor_scalar(
            stats[:, 38:39], stats[:, 36:37], float(2 ** -9), MC, ALU.mult,
            ALU.add))
        b.emit("vector", lambda E: E.tensor_scalar(
            stats[:, 40:41], stats[:, 37:38], float(2 ** -5), MC, ALU.mult,
            ALU.add))
        b.link("vector")
        b.emit("vector", lambda E: E.tensor_scalar(
            stats[:, 38:39], stats[:, 38:39], -MC, None, ALU.add))
        b.emit("vector", lambda E: E.tensor_scalar(
            stats[:, 40:41], stats[:, 40:41], -MC, None, ALU.add))
        b.link("vector")
        b.emit("vector", lambda E: E.scalar_tensor_tensor(
            stats[:, 39:40], stats[:, 38:39], float(-(2 ** 9)),
            stats[:, 36:37], ALU.mult, ALU.add))
        b.emit("vector", lambda E: E.scalar_tensor_tensor(
            stats[:, 41:42], stats[:, 40:41], float(-(2 ** 5)),
            stats[:, 37:38], ALU.mult, ALU.add), tick="wq_v")
        b.wait("tensor", "wq_v", b.semval("wq_v"))
        mm_group([lambda E: E.matmul(ps[0][0:1, 0:6], ones_c[:, 0:1],
                                     stats[:, 38:44], start=True,
                                     stop=True)], 0)
        evac("vector", lambda E: E.tensor_copy(stats[0:1, 38:44],
                                               ps[0][0:1, 0:6]), 0)
        b.link("vector")
        b.emit("vector", lambda E: E.scalar_tensor_tensor(
            stats[0:1, 38:39], stats[0:1, 38:39], float(2 ** 9),
            stats[0:1, 39:40], ALU.mult, ALU.add))
        b.emit("vector", lambda E: E.scalar_tensor_tensor(
            stats[0:1, 40:41], stats[0:1, 40:41], float(2 ** 5),
            stats[0:1, 41:42], ALU.mult, ALU.add))
        b.link("vector")
        b.emit("vector", lambda E: E.scalar_tensor_tensor(
            stats[0:1, 38:39], stats[0:1, 38:39], float(2 ** 10),
            stats[0:1, 40:41], ALU.mult, ALU.add))
        b.link("vector")
        b.emit("vector", lambda E: E.tensor_add(
            stats[0:1, 38:39], stats[0:1, 38:39], stats[0:1, 43:44]))
        b.link("vector")
        b.emit("vector", lambda E, n=nrow * ncol: E.tensor_scalar(
            stats[0:1, 33:34], stats[0:1, 38:39], float(2.0 ** -20 / n), EPS,
            ALU.mult, ALU.max), tick="wqv")
        b.wait("tensor", "wqv", b.semval("wqv"))
        b.wait("tensor", "bev0", bank_ev[0])
        mm_group([lambda E: E.matmul(ps[1][:, 0:1], ones_r[0:1, :],
                                     stats[0:1, 33:34], start=True,
                                     stop=True)], 1)
        evac("vector", lambda E, mc=mucol: E.tensor_copy(
            mu[:, mc:mc + 1], ps[1][:, 0:1]), 1)
        b.link("vector")
        b.emit("vector", lambda E, mc=mucol: E.reciprocal(
            stats[:, 34:35], mu[:, mc:mc + 1]))
        b.link("vector")
        b.emit("vector", lambda E, mc=mucol: E.scalar_tensor_tensor(
            stats[:, 35:36], stats[:, 34:35], -1.0, mu[:, mc:mc + 1],
            ALU.mult, ALU.mult))
        b.link("vector")
        b.emit("vector", lambda E: E.tensor_scalar(
            stats[:, 35:36], stats[:, 35:36], 2.0, None, ALU.add))
        b.link("vector")
        b.emit("vector", lambda E: E.tensor_mul(
            stats[:, 34:35], stats[:, 34:35], stats[:, 35:36]))
        b.link("vector")
        for c in range(nch):
            b.emit("vector", lambda E, c=c, n=ncol: E.tensor_scalar(
                wqv[:, 0:n], wqin[:, c, 0:n], stats[:, 34:35], None,
                ALU.mult), tick="wq_v")
            b.wait("scalar", "wq_v", b.semval("wq_v"))
            b.emit("scalar", lambda E, n=ncol: E.activation(
                wqq0[:, 0:n], wqv[:, 0:n], AF.Copy, bias=256.75, scale=1.0),
                tick="wq_s")
            b.wait("vector", "wq_s", b.semval("wq_s"))
            b.emit("vector", lambda E, n=ncol: E.scalar_tensor_tensor(
                wqv[:, 0:n], wqq0[:, 0:n], -256.0, wqv[:, 0:n], ALU.add,
                ALU.subtract))
            b.link("vector")
            b.emit("vector", lambda E, n=ncol: E.scalar_tensor_tensor(
                wqv[:, 0:n], wqv[:, 0:n], 0.5, wqq0[:, 0:n], ALU.is_le,
                ALU.add))
            b.link("vector")
            b.emit("vector", lambda E, n=ncol: E.tensor_scalar(
                wqr[:, 0:n], wqv[:, 0:n], -257.0, 1.0, ALU.add, ALU.min))
            b.link("vector")
            b.emit("vector", lambda E, n=ncol: E.tensor_scalar(
                wqr[:, 0:n], wqr[:, 0:n], -1.0, None, ALU.max), tick="wq_v")
            b.wait("sync", "wq_v", b.semval("wq_v"))
            sv = b.dma(w_dst[c * 128:(c + 1) * 128, 0:ncol],
                       wqr[:, 0:ncol], "wq_st")
            b.wait("vector", "wq_st", sv)
            b.wait("scalar", "wq_st", sv)

    wquant(wgT, wg_q, D, INNER, 0)
    wquant(wuT, wu_q, D, INNER, 1)
    wquant(wdT, wd_q, INNER, D, 2)
    barrier(done=[("wq_st", b.semval("wq_st")), ("awst", b.semval("awst"))])
    wq_scope.close()

    def ffn_bufs(scope):
        Bf = SimpleNamespace()
        Bf.xt = [alloc(scope, [TC, D], F32), alloc(scope, [TC, D], F32)]
        Bf.vql = alloc(scope, [TC, D], F32)
        Bf.q0x = alloc(scope, [D], I16)
        Bf.xq_fm = alloc(scope, [8, TT], BF16)
        Bf.gu = alloc(scope, [TC, INNER], F32)
        Bf.hv = alloc(scope, [INNER], F32)
        Bf.hq_fm = alloc(scope, [32, TT], BF16)
        Bf.wbuf = alloc(scope, [2, 8, 512], BF16)
        Bf.wdbuf = alloc(scope, [2, D], BF16)
        return Bf

    def ffn_sublayer(Bf, x_src, x_dst, tag):
        xt, vql, q0x, xq_fm = Bf.xt, Bf.vql, Bf.q0x, Bf.xq_fm
        gu, hv, hq_fm, wbuf, wdbuf = (Bf.gu, Bf.hv, Bf.hq_fm, Bf.wbuf,
                                      Bf.wdbuf)
        wfree_state = [None, None]
        lad_guard = [None]
        xqT_done = [None]

        def stage1(it):
            p = it % 2
            so = 16 * p
            qo = 32 + 16 * p
            X = xt[p]
            if it >= 2:
                b.wait("sync", f"{tag}stx{p}", (it // 2) * 16)
            if lad_guard[0] is not None:
                b.wait("sync", f"{tag}v", lad_guard[0])
            src = x_src[it * TT:(it + 1) * TT, :].rearrange(
                "(c p) f -> p c f", p=128)
            lv = b.dma(X, src, f"{tag}ldx{p}")
            b.wait("vector", f"{tag}ldx{p}", lv)
            b.wait("scalar", f"{tag}ldx{p}", lv)
            b.wait("scalar", f"{tag}v", b.semval(f"{tag}v"))
            for tc in range(TC):
                b.emit("vector", lambda E, tc=tc, X=X, so=so: E.tensor_reduce(
                    stats[:, so + tc:so + tc + 1], X[:, tc], AX.X, ALU.add))
                b.emit("vector", lambda E, tc=tc, X=X, so=so: E.tensor_reduce(
                    stats[:, so + 8 + tc:so + 9 + tc], X[:, tc], AX.X, ALU.max))
                b.emit("vector", lambda E, tc=tc, X=X, so=so: E.tensor_reduce(
                    stats[:, so + 12 + tc:so + 13 + tc], X[:, tc], AX.X,
                    ALU.min))
            b.link("vector")
            for tc in range(TC):
                if tc > 0:
                    b.link("scalar")
                b.emit("scalar", lambda E, tc=tc, X=X, so=so: E.activation(
                    dump[:, :], X[:, tc], AF.Square,
                    accum_out=stats[:, so + 4 + tc:so + 5 + tc]),
                    tick=f"{tag}s")
            ssq = b.semval(f"{tag}s")
            mcol = stats[:, so:so + 4]
            vcol = stats[:, so + 4:so + 8]
            mxc = stats[:, so + 8:so + 12]
            mnc = stats[:, so + 12:so + 16]
            shc = stats[:, qo:qo + 4]
            dqg = stats[:, qo + 4:qo + 8]
            dqu = stats[:, qo + 8:qo + 12]
            dqd = stats[:, 64 + 4 * p:68 + 4 * p]
            b.wait("vector", f"{tag}s", ssq)
            b.emit("vector", lambda E, m=mcol: E.tensor_scalar(
                m, m, 1.0 / D, None, ALU.mult))
            b.link("vector")
            b.emit("vector", lambda E, v=vcol: E.tensor_scalar(
                v, v, 1.0 / D, None, ALU.mult))
            b.link("vector")
            b.emit("vector", lambda E, m=mcol: E.scalar_tensor_tensor(
                dump[:, 0:4], m, 1.0, m, ALU.bypass, ALU.mult))
            b.link("vector")
            b.emit("vector", lambda E, v=vcol: E.tensor_sub(
                v, v, dump[:, 0:4]), tick=f"{tag}v")
            b.wait("scalar", f"{tag}v", b.semval(f"{tag}v"))
            b.emit("scalar", lambda E, v=vcol: E.activation(
                dump[:, 4:8], v, AF.Sqrt, bias=EPS, scale=1.0),
                tick=f"{tag}s")
            b.wait("vector", f"{tag}s", b.semval(f"{tag}s"))
            b.emit("vector", lambda E: E.reciprocal(dump[:, 8:12],
                                                    dump[:, 4:8]))
            b.link("vector")
            b.emit("vector", lambda E, v=vcol: E.tensor_scalar(
                dump[:, 12:16], v, 1.0, EPS, ALU.mult, ALU.add))
            b.link("vector")
            b.emit("vector", lambda E: E.scalar_tensor_tensor(
                dump[:, 16:20], dump[:, 8:12], 1.0, dump[:, 8:12],
                ALU.bypass, ALU.mult))
            b.link("vector")
            b.emit("vector", lambda E: E.tensor_mul(
                dump[:, 16:20], dump[:, 16:20], dump[:, 12:16]))
            b.link("vector")
            b.emit("vector", lambda E: E.tensor_scalar(
                dump[:, 16:20], dump[:, 16:20], -0.5, 1.5, ALU.mult,
                ALU.add))
            b.link("vector")
            b.emit("vector", lambda E, v=vcol: E.tensor_mul(
                v, dump[:, 8:12], dump[:, 16:20]))
            b.link("vector")
            b.emit("vector", lambda E, mx=mxc, m=mcol: E.tensor_sub(mx, mx, m))
            b.link("vector")
            b.emit("vector", lambda E, mn=mnc, m=mcol: E.tensor_sub(mn, m, mn))
            b.link("vector")
            b.emit("vector", lambda E, mx=mxc, mn=mnc: E.tensor_max(mx, mx, mn))
            b.link("vector")
            b.emit("vector", lambda E, mx=mxc, v=vcol: E.tensor_mul(mx, mx, v))
            b.link("vector")
            b.emit("vector", lambda E, mx=mxc: E.tensor_scalar(
                mx, mx, EPS, None, ALU.max))
            b.link("vector")
            b.emit("vector", lambda E, d=dqg, mx=mxc: E.scalar_tensor_tensor(
                d, mx, 1.0 / 127.0, mu[:, 0:1].broadcast_to([128, 4]),
                ALU.mult, ALU.mult))
            b.emit("vector", lambda E, d=dqu, mx=mxc: E.scalar_tensor_tensor(
                d, mx, 1.0 / 127.0, mu[:, 1:2].broadcast_to([128, 4]),
                ALU.mult, ALU.mult))
            b.link("vector")
            b.emit("vector", lambda E, mx=mxc: E.reciprocal(mx, mx))
            b.link("vector")
            b.emit("vector", lambda E, mx=mxc: E.tensor_scalar(
                mx, mx, 127.0, None, ALU.mult))
            b.link("vector")
            b.emit("vector", lambda E, mx=mxc, v=vcol: E.tensor_mul(
                mx, mx, v))
            b.link("vector")
            b.emit("vector", lambda E, mn=mnc, m=mcol, mx=mxc:
                   E.scalar_tensor_tensor(mn, m, -1.0, mx, ALU.mult,
                                          ALU.mult), tick=f"{tag}v")
            vAB = b.semval(f"{tag}v")
            if xqT_done[0] is not None:
                b.wait("vector", *xqT_done[0])
            for tc in range(TC):
                if tc == 0:
                    b.wait("vector", f"{tag}v", vAB)
                b.emit("vector", lambda E, tc=tc, X=X, mx=mxc, mn=mnc:
                       E.tensor_scalar(vql[:, tc], X[:, tc],
                                       mx[:, tc:tc + 1],
                                       mn[:, tc:tc + 1], ALU.mult, ALU.add),
                       tick=f"{tag}v")
                b.wait("scalar", f"{tag}v", b.semval(f"{tag}v"))
                b.emit("scalar", lambda E, tc=tc: E.activation(
                    q0x[:], vql[:, tc], AF.Copy, bias=256.75, scale=1.0),
                    tick=f"{tag}s")
                b.wait("vector", f"{tag}s", b.semval(f"{tag}s"))
                b.emit("vector", lambda E, tc=tc: E.scalar_tensor_tensor(
                    vql[:, tc], q0x[:], -256.0, vql[:, tc], ALU.add,
                    ALU.subtract))
                b.link("vector")
                b.emit("vector", lambda E, tc=tc: E.scalar_tensor_tensor(
                    vql[:, tc], vql[:, tc], 0.5, q0x[:], ALU.is_le,
                    ALU.add))
                b.link("vector")
                b.emit("vector", lambda E, tc=tc: E.tensor_scalar(
                    vql[:, tc], vql[:, tc], -257.0, None, ALU.add),
                    tick=f"{tag}v")
            return b.semval(f"{tag}v")

        def evac_cb(it, tdown):
            p = it % 2
            X = xt[p]
            dqd = stats[:, 64 + 4 * p:68 + 4 * p]
            b.wait("vector", f"{tag}dmm", tdown)
            for dc in range(2):
                for tc in range(TC):
                    bk = dc * 4 + tc
                    bank_ev[bk] = b.emit(
                        "vector", lambda E, dc=dc, tc=tc, bk=bk, X=X, d=dqd:
                        E.scalar_tensor_tensor(
                            X[:, tc, dc * 512:(dc + 1) * 512], ps[bk][:],
                            d[:, tc:tc + 1],
                            X[:, tc, dc * 512:(dc + 1) * 512],
                            ALU.mult, ALU.add), tick=f"bev{bk}")
            b.wait("sync", f"bev{7}", bank_ev[7])
            dstv = x_dst[it * TT:(it + 1) * TT, :].rearrange(
                "(c p) f -> p c f", p=128)
            b.dma(dstv, X, f"{tag}stx{p}")

        def stage2(it, chain_v):
            p = it % 2
            so = 16 * p
            qo = 32 + 16 * p
            X = xt[p]
            shc = stats[:, qo:qo + 4]
            dqg = stats[:, qo + 4:qo + 8]
            dqu = stats[:, qo + 8:qo + 12]
            dqd = stats[:, 64 + 4 * p:68 + 4 * p]
            xq_ev = []
            b.wait("tensor", f"{tag}v", chain_v)
            for tc in range(TC):
                for g2 in range(2):
                    tb = 4 + g2
                    b.wait("tensor", f"bev{tb}", bank_ev[tb])
                    for j in range(4):
                        ic = g2 * 4 + j
                        b.emit("tensor", lambda E, tc=tc, ic=ic, j=j, tb=tb:
                               E.transpose(
                                   ps[tb][:, j * 128:(j + 1) * 128],
                                   vql[:, tc, ic * 128:(ic + 1) * 128],
                                   idtf[:]))
                    bank_mm[tb] = b.tick_last("tensor", f"bmm{tb}")
                    ev = evac("scalar", lambda E, tc=tc, g2=g2, tb=tb: E.copy(
                        xq_fm[:, g2 * 4:(g2 + 1) * 4,
                              tc * 128:(tc + 1) * 128],
                        ps[tb][:, 0:512].rearrange("p (a t) -> p a t", a=4)),
                        tb)
                xq_ev.append((f"bmm{4 + 1}", bank_mm[5]))
                xq_ev.append((f"bev{tb}", ev))
            xqT_done[0] = (f"bmm{5}", bank_mm[5])
            wfree = wfree_state
            silu_last = None
            up_last = None
            for phase in range(2):
                wsrc = wg_q if phase == 0 else wu_q
                if phase == 1:
                    b.wait("vector", *silu_last)
                for oc in range(8):
                    wp = (phase * 8 + oc) % 2
                    if wfree[wp] is not None:
                        b.wait("sync", *wfree[wp])
                    wl = b.dma(wbuf[:, wp],
                               wsrc[:, oc * 512:(oc + 1) * 512].rearrange(
                                   "(c p) f -> p c f", p=128), f"{tag}wld{wp}")
                    b.wait("tensor", f"{tag}wld{wp}", wl)
                    if phase == 0 and oc == 0:
                        b.wait("tensor", *xq_ev[-1])
                        b.wait("tensor", *xq_ev[-2])
                    for tc in range(TC):
                        bk = ((phase * 8 + oc) % 2) * 4 + tc
                        mms = [
                            (lambda E, ic=ic, tc=tc, wp=wp, bk=bk: E.matmul(
                                ps[bk][:],
                                xq_fm[:, ic, tc * 128:(tc + 1) * 128],
                                wbuf[:, wp, ic], start=(ic == 0),
                                stop=(ic == 7)))
                            for ic in range(8)]
                        mm_group(mms, bk)
                        if phase == 0:
                            evac("scalar",
                                 lambda E, tc=tc, oc=oc, bk=bk, d=dqg:
                                 E.activation(
                                     gu[:, tc, oc * 512:(oc + 1) * 512],
                                     ps[bk][:], AF.Silu,
                                     scale=d[:, tc:tc + 1]), bk)
                        else:
                            evac("vector",
                                 lambda E, tc=tc, oc=oc, bk=bk, d=dqu:
                                 E.scalar_tensor_tensor(
                                     gu[:, tc, oc * 512:(oc + 1) * 512],
                                     ps[bk][:], d[:, tc:tc + 1],
                                     gu[:, tc, oc * 512:(oc + 1) * 512],
                                     ALU.mult, ALU.mult), bk)
                    bklast = ((phase * 8 + oc) % 2) * 4 + 3
                    wfree[wp] = (f"bmm{bklast}", bank_mm[bklast])
                    if phase == 0:
                        silu_last = (f"bev{bklast}", bank_ev[bklast])
                    else:
                        up_last = (f"bev{bklast}", bank_ev[bklast])
            b.wait("vector", *silu_last)
            b.wait("vector", *up_last)
            for tc in range(TC):
                b.emit("vector", lambda E, tc=tc, s=shc: E.tensor_reduce(
                    s[:, tc:tc + 1], gu[:, tc], AX.X, ALU.max,
                    apply_absolute_value=True))
            b.link("vector")
            b.emit("vector", lambda E, s=shc: E.tensor_scalar(
                s, s, EPS, None, ALU.max))
            b.link("vector")
            b.emit("vector", lambda E, d=dqd, s=shc: E.scalar_tensor_tensor(
                d, s, 0.5 / 127.0, mu[:, 2:3].broadcast_to([128, 4]),
                ALU.mult, ALU.mult))
            b.link("vector")
            b.emit("vector", lambda E, s=shc: E.reciprocal(s, s))
            b.link("vector")
            b.emit("vector", lambda E, s=shc: E.tensor_scalar(
                s, s, 127.0, None, ALU.mult), tick=f"{tag}v")
            vsh = b.semval(f"{tag}v")
            scr = xt[1 - p].rearrange("p a d -> p (a d)")
            if it >= 1:
                b.wait("scalar", f"{tag}stx{1 - p}",
                       ((it - 1) // 2 + 1) * 16)
                b.wait("scalar", "bev7", b.semval("bev7"))
            hq_ev = []
            prevG = None
            for tc in range(TC):
                b.wait("scalar", f"{tag}v", vsh)
                if prevG is not None:
                    b.wait("scalar", f"{tag}v", prevG)
                b.emit("scalar", lambda E, tc=tc, s=shc, scr=scr:
                       E.activation(scr, gu[:, tc], AF.Copy, bias=500.0,
                                    scale=s[:, tc:tc + 1]), tick=f"{tag}s")
                bB = b.semval(f"{tag}s")
                b.wait("vector", f"{tag}v", vsh)
                b.emit("vector", lambda E, tc=tc, s=shc: E.tensor_scalar(
                    hv[:], gu[:, tc], s[:, tc:tc + 1], C32M, ALU.mult,
                    ALU.add))
                b.link("vector")
                b.wait("vector", f"{tag}s", bB)
                b.emit("vector", lambda E, tc=tc: E.tensor_scalar(
                    gu[:, tc], hv[:], -C32M, None, ALU.add), tick=f"{tag}v")
                vC = b.semval(f"{tag}v")
                b.wait("scalar", f"{tag}v", vC)
                b.emit("scalar", lambda E, scr=scr: E.activation(
                    hv[:], scr, AF.Copy, bias=C52M, scale=1.0))
                b.link("scalar")
                b.emit("scalar", lambda E, scr=scr: E.activation(
                    scr, hv[:], AF.Copy, bias=-C52M, scale=1.0),
                    tick=f"{tag}s")
                eE = b.semval(f"{tag}s")
                b.link("vector")
                b.emit("vector", lambda E, tc=tc: E.tensor_scalar(
                    gu[:, tc], gu[:, tc], cst[:, 5:6], cst[:, 6:7],
                    ALU.mult, ALU.add))
                b.link("vector")
                b.wait("vector", f"{tag}s", eE)
                b.emit("vector", lambda E, tc=tc, scr=scr:
                       E.scalar_tensor_tensor(
                           gu[:, tc], scr, cst[:, 4:5], gu[:, tc],
                           ALU.mult, ALU.add), tick=f"{tag}v")
                prevG = b.semval(f"{tag}v")
                b.wait("tensor", f"{tag}v", b.semval(f"{tag}v"))
                for g in range(8):
                    tb = 4 + (g % 2)
                    b.wait("tensor", f"bev{tb}", bank_ev[tb])
                    for j in range(4):
                        oc = g * 4 + j
                        b.emit("tensor", lambda E, tc=tc, oc=oc, j=j, tb=tb:
                               E.transpose(
                                   ps[tb][:, j * 128:(j + 1) * 128],
                                   gu[:, tc, oc * 128:(oc + 1) * 128],
                                   idtf[:]))
                    bank_mm[tb] = b.tick_last("tensor", f"bmm{tb}")
                    ev = evac("vector",
                              lambda E, tc=tc, g=g, tb=tb: E.tensor_copy(
                                  hq_fm[:, g * 4:(g + 1) * 4,
                                        tc * 128:(tc + 1) * 128],
                                  ps[tb][:, 0:512].rearrange(
                                      "p (a t) -> p a t", a=4)), tb)
                    hq_ev.append((f"bev{tb}", ev))
                hq_ev.append((f"bmm{4}", bank_mm[4]))
                hq_ev.append((f"bmm{5}", bank_mm[5]))
            lad_guard[0] = b.semval(f"{tag}v")
            b.wait("tensor", *hq_ev[-1])
            b.wait("tensor", *hq_ev[-2])
            for bk in range(8):
                b.wait("tensor", f"bev{bk}", bank_ev[bk])
            for oc in range(32):
                wp = oc % 2
                if oc >= 2:
                    b.wait("sync", f"{tag}dmm", b.semval(f"{tag}dmm") - 1)
                wl = b.dma(wdbuf[:, wp], wd_q[oc * 128:(oc + 1) * 128, :],
                           f"{tag}dld{wp}")
                b.wait("tensor", f"{tag}dld{wp}", wl)
                for dc in range(2):
                    for tc in range(TC):
                        bk = dc * 4 + tc
                        b.emit("tensor",
                               lambda E, oc=oc, dc=dc, tc=tc, wp=wp, bk=bk:
                               E.matmul(ps[bk][:],
                                        hq_fm[:, oc, tc * 128:(tc + 1) * 128],
                                        wdbuf[:, wp, dc * 512:(dc + 1) * 512],
                                        start=(oc == 0), stop=(oc == 31)))
                b.tick_last("tensor", f"{tag}dmm")
            return b.semval(f"{tag}dmm")

        pending = None
        for it in range(NTILE):
            chain_v = stage1(it)
            if pending is not None:
                evac_cb(*pending)
            tdown = stage2(it, chain_v)
            pending = (it, tdown)
        evac_cb(*pending)
        return [(f"{tag}stx0", b.semval(f"{tag}stx0")),
                (f"{tag}stx1", b.semval(f"{tag}stx1"))]


    def attn_core_bufs(scope):
        Ba = SimpleNamespace()
        Ba.o_fm = alloc(scope, [8, SEQ], BF16)
        Ba.v_tm = alloc(scope, [16, D], BF16)
        Ba.k_fm = alloc(scope, [8, SEQ], BF16)
        Ba.q_fm = alloc(scope, [8, SEQ], BF16)
        return Ba

    def attn_ln_bufs(scope):
        Bl = SimpleNamespace()
        Bl.xn_fm = alloc(scope, [8, SEQ], BF16)
        Bl.x1t_a = alloc(scope, [2, D], F32)
        Bl.xn_tm = alloc(scope, [2, D], F32)
        Bl.qkvw = alloc(scope, [2, 8, 128], BF16)
        Bl.vw_r = alloc(scope, [8, D], BF16)
        return Bl

    def attn_head_bufs(scope):
        Bh = SimpleNamespace()
        Bh.p0a = alloc(scope, [SEQ], F32)
        Bh.ppt = [alloc(scope, [SEQ], BF16), alloc(scope, [SEQ], BF16)]
        Bh.sc_B = alloc(scope, [16, NKV], F32)
        Bh.sc_E = alloc(scope, [16, NKV], F32)
        Bh.sc_z = alloc(scope, [16, NKV], F32)
        Bh.sc_m = alloc(scope, [16, NKV], F32)
        Bh.hi16 = alloc(scope, [16, NKV], BF16)
        Bh.lo16 = alloc(scope, [16, NKV], BF16)
        Bh.sc_Mq = alloc(scope, [16, 4], F32)
        Bh.sc_cf = alloc(scope, [16, 4], F32)
        Bh.aughl = alloc(scope, [128], F32)
        Bh.q_aug2 = [alloc(scope, [SEQ], BF16), alloc(scope, [SEQ], BF16)]
        Bh.k_aug2 = [alloc(scope, [SEQ], BF16), alloc(scope, [SEQ], BF16)]
        return Bh

    def attn_wo_bufs(scope):
        Bw = SimpleNamespace()
        Bw.x1t_b = alloc(scope, [2, D], F32)
        Bw.wo_r = alloc(scope, [8, D], BF16)
        return Bw

    def attn_ln_qkv(Ba, Bl, sq, tag):
        o_fm, v_tm, k_fm, q_fm = Ba.o_fm, Ba.v_tm, Ba.k_fm, Ba.q_fm
        xn_fm, x1t_a, xn_tm = Bl.xn_fm, Bl.x1t_a, Bl.xn_tm
        qkvw, vw_r = Bl.qkvw, Bl.vw_r
        row0 = sq * SEQ
        for tc in range(16):
            pp = tc % 2
            so = 48 + 4 * pp
            if tc >= 2:
                b.wait("sync", f"{tag}xnf{pp}", b.semval(f"{tag}xnf{pp}"))
            lv = b.dma(x1t_a[:, pp],
                       x1d[row0 + tc * 128:row0 + (tc + 1) * 128, :],
                       f"{tag}lx{pp}")
            b.wait("vector", f"{tag}lx{pp}", lv)
            b.wait("scalar", f"{tag}lx{pp}", lv)
            XA = x1t_a[:, pp]
            b.wait("scalar", f"{tag}v", b.semval(f"{tag}v"))
            b.emit("vector", lambda E, XA=XA, so=so: E.tensor_reduce(
                stats[:, so:so + 1], XA, AX.X, ALU.add), tick=f"{tag}v")
            b.emit("scalar", lambda E, XA=XA, so=so: E.activation(
                dump[:, :], XA, AF.Square,
                accum_out=stats[:, so + 1:so + 2]), tick=f"{tag}s")
            b.wait("vector", f"{tag}s", b.semval(f"{tag}s"))
            b.wait("vector", f"{tag}v", b.semval(f"{tag}v"))
            b.emit("vector", lambda E, so=so: E.tensor_scalar(
                stats[:, so:so + 2], stats[:, so:so + 2], 1.0 / D, None,
                ALU.mult))
            b.link("vector")
            b.emit("vector", lambda E, so=so: E.scalar_tensor_tensor(
                stats[:, so + 2:so + 3], stats[:, so:so + 1], 1.0,
                stats[:, so:so + 1], ALU.bypass, ALU.mult))
            b.link("vector")
            b.emit("vector", lambda E, so=so: E.tensor_sub(
                stats[:, so + 1:so + 2], stats[:, so + 1:so + 2],
                stats[:, so + 2:so + 3]), tick=f"{tag}v")
            b.wait("scalar", f"{tag}v", b.semval(f"{tag}v"))
            b.emit("scalar", lambda E, so=so: E.activation(
                stats[:, so + 2:so + 3], stats[:, so + 1:so + 2], AF.Sqrt,
                bias=EPS, scale=1.0), tick=f"{tag}s")
            b.wait("vector", f"{tag}s", b.semval(f"{tag}s"))
            b.emit("vector", lambda E, so=so: E.reciprocal(
                stats[:, so + 2:so + 3], stats[:, so + 2:so + 3]))
            b.link("vector")
            b.emit("vector", lambda E, so=so: E.tensor_scalar(
                stats[:, so + 3:so + 4], stats[:, so + 1:so + 2], 1.0, EPS,
                ALU.mult, ALU.add))
            b.link("vector")
            b.emit("vector", lambda E, so=so: E.scalar_tensor_tensor(
                dump[:, 20:21], stats[:, so + 2:so + 3], 1.0,
                stats[:, so + 2:so + 3], ALU.bypass, ALU.mult))
            b.link("vector")
            b.emit("vector", lambda E, so=so: E.tensor_mul(
                dump[:, 20:21], dump[:, 20:21], stats[:, so + 3:so + 4]))
            b.link("vector")
            b.emit("vector", lambda E: E.tensor_scalar(
                dump[:, 20:21], dump[:, 20:21], -0.5, 1.5, ALU.mult,
                ALU.add))
            b.link("vector")
            b.emit("vector", lambda E, so=so: E.tensor_mul(
                stats[:, so + 1:so + 2], stats[:, so + 2:so + 3],
                dump[:, 20:21]))
            b.link("vector")
            b.emit("vector", lambda E, so=so: E.scalar_tensor_tensor(
                stats[:, so:so + 1], stats[:, so:so + 1], -1.0,
                stats[:, so + 1:so + 2], ALU.mult, ALU.mult))
            b.link("vector")
            b.emit("vector", lambda E, XA=XA, pp=pp, so=so: E.tensor_scalar(
                xn_tm[:, pp], XA, stats[:, so + 1:so + 2],
                stats[:, so:so + 1], ALU.mult, ALU.add), tick=f"{tag}v")
            b.wait("tensor", f"{tag}v", b.semval(f"{tag}v"))
            for g2 in range(2):
                tb = 4 + g2
                b.wait("tensor", f"bev{tb}", bank_ev[tb])
                for j in range(4):
                    ic = g2 * 4 + j
                    b.emit("tensor", lambda E, ic=ic, j=j, pp=pp, tb=tb:
                           E.transpose(
                               ps[tb][:, j * 128:(j + 1) * 128],
                               xn_tm[:, pp, ic * 128:(ic + 1) * 128],
                               idtf[:]))
                bank_mm[tb] = b.tick_last("tensor", f"bmm{tb}")
                ev = evac("scalar", lambda E, tc=tc, g2=g2, tb=tb: E.copy(
                    xn_fm[:, g2 * 4:(g2 + 1) * 4, tc * 128:(tc + 1) * 128],
                    ps[tb][:, 0:512].rearrange("p (a t) -> p a t", a=4)), tb)
            b.wait("sync", f"bmm{4}", bank_mm[4])
            b.wait("sync", f"bmm{5}", bank_mm[5])
            b.emit("sync", lambda E: E.nop(), tick=f"{tag}xnf{pp}")
        xn_done = [(f"bev{4}", bank_ev[4]), (f"bev{5}", bank_ev[5])]
        vwl = b.dma(vw_r, wv_b[:, :].rearrange("(c p) f -> p c f", p=128),
                    f"{tag}vwl")
        qk_wfree = {}
        for (wsrcd, dstfm, scl, qt) in ((wq_b, q_fm, 0.125, "q"),
                                        (wk_b, k_fm, 1.0, "k")):
            for dc in range(8):
                wp = dc % 2
                if (qt, wp) in qk_wfree:
                    b.wait("sync", *qk_wfree[(qt, wp)])
                if qt == "k" and dc < 2:
                    b.wait("sync", *qk_wfree[("q", wp)])
                wl = b.dma(qkvw[:, wp],
                           wsrcd[:, dc * 128:(dc + 1) * 128].rearrange(
                               "(c p) f -> p c f", p=128), f"{tag}{qt}wl{wp}")
                b.wait("tensor", f"{tag}{qt}wl{wp}", wl)
                if dc == 0 and qt == "q":
                    for nm, v in xn_done:
                        b.wait("tensor", nm, v)
                for qq in range(4):
                    bk = (dc * 4 + qq) % 4
                    mms = [
                        (lambda E, ic=ic, qq=qq, wp=wp, bk=bk: E.matmul(
                            ps[bk][:], qkvw[:, wp, ic],
                            xn_fm[:, ic, qq * 512:(qq + 1) * 512],
                            start=(ic == 0), stop=(ic == 7)))
                        for ic in range(8)]
                    mm_group(mms, bk)
                    evac("scalar", lambda E, dc=dc, qq=qq, bk=bk, d=dstfm,
                         scl=scl: E.activation(
                             d[:, dc, qq * 512:(qq + 1) * 512], ps[bk][:],
                             AF.Copy, bias=0.0, scale=scl), bk)
                    if qq == 3:
                        qk_wfree[(qt, wp)] = (f"bmm{bk}", bank_mm[bk])
        b.wait("tensor", f"{tag}vwl", vwl)
        for tc in range(16):
            for half in range(2):
                bk = (tc * 2 + half) % 4
                mms = [
                    (lambda E, ic=ic, tc=tc, half=half, bk=bk: E.matmul(
                        ps[bk][:], xn_fm[:, ic, tc * 128:(tc + 1) * 128],
                        vw_r[:, ic, half * 512:(half + 1) * 512],
                        start=(ic == 0), stop=(ic == 7)))
                    for ic in range(8)]
                mm_group(mms, bk)
                evac("scalar", lambda E, tc=tc, half=half, bk=bk: E.copy(
                    v_tm[:, tc, half * 512:(half + 1) * 512], ps[bk][:]), bk)
        vdone = [(f"bev{i}", bank_ev[i]) for i in range(4)]
        return vdone

    def attn_heads(Ba, Bh, sq, tag, vdone):
        o_fm, v_tm, k_fm, q_fm = Ba.o_fm, Ba.v_tm, Ba.k_fm, Ba.q_fm
        p0a, ppt = Bh.p0a, Bh.ppt
        sc_B, sc_E, sc_z, sc_m = Bh.sc_B, Bh.sc_E, Bh.sc_z, Bh.sc_m
        sc_E3 = Bh.sc_E.rearrange("p a b -> p (a b)").rearrange(
            "p (a b c) -> p a b c", a=16, b=4)
        hi16, lo16 = Bh.hi16, Bh.lo16
        sc_Mq, sc_cf, aughl = Bh.sc_Mq, Bh.sc_cf, Bh.aughl
        q_aug2, k_aug2 = Bh.q_aug2, Bh.k_aug2
        row0 = sq * SEQ
        b.dma(k_aug2[0][64:128, :], indic[:], f"{tag}ind")
        b.dma(k_aug2[1][0:64, :], indic[:], f"{tag}ind")
        pB_vals = {}
        augv = {}
        augT_prev = [None]

        def prologue(h):
            pi = h % 2
            ch = h // 2
            qof = 64 * pi
            ka, qa = k_aug2[pi], q_aug2[pi]
            if h >= 2:
                b.wait("vector", f"{tag}pB", pB_vals[h - 2])
            if h == 0:
                for nm, v in vdone:
                    b.wait("vector", nm, v)
            b.emit("vector", lambda E, ch=ch, qof=qof, ka=ka: E.tensor_copy(
                ka[qof:qof + 64, :], k_fm[qof:qof + 64, ch, :]))
            b.emit("vector", lambda E, ch=ch, qof=qof, qa=qa: E.tensor_copy(
                qa[qof:qof + 64, :], q_fm[qof:qof + 64, ch, :]),
                tick=f"{tag}aug")
            augv[h] = b.semval(f"{tag}aug")

        def passA_mm_max(h, qc, kq):
            pi = h % 2
            ch = h // 2
            qof = 64 * pi
            bk = kq % 2
            mm_group([
                (lambda E, qc=qc, kq=kq, bk=bk, ch=ch, qof=qof:
                 E.matmul(ps[bk][:],
                          q_fm[qof:qof + 64, ch,
                               qc * 128:(qc + 1) * 128],
                          k_fm[qof:qof + 64, ch,
                               kq * 512:(kq + 1) * 512],
                          start=True, stop=True,
                          tile_position=(qof, 0)))], bk)
            b.wait("vector", f"bmm{bk}", bank_mm[bk])
            b.emit("vector", lambda E, qc=qc, kq=kq, bk=bk:
                   E.tensor_reduce(
                       sc_B[:, qc, kq * 8:(kq + 1) * 8],
                       ps[bk][:].rearrange("p (a x) -> p a x", a=8),
                       AX.X, ALU.max))
            b.link("vector")
            b.emit("vector", lambda E, qc=qc, kq=kq:
                   E.tensor_reduce(sc_Mq[:, qc, kq:kq + 1],
                                   sc_B[:, qc, 0:(kq + 1) * 8], AX.X,
                                   ALU.max), tick=f"{tag}v")
            nmcol = 56 + bk
            b.wait("scalar", f"{tag}v", b.semval(f"{tag}v"))
            b.emit("scalar", lambda E, qc=qc, kq=kq, nmcol=nmcol:
                   E.activation(stats[:, nmcol:nmcol + 1],
                                sc_Mq[:, qc, kq:kq + 1], AF.Copy,
                                bias=0.0, scale=-1.0))
            b.link("scalar")
            ev = evac("scalar", lambda E, kq=kq, bk=bk, nmcol=nmcol:
                      E.activation(p0a[:, kq * 512:(kq + 1) * 512],
                                   ps[bk][:], AF.Exp,
                                   bias=stats[:, nmcol:nmcol + 1],
                                   scale=1.0), bk)
            return ev

        def passA_sum(qc, kq, ev):
            bk = kq % 2
            b.wait("vector", f"bev{bk}", ev)
            b.emit("vector", lambda E, qc=qc, kq=kq: E.tensor_reduce(
                sc_E[:, qc, kq * 8:(kq + 1) * 8],
                p0a[:, kq * 512:(kq + 1) * 512].rearrange(
                    "p (a x) -> p a x", a=8), AX.X, ALU.add),
                tick=f"{tag}v")

        def smalls_augT(h, tail_cb=None):
            pi = h % 2
            aof = 64 * (1 - pi)
            qa = q_aug2[pi]
            if tail_cb is not None:
                tail_cb()
            vA = b.semval(f"{tag}v")
            b.wait("vector", f"{tag}v", vA)
            b.emit("vector", lambda E: E.tensor_sub(
                sc_cf[:], sc_Mq[:],
                sc_Mq[:, :, 3:4].broadcast_to([128, 16, 4])), tick=f"{tag}v")
            b.wait("scalar", f"{tag}v", b.semval(f"{tag}v"))
            b.emit("scalar", lambda E: E.activation(
                sc_cf[:], sc_cf[:], AF.Exp), tick=f"{tag}s")
            b.wait("vector", f"{tag}s", b.semval(f"{tag}s"))
            b.emit("vector", lambda E: E.tensor_mul(
                sc_E3[:], sc_E3[:],
                sc_cf[:].rearrange("p a x -> p a x 1" if False else
                                   "p a x -> p a x").unsqueeze(3)
                .broadcast_to([128, 16, 4, 8])))
            b.link("vector")
            for qc in range(16):
                b.emit("vector", lambda E, qc=qc: E.tensor_tensor_scan(
                    sc_z[:, qc, :], sc_E[:, qc, :], sc_E[:, qc, :], 0.0,
                    ALU.add, ALU.bypass))
                b.emit("vector", lambda E, qc=qc: E.tensor_tensor_scan(
                    sc_m[:, qc, :], sc_B[:, qc, :], sc_B[:, qc, :], -3.0e38,
                    ALU.max, ALU.bypass))
            b.link("vector")
            vz = b.tick_last("vector", f"{tag}v") if False else None
            b.emit("vector", lambda E: E.nop(), tick=f"{tag}v")
            b.wait("scalar", f"{tag}v", b.semval(f"{tag}v"))
            b.emit("scalar", lambda E: E.activation(
                sc_z[:], sc_z[:], AF.Ln), tick=f"{tag}s")
            b.wait("vector", f"{tag}s", b.semval(f"{tag}s"))
            b.emit("vector", lambda E: E.tensor_sub(
                sc_B[:], sc_Mq[:, :, 3:4].broadcast_to([128, 16, 32]),
                sc_m[:]))
            b.link("vector")
            b.emit("vector", lambda E: E.tensor_add(sc_z[:], sc_z[:],
                                                    sc_B[:]))
            b.link("vector")
            for qc in range(16):
                b.emit("vector", lambda E, qc=qc: E.tensor_tensor_scan(
                    sc_m[:, qc, :], sc_z[:, qc, :], sc_z[:, qc, :], 0.0,
                    ALU.add, ALU.bypass))
            b.link("vector")
            b.emit("vector", lambda E: E.tensor_sub(
                sc_B[:], sc_m[:, :, 31:32].broadcast_to([128, 16, 32]),
                sc_m[:]))
            b.link("vector")
            b.emit("vector", lambda E: E.tensor_add(sc_B[:], sc_B[:],
                                                    sc_z[:]))
            b.link("vector")
            b.emit("vector", lambda E: E.scalar_tensor_tensor(
                sc_E[:], sc_B[:], -1.0,
                sc_Mq[:, :, 3:4].broadcast_to([128, 16, 32]),
                ALU.mult, ALU.subtract))
            b.link("vector")
            b.emit("vector", lambda E: E.tensor_copy(hi16[:], sc_E[:]))
            b.link("vector")
            b.emit("vector", lambda E: E.tensor_sub(lo16[:], sc_E[:],
                                                    hi16[:]),
                   tick=f"{tag}v")
            vhl = b.semval(f"{tag}v")
            b.wait("vector", f"{tag}v", vhl)
            for qc in range(16):
                tb = 2 + qc % 2
                if augT_prev[0] is not None:
                    b.wait("vector", *augT_prev[0])
                b.emit("vector", lambda E, qc=qc, aof=aof: E.tensor_copy(
                    aughl[:, aof:aof + 32], hi16[:, qc, :]))
                b.link("vector")
                b.emit("vector", lambda E, qc=qc, aof=aof: E.tensor_copy(
                    aughl[:, aof + 32:aof + 64], lo16[:, qc, :]),
                    tick=f"{tag}v")
                b.wait("tensor", f"{tag}v", b.semval(f"{tag}v"))
                b.wait("tensor", f"bev{tb}", bank_ev[tb])
                b.emit("tensor", lambda E, tb=tb: E.transpose(
                    ps[tb][:, 0:128], aughl[:], idtf[:]))
                bank_mm[tb] = b.tick_last("tensor", f"bmm{tb}")
                augT_prev[0] = (f"bmm{tb}", bank_mm[tb])
                evac("scalar", lambda E, qc=qc, tb=tb, aof=aof, qa=qa: E.copy(
                    qa[aof:aof + 64, qc * 128:(qc + 1) * 128],
                    ps[tb][aof:aof + 64, 0:128]), tb)

        def passB_pre(h):
            b.wait("tensor", f"{tag}aug", augv[h])
            if h == 0:
                b.wait("tensor", f"{tag}ind", 32)
            for pvb in (4, 5, 6, 7):
                b.wait("tensor", f"bev{pvb}", bank_ev[pvb])

        pend_pv = [None]

        def emit_pv(h, kc, qq, ev):
            pi = h % 2
            sp = kc % 2
            bk = 2 + qq % 2
            pvb = 4 + qq
            b.wait("tensor", f"bev{bk}", ev)
            b.emit("tensor", lambda E, kc=kc, qq=qq, sp=sp,
                   h=h, pi=pi, pvb=pvb: E.matmul(
                       ps[pvb][64 * pi:64 * pi + 64, :],
                       v_tm[:, kc, h * 64:(h + 1) * 64],
                       ppt[sp][:, qq * 512:(qq + 1) * 512],
                       start=(kc == 0), stop=(kc == 15),
                       tile_position=(0, 64 * pi)))
            if qq == 3:
                b.tick_last("tensor", f"{tag}pv")

        def passB_sub(h, kc, qq):
            pi = h % 2
            ka, qa = k_aug2[pi], q_aug2[pi]
            sp = kc % 2
            bk = 2 + qq % 2
            mm_group([
                (lambda E, kc=kc, qq=qq, bk=bk, ka=ka, qa=qa: E.matmul(
                    ps[bk][:], ka[:, kc * 128:(kc + 1) * 128],
                    qa[:, qq * 512:(qq + 1) * 512],
                    start=True, stop=True))], bk)
            if pend_pv[0] is not None:
                emit_pv(*pend_pv[0])
                pend_pv[0] = None
            if qq == 0:
                if kc >= 2:
                    b.wait("scalar", f"{tag}pv", b.semval(f"{tag}pv") - 1)
                else:
                    b.wait("scalar", f"{tag}pv", b.semval(f"{tag}pv"))
            ev = evac("scalar", lambda E, qq=qq, bk=bk, sp=sp:
                      E.activation(
                          ppt[sp][:, qq * 512:(qq + 1) * 512],
                          ps[bk][:], AF.Exp), bk)
            pend_pv[0] = (h, kc, qq, ev)

        def passB_tail(h):
            pi = h % 2
            ch = h // 2
            pvv = b.semval(f"{tag}pv")
            for qq in range(4):
                pvb = 4 + qq
                b.wait("scalar", f"{tag}pv", pvv)
                v2 = b.emit("scalar", lambda E, qq=qq, ch=ch, pi=pi,
                            pvb=pvb: E.copy(
                                o_fm[64 * pi:64 * pi + 64, ch,
                                     qq * 512:(qq + 1) * 512],
                                ps[pvb][64 * pi:64 * pi + 64, :]),
                            tick=f"bev{pvb}")
                bank_ev[pvb] = v2
            b.wait("sync", f"{tag}pv", pvv)
            b.emit("sync", lambda E: E.nop(), tick=f"{tag}pB")
            pB_vals[h] = b.semval(f"{tag}pB")

        def passA_loop(h, hB=None):
            pend = None
            i = 0
            for qc in range(16):
                for kq in range(4):
                    ev = passA_mm_max(h, qc, kq)
                    if pend is not None:
                        passA_sum(*pend)
                    pend = (qc, kq, ev)
                    if hB is not None and i < 48:
                        passB_sub(hB, i // 4, i % 4)
                    i += 1
            passA_sum(*pend)

        def passB_rest(h):
            for j in range(48, 64):
                passB_sub(h, j // 4, j % 4)
            emit_pv(*pend_pv[0])
            pend_pv[0] = None
            passB_tail(h)

        for h in range(H):
            prologue(h)
            if h == 0:
                passA_loop(h)
            else:
                passB_pre(h - 1)
                passA_loop(h, hB=h - 1)
            smalls_augT(h, tail_cb=(
                (lambda hh=h - 1: passB_rest(hh)) if h >= 1 else None))
        passB_pre(H - 1)
        for kc in range(16):
            for qq in range(4):
                passB_sub(H - 1, kc, qq)
        emit_pv(*pend_pv[0])
        pend_pv[0] = None
        passB_tail(H - 1)
        heads_done = [(f"bev{i}", bank_ev[i]) for i in (4, 5, 6, 7)]
        return heads_done

    def attn_wo(Ba, Bw, sq, tag, heads_done):
        o_fm = Ba.o_fm
        x1t_b, wo_r = Bw.x1t_b, Bw.wo_r
        row0 = sq * SEQ
        wol = b.dma(wo_r, wo_b[:, :].rearrange("(c p) f -> p c f", p=128),
                    f"{tag}wol")
        b.wait("tensor", f"{tag}wol", wol)
        for nm, v in heads_done:
            b.wait("tensor", nm, v)
        for qc in range(16):
            pp = qc % 2
            if qc >= 2:
                b.wait("sync", f"{tag}x2{pp}", b.semval(f"{tag}x2{pp}"))
            lv = b.dma(x1t_b[:, pp],
                       x1d[row0 + qc * 128:row0 + (qc + 1) * 128, :],
                       f"{tag}lb{pp}")
            for half in range(2):
                bk = (qc * 2 + half) % 4
                mms = [
                    (lambda E, mc=mc, qc=qc, half=half, bk=bk: E.matmul(
                        ps[bk][:], o_fm[:, mc, qc * 128:(qc + 1) * 128],
                        wo_r[:, mc, half * 512:(half + 1) * 512],
                        start=(mc == 0), stop=(mc == 7)))
                    for mc in range(8)]
                mm_group(mms, bk)
            b.wait("vector", f"{tag}lb{pp}", lv)
            for half in range(2):
                bk = (qc * 2 + half) % 4
                evac("vector", lambda E, pp=pp, half=half, bk=bk:
                     E.tensor_add(x1t_b[:, pp, half * 512:(half + 1) * 512],
                                  ps[bk][:],
                                  x1t_b[:, pp, half * 512:(half + 1) * 512]),
                     bk)
            b.link("vector")
            b.emit("vector", lambda E: E.nop(), tick=f"{tag}ev")
            b.wait("sync", f"{tag}ev", b.semval(f"{tag}ev"))
            b.dma(x2d[row0 + qc * 128:row0 + (qc + 1) * 128, :],
                  x1t_b[:, pp], f"{tag}x2{pp}")
        return [(f"{tag}x2{0}", b.semval(f"{tag}x2{0}")),
                (f"{tag}x2{1}", b.semval(f"{tag}x2{1}"))]

    b.wait("vector", "bev1", bank_ev[1])

    f1_scope = mkscope()
    Bf1 = ffn_bufs(f1_scope)
    st1 = ffn_sublayer(Bf1, x0, x1d, "f1")
    barrier(done=st1)
    f1_scope.close()

    at_scope = mkscope()
    Ba = attn_core_bufs(at_scope)
    for sq, tag in ((0, "a0"), (1, "a1")):
        ln_scope = mkscope()
        Bl = attn_ln_bufs(ln_scope)
        vdone = attn_ln_qkv(Ba, Bl, sq, tag)
        barrier()
        ln_scope.close()
        hd_scope = mkscope()
        Bh = attn_head_bufs(hd_scope)
        hdone = attn_heads(Ba, Bh, sq, tag, vdone)
        barrier()
        hd_scope.close()
        wo_scope = mkscope()
        Bw = attn_wo_bufs(wo_scope)
        sta = attn_wo(Ba, Bw, sq, tag, hdone)
        barrier(done=sta)
        wo_scope.close()
    at_scope.close()

    f3_scope = mkscope()
    Bf3 = ffn_bufs(f3_scope)
    st3 = ffn_sublayer(Bf3, x2d, y, "f3")
    for nm, v in st3:
        b.wait("sync", nm, v)
    f3_scope.close()

    b.run_block()
    b.ctx.close()
    return nc


def _host_inputs(L0_attn_x, Wq, Wk, Wv, Wo, w_gate, w_up, w_down):
    import ml_dtypes
    consts = np.zeros((128, 8), dtype=np.float32)
    consts[:, 0] = 256.0
    consts[:, 1] = EPS
    consts[:, 2] = 0.125
    ind = np.zeros((64, SEQ), dtype=np.float32)
    for t in range(NKV):
        ind[t, t * 64:(t + 1) * 64] = 1.0
        ind[32 + t, t * 64:(t + 1) * 64] = 1.0
    ind = ind.astype(ml_dtypes.bfloat16)
    ident = np.eye(128, dtype=np.float32).astype(ml_dtypes.bfloat16)
    com = {
        "wq": np.ascontiguousarray(Wq), "wk": np.ascontiguousarray(Wk),
        "wv": np.ascontiguousarray(Wv), "wo": np.ascontiguousarray(Wo),
        "wgT": np.ascontiguousarray(w_gate.T),
        "wuT": np.ascontiguousarray(w_up.T),
        "wdT": np.ascontiguousarray(w_down.T),
        "consts": consts, "indic": ind, "ident": ident,
        "identf": np.eye(128, dtype=np.float32),
    }
    xs = np.ascontiguousarray(L0_attn_x).reshape(8, NTOK, D)
    return [dict(com, x0=np.ascontiguousarray(xs[c])) for c in range(8)]


def kernel(L0_attn_x, L0_attn_Wq, L0_attn_Wk, L0_attn_Wv, L0_attn_Wo,
           L0_ffn_w_gate, L0_ffn_w_up, L0_ffn_w_down):
    in_maps = _host_inputs(
        np.asarray(L0_attn_x, dtype=np.float32),
        np.asarray(L0_attn_Wq, dtype=np.float32),
        np.asarray(L0_attn_Wk, dtype=np.float32),
        np.asarray(L0_attn_Wv, dtype=np.float32),
        np.asarray(L0_attn_Wo, dtype=np.float32),
        np.asarray(L0_ffn_w_gate, dtype=np.float32),
        np.asarray(L0_ffn_w_up, dtype=np.float32),
        np.asarray(L0_ffn_w_down, dtype=np.float32))
    nc = build_program()
    res = run_bass_kernel_spmd(nc, in_maps, core_ids=list(range(8)))
    out = np.stack([r["y"] for r in res.results], axis=0)
    return out.reshape(16, SEQ, D)


# version 15
# speedup vs baseline: 1.0172x; 1.0172x over previous
from contextlib import ExitStack

import numpy as np
import concourse.bass as bass
import concourse.mybir as mybir
from concourse.bass_utils import run_bass_kernel_spmd

F32 = mybir.dt.float32
BF16 = mybir.dt.bfloat16
I16 = mybir.dt.int16
U8 = mybir.dt.uint8
AF = mybir.ActivationFunctionType
ALU = mybir.AluOpType
AX = mybir.AxisListType

D = 1024
INNER = 4096
H = 16
DH = 64
SEQ = 2048
NSEQ = 2
NTOK = NSEQ * SEQ
TT = 512
NTILE = NTOK // TT
TC = TT // 128
EPS = 1e-5
NKV = 32
C32M = 12582912.0
C52M = 6755399441055744.0

ENGS = ("sync", "vector", "scalar", "tensor", "gpsimd")

STAGE_FFN1 = True
STAGE_ATTN = True
STAGE_FFN2 = True


class Builder:
    def __init__(self):
        self.nc = bass.Bass("TRN2", target_bir_lowering=False)
        self.ctx = ExitStack()
        self.q = {e: [] for e in ENGS}
        self.sems = {}
        self.semctr = {}
        self.last_tick = {e: None for e in ENGS}

    def sem(self, name):
        if name not in self.sems:
            self.sems[name] = self.ctx.enter_context(self.nc.semaphore(name))
            self.semctr[name] = 0
        return self.sems[name]

    def semval(self, name):
        self.sem(name)
        return self.semctr[name]

    def emit(self, eng, fn, tick=None, tickn=1):
        if tick is None:
            self.q[eng].append(fn)
            self.last_tick[eng] = None
            return None
        h = self.sem(tick)
        self.semctr[tick] += tickn
        val = self.semctr[tick]
        self.q[eng].append(lambda E, f=fn, h=h, n=tickn: f(E).then_inc(h, n))
        self.last_tick[eng] = (tick, val)
        return val

    def tick_last(self, eng, name, n=1):
        if self.last_tick[eng] is not None:
            return self.emit(eng, lambda E: E.nop(), tick=name, tickn=n)
        h = self.sem(name)
        self.semctr[name] += n
        val = self.semctr[name]
        last = self.q[eng][-1]
        assert not isinstance(last, tuple)
        self.q[eng][-1] = (lambda E, f=last, h=h, m=n: f(E).then_inc(h, m))
        self.last_tick[eng] = (name, val)
        return val

    def wait(self, eng, name, val):
        if val <= 0:
            self.sem(name)
            return
        h = self.sem(name)
        self.q[eng].append(("w", h, val))
        self.last_tick[eng] = None

    def link(self, eng):
        if self.last_tick[eng] is not None:
            name, v = self.last_tick[eng]
            self.wait(eng, name, v)
            return
        name = f"_lk_{eng}"
        v = self.tick_last(eng, name)
        self.wait(eng, name, v)

    def dma(self, out, in_, tick, eng="sync"):
        return self.emit(eng, lambda E, o=out, i=in_: E.dma_start(out=o, in_=i),
                         tick=tick, tickn=16)

    def run_block(self):
        block = self.ctx.enter_context(self.nc.Block())

        def mk(engname):
            def body(E):
                for item in self.q[engname]:
                    if isinstance(item, tuple):
                        _, h, val = item
                        E.wait_ge(h, val)
                    else:
                        item(E)
            return body

        block.sync(mk("sync"))
        block.vector(mk("vector"))
        block.scalar(mk("scalar"))
        block.tensor(mk("tensor"))
        block.gpsimd(mk("gpsimd"))


def build_program():
    b = Builder()
    nc = b.nc
    e = b.ctx.enter_context

    x0 = nc.dram_tensor("x0", [NTOK, D], F32, kind="ExternalInput").ap()
    wq_f = nc.dram_tensor("wq", [D, D], F32, kind="ExternalInput").ap()
    wk_f = nc.dram_tensor("wk", [D, D], F32, kind="ExternalInput").ap()
    wv_f = nc.dram_tensor("wv", [D, D], F32, kind="ExternalInput").ap()
    wo_f = nc.dram_tensor("wo", [D, D], F32, kind="ExternalInput").ap()
    wgT = nc.dram_tensor("wgT", [D, INNER], F32, kind="ExternalInput").ap()
    wuT = nc.dram_tensor("wuT", [D, INNER], F32, kind="ExternalInput").ap()
    wdT = nc.dram_tensor("wdT", [INNER, D], F32, kind="ExternalInput").ap()
    consts = nc.dram_tensor("consts", [128, 8], F32, kind="ExternalInput").ap()
    indic = nc.dram_tensor("indic", [64, SEQ], BF16, kind="ExternalInput").ap()
    ident = nc.dram_tensor("ident", [128, 128], BF16, kind="ExternalInput").ap()
    identf = nc.dram_tensor("identf", [128, 128], F32, kind="ExternalInput").ap()
    y = nc.dram_tensor("y", [NTOK, D], F32, kind="ExternalOutput").ap()

    x1d = nc.dram_tensor("x1d", [NTOK, D], F32, kind="Internal").ap()
    x2d = nc.dram_tensor("x2d", [NTOK, D], F32, kind="Internal").ap()
    wg_q = nc.dram_tensor("wg_q", [D, INNER], BF16, kind="Internal").ap()
    wu_q = nc.dram_tensor("wu_q", [D, INNER], BF16, kind="Internal").ap()
    wd_q = nc.dram_tensor("wd_q", [INNER, D], BF16, kind="Internal").ap()
    wq_b = nc.dram_tensor("wq_b", [D, D], BF16, kind="Internal").ap()
    wk_b = nc.dram_tensor("wk_b", [D, D], BF16, kind="Internal").ap()
    wv_b = nc.dram_tensor("wv_b", [D, D], BF16, kind="Internal").ap()
    wo_b = nc.dram_tensor("wo_b", [D, D], BF16, kind="Internal").ap()

    cst = e(nc.sbuf_tensor("cst", [128, 8], F32))
    nc.const_aps.register(F32, 256.0, cst[:, 0:1])
    nc.const_aps.register(F32, EPS, cst[:, 1:2])
    nc.const_aps.register(F32, 0.125, cst[:, 2:3])
    ones_r = e(nc.sbuf_tensor("ones_r", [1, 128], F32))
    ones_c = e(nc.sbuf_tensor("ones_c", [128, 1], F32))
    mu = e(nc.sbuf_tensor("mu", [128, 4], F32))
    idt = e(nc.sbuf_tensor("idt", [128, 128], BF16))
    idtf = e(nc.sbuf_tensor("idtf", [128, 128], F32))
    stats = e(nc.sbuf_tensor("stats", [128, 72], F32))
    dump = e(nc.sbuf_tensor("dump", [128, D], F32))

    from types import SimpleNamespace

    def mkscope():
        return ExitStack()

    _alloc_n = [0]

    def alloc(scope, shape, dt):
        _alloc_n[0] += 1
        t = scope.enter_context(
            nc.sbuf_tensor(f"t{_alloc_n[0]}", [128] + list(shape), dt))
        return t[:]

    def barrier(done=()):
        vals = {}
        for eng in ("sync", "vector", "scalar", "tensor", "gpsimd"):
            vals[eng] = b.emit(eng, lambda E: E.nop(), tick=f"ph_{eng}")
        for eng in ("sync", "vector", "scalar", "tensor", "gpsimd"):
            for oeng, v in vals.items():
                if oeng != eng:
                    b.wait(eng, f"ph_{oeng}", v)
            for nm, v in done:
                b.wait(eng, nm, v)

    ps = [e(nc.psum_tensor(f"psbank{i}", [128, 512], F32)) for i in range(8)]
    psb = [p[:].bitcast(BF16) for p in ps]

    bank_mm = [0] * 8
    bank_ev = [0] * 8

    def mm_group(matmuls, bank):
        b.wait("tensor", f"bev{bank}", bank_ev[bank])
        for m in matmuls[:-1]:
            b.emit("tensor", m)
        v = b.emit("tensor", matmuls[-1], tick=f"bmm{bank}")
        bank_mm[bank] = v
        return v

    def evac(eng, fn, bank):
        b.wait(eng, f"bmm{bank}", bank_mm[bank])
        v = b.emit(eng, fn, tick=f"bev{bank}")
        bank_ev[bank] = v
        return v

    b.dma(cst[:], consts[:], "cld")
    b.dma(idt[:], ident[:], "cld")
    b.dma(idtf[:], identf[:], "cld")
    for eng in ("vector", "scalar", "tensor", "gpsimd"):
        b.wait(eng, "cld", 48)
    b.emit("vector", lambda E: E.memset(ones_r[:], 1.0))
    vo = b.emit("vector", lambda E: E.memset(ones_c[:], 1.0), tick="vones")
    b.wait("tensor", "vones", vo)
    b.emit("vector", lambda E: E.memset(cst[:, 3:4], 0.75))
    b.link("vector")
    b.emit("vector", lambda E: E.tensor_scalar(
        cst[:, 4:5], cst[:, 3:4], C52M, None, ALU.add))
    b.link("vector")
    b.emit("vector", lambda E: E.tensor_scalar(
        cst[:, 4:5], cst[:, 4:5], -C52M, None, ALU.add))
    b.link("vector")
    b.emit("vector", lambda E: E.tensor_scalar(
        cst[:, 5:6], cst[:, 4:5], -1.0, 1.0, ALU.mult, ALU.add))
    b.link("vector")
    b.emit("vector", lambda E: E.tensor_scalar(
        cst[:, 6:7], cst[:, 4:5], -500.0, None, ALU.mult), tick="wcst")

    wq_scope = mkscope()
    awbuf = alloc(wq_scope, [4, D], BF16)
    for (src, dst) in ((wq_f, wq_b), (wk_f, wk_b), (wv_f, wv_b), (wo_f, wo_b)):
        for ch in range(2):
            seg = src[ch * 512:(ch + 1) * 512, :].rearrange("(c p) f -> p c f", p=128)
            lv = b.dma(awbuf, seg, "awld", eng="gpsimd")
            b.wait("sync", "awld", lv)
            dseg = dst[ch * 512:(ch + 1) * 512, :].rearrange("(c p) f -> p c f", p=128)
            sv = b.dma(dseg, awbuf, "awst")
            b.wait("gpsimd", "awst", sv)

    wqin8 = alloc(wq_scope, [8, INNER], F32)
    wqin32 = wqin8.rearrange("p a b -> p (a b)").rearrange(
        "p (a b) -> p a b", a=32)
    wqv = alloc(wq_scope, [INNER], F32)
    wqq0 = alloc(wq_scope, [INNER], I16)
    wqr = alloc(wq_scope, [INNER], BF16)
    wqhi = alloc(wq_scope, [INNER], F32)
    wqleaf = alloc(wq_scope, [1024], F32)

    def wquant(w_src, w_dst, nrow, ncol, mucol):
        nch = nrow // 128
        wqin = wqin8 if nch == 8 else wqin32
        b.wait("vector", "bmm0", bank_mm[0])
        b.wait("vector", "bmm1", bank_mm[1])
        b.wait("sync", "awst", b.semval("awst"))
        b.wait("sync", "wq_v", b.semval("wq_v"))
        b.wait("sync", "wq_s", b.semval("wq_s"))
        lv = b.dma(wqin[:, 0:nch, 0:ncol],
                   w_src[:].rearrange("(c p) f -> p c f", p=128), "wqld")
        b.wait("vector", "wqld", lv)
        b.wait("scalar", "wqld", lv)
        MC = float((1 << 23) + (1 << 22))
        nl = ncol // 128
        b.emit("vector", lambda E: E.memset(stats[:, 43:44], 0.0))
        b.link("vector")
        for c in range(nch):
            b.wait("scalar", "wq_v", b.semval("wq_v"))
            b.emit("scalar", lambda E, c=c, n=ncol: E.activation(
                wqv[:, 0:n], wqin[:, c, 0:n], AF.Abs), tick="wq_s")
            b.wait("vector", "wq_s", b.semval("wq_s"))
            b.emit("vector", lambda E, n=ncol: E.tensor_scalar(
                wqhi[:, 0:n], wqv[:, 0:n], float(2 ** 20), MC, ALU.mult,
                ALU.add))
            b.link("vector")
            b.emit("vector", lambda E, n=ncol: E.tensor_scalar(
                wqhi[:, 0:n], wqhi[:, 0:n], -MC, None, ALU.add))
            b.link("vector")
            b.emit("vector", lambda E, n=ncol: E.scalar_tensor_tensor(
                wqv[:, 0:n], wqv[:, 0:n], float(2 ** 20), wqhi[:, 0:n],
                ALU.mult, ALU.subtract))
            b.link("vector")
            b.emit("vector", lambda E, n=ncol: E.tensor_reduce(
                stats[:, 42:43], wqv[:, 0:n], AX.X, ALU.add))
            b.link("vector")
            b.emit("vector", lambda E: E.tensor_add(
                stats[:, 43:44], stats[:, 43:44], stats[:, 42:43]))
            b.link("vector")
            b.emit("vector", lambda E, c=c, n=ncol, nl=nl: E.tensor_reduce(
                wqleaf[:, c * nl:(c + 1) * nl],
                wqhi[:, 0:n].rearrange("p (a x) -> p a x", x=128),
                AX.X, ALU.add), tick="wq_v")
        nlf = nch * nl
        b.wait("vector", "wq_v", b.semval("wq_v"))
        b.emit("vector", lambda E, m=nlf: E.tensor_scalar(
            wqhi[:, 0:m], wqleaf[:, 0:m], float(2 ** -10), MC, ALU.mult,
            ALU.add))
        b.link("vector")
        b.emit("vector", lambda E, m=nlf: E.tensor_scalar(
            wqhi[:, 0:m], wqhi[:, 0:m], -MC, None, ALU.add))
        b.link("vector")
        b.emit("vector", lambda E, m=nlf: E.scalar_tensor_tensor(
            wqleaf[:, 0:m], wqhi[:, 0:m], float(-(2 ** 10)), wqleaf[:, 0:m],
            ALU.mult, ALU.add))
        b.link("vector")
        b.emit("vector", lambda E, m=nlf: E.tensor_reduce(
            stats[:, 36:37], wqhi[:, 0:m], AX.X, ALU.add))
        b.emit("vector", lambda E, m=nlf: E.tensor_reduce(
            stats[:, 37:38], wqleaf[:, 0:m], AX.X, ALU.add))
        b.link("vector")
        b.emit("vector", lambda E: E.tensor_scalar(
            stats[:, 38:39], stats[:, 36:37], float(2 ** -9), MC, ALU.mult,
            ALU.add))
        b.emit("vector", lambda E: E.tensor_scalar(
            stats[:, 40:41], stats[:, 37:38], float(2 ** -5), MC, ALU.mult,
            ALU.add))
        b.link("vector")
        b.emit("vector", lambda E: E.tensor_scalar(
            stats[:, 38:39], stats[:, 38:39], -MC, None, ALU.add))
        b.emit("vector", lambda E: E.tensor_scalar(
            stats[:, 40:41], stats[:, 40:41], -MC, None, ALU.add))
        b.link("vector")
        b.emit("vector", lambda E: E.scalar_tensor_tensor(
            stats[:, 39:40], stats[:, 38:39], float(-(2 ** 9)),
            stats[:, 36:37], ALU.mult, ALU.add))
        b.emit("vector", lambda E: E.scalar_tensor_tensor(
            stats[:, 41:42], stats[:, 40:41], float(-(2 ** 5)),
            stats[:, 37:38], ALU.mult, ALU.add), tick="wq_v")
        b.wait("tensor", "wq_v", b.semval("wq_v"))
        mm_group([lambda E: E.matmul(ps[0][0:1, 0:6], ones_c[:, 0:1],
                                     stats[:, 38:44], start=True,
                                     stop=True)], 0)
        evac("vector", lambda E: E.tensor_copy(stats[0:1, 38:44],
                                               ps[0][0:1, 0:6]), 0)
        b.link("vector")
        b.emit("vector", lambda E: E.scalar_tensor_tensor(
            stats[0:1, 38:39], stats[0:1, 38:39], float(2 ** 9),
            stats[0:1, 39:40], ALU.mult, ALU.add))
        b.emit("vector", lambda E: E.scalar_tensor_tensor(
            stats[0:1, 40:41], stats[0:1, 40:41], float(2 ** 5),
            stats[0:1, 41:42], ALU.mult, ALU.add))
        b.link("vector")
        b.emit("vector", lambda E: E.scalar_tensor_tensor(
            stats[0:1, 38:39], stats[0:1, 38:39], float(2 ** 10),
            stats[0:1, 40:41], ALU.mult, ALU.add))
        b.link("vector")
        b.emit("vector", lambda E: E.tensor_add(
            stats[0:1, 38:39], stats[0:1, 38:39], stats[0:1, 43:44]))
        b.link("vector")
        b.emit("vector", lambda E, n=nrow * ncol: E.tensor_scalar(
            stats[0:1, 33:34], stats[0:1, 38:39], float(2.0 ** -20 / n), EPS,
            ALU.mult, ALU.max), tick="wqv")
        b.wait("tensor", "wqv", b.semval("wqv"))
        b.wait("tensor", "bev0", bank_ev[0])
        mm_group([lambda E: E.matmul(ps[1][:, 0:1], ones_r[0:1, :],
                                     stats[0:1, 33:34], start=True,
                                     stop=True)], 1)
        evac("vector", lambda E, mc=mucol: E.tensor_copy(
            mu[:, mc:mc + 1], ps[1][:, 0:1]), 1)
        b.link("vector")
        b.emit("vector", lambda E, mc=mucol: E.reciprocal(
            stats[:, 34:35], mu[:, mc:mc + 1]))
        b.link("vector")
        b.emit("vector", lambda E, mc=mucol: E.scalar_tensor_tensor(
            stats[:, 35:36], stats[:, 34:35], -1.0, mu[:, mc:mc + 1],
            ALU.mult, ALU.mult))
        b.link("vector")
        b.emit("vector", lambda E: E.tensor_scalar(
            stats[:, 35:36], stats[:, 35:36], 2.0, None, ALU.add))
        b.link("vector")
        b.emit("vector", lambda E: E.tensor_mul(
            stats[:, 34:35], stats[:, 34:35], stats[:, 35:36]))
        b.link("vector")
        for c in range(nch):
            b.emit("vector", lambda E, c=c, n=ncol: E.tensor_scalar(
                wqv[:, 0:n], wqin[:, c, 0:n], stats[:, 34:35], None,
                ALU.mult), tick="wq_v")
            b.wait("scalar", "wq_v", b.semval("wq_v"))
            b.emit("scalar", lambda E, n=ncol: E.activation(
                wqq0[:, 0:n], wqv[:, 0:n], AF.Copy, bias=256.75, scale=1.0),
                tick="wq_s")
            b.wait("vector", "wq_s", b.semval("wq_s"))
            b.emit("vector", lambda E, n=ncol: E.scalar_tensor_tensor(
                wqv[:, 0:n], wqq0[:, 0:n], -256.0, wqv[:, 0:n], ALU.add,
                ALU.subtract))
            b.link("vector")
            b.emit("vector", lambda E, n=ncol: E.scalar_tensor_tensor(
                wqv[:, 0:n], wqv[:, 0:n], 0.5, wqq0[:, 0:n], ALU.is_le,
                ALU.add))
            b.link("vector")
            b.emit("vector", lambda E, n=ncol: E.tensor_scalar(
                wqr[:, 0:n], wqv[:, 0:n], -257.0, 1.0, ALU.add, ALU.min))
            b.link("vector")
            b.emit("vector", lambda E, n=ncol: E.tensor_scalar(
                wqr[:, 0:n], wqr[:, 0:n], -1.0, None, ALU.max), tick="wq_v")
            b.wait("sync", "wq_v", b.semval("wq_v"))
            sv = b.dma(w_dst[c * 128:(c + 1) * 128, 0:ncol],
                       wqr[:, 0:ncol], "wq_st")
            b.wait("vector", "wq_st", sv)
            b.wait("scalar", "wq_st", sv)

    wquant(wgT, wg_q, D, INNER, 0)
    wquant(wuT, wu_q, D, INNER, 1)
    wquant(wdT, wd_q, INNER, D, 2)
    barrier(done=[("wq_st", b.semval("wq_st")), ("awst", b.semval("awst"))])
    wq_scope.close()

    def ffn_bufs(scope):
        Bf = SimpleNamespace()
        Bf.xt = [alloc(scope, [TC, D], F32), alloc(scope, [TC, D], F32)]
        Bf.vql = alloc(scope, [TC, D], F32)
        Bf.q0x = alloc(scope, [D], I16)
        Bf.xq_fm = alloc(scope, [8, TT], BF16)
        Bf.gu = alloc(scope, [TC, INNER], F32)
        Bf.hv = alloc(scope, [INNER], F32)
        Bf.hq_fm = alloc(scope, [32, TT], BF16)
        Bf.wbuf = alloc(scope, [2, 8, 512], BF16)
        Bf.wdbuf = alloc(scope, [2, D], BF16)
        return Bf

    def ffn_sublayer(Bf, x_src, x_dst, tag):
        xt, vql, q0x, xq_fm = Bf.xt, Bf.vql, Bf.q0x, Bf.xq_fm
        gu, hv, hq_fm, wbuf, wdbuf = (Bf.gu, Bf.hv, Bf.hq_fm, Bf.wbuf,
                                      Bf.wdbuf)
        wfree_state = [None, None]
        lad_guard = [None]
        xqT_done = [None]

        def stage1(it):
            p = it % 2
            so = 16 * p
            qo = 32 + 16 * p
            X = xt[p]
            if it >= 2:
                b.wait("sync", f"{tag}stx{p}", (it // 2) * 16)
            if lad_guard[0] is not None:
                b.wait("sync", f"{tag}v", lad_guard[0])
            src = x_src[it * TT:(it + 1) * TT, :].rearrange(
                "(c p) f -> p c f", p=128)
            lv = b.dma(X, src, f"{tag}ldx{p}")
            b.wait("vector", f"{tag}ldx{p}", lv)
            b.wait("scalar", f"{tag}ldx{p}", lv)
            b.wait("scalar", f"{tag}v", b.semval(f"{tag}v"))
            for tc in range(TC):
                b.emit("vector", lambda E, tc=tc, X=X, so=so: E.tensor_reduce(
                    stats[:, so + tc:so + tc + 1], X[:, tc], AX.X, ALU.add))
                b.emit("vector", lambda E, tc=tc, X=X, so=so: E.tensor_reduce(
                    stats[:, so + 8 + tc:so + 9 + tc], X[:, tc], AX.X, ALU.max))
                b.emit("vector", lambda E, tc=tc, X=X, so=so: E.tensor_reduce(
                    stats[:, so + 12 + tc:so + 13 + tc], X[:, tc], AX.X,
                    ALU.min))
            b.link("vector")
            for tc in range(TC):
                if tc > 0:
                    b.link("scalar")
                b.emit("scalar", lambda E, tc=tc, X=X, so=so: E.activation(
                    dump[:, :], X[:, tc], AF.Square,
                    accum_out=stats[:, so + 4 + tc:so + 5 + tc]),
                    tick=f"{tag}s")
            ssq = b.semval(f"{tag}s")
            mcol = stats[:, so:so + 4]
            vcol = stats[:, so + 4:so + 8]
            mxc = stats[:, so + 8:so + 12]
            mnc = stats[:, so + 12:so + 16]
            shc = stats[:, qo:qo + 4]
            dqg = stats[:, qo + 4:qo + 8]
            dqu = stats[:, qo + 8:qo + 12]
            dqd = stats[:, 64 + 4 * p:68 + 4 * p]
            b.wait("vector", f"{tag}s", ssq)
            b.emit("vector", lambda E, m=mcol: E.tensor_scalar(
                m, m, 1.0 / D, None, ALU.mult))
            b.link("vector")
            b.emit("vector", lambda E, v=vcol: E.tensor_scalar(
                v, v, 1.0 / D, None, ALU.mult))
            b.link("vector")
            b.emit("vector", lambda E, m=mcol: E.scalar_tensor_tensor(
                dump[:, 0:4], m, 1.0, m, ALU.bypass, ALU.mult))
            b.link("vector")
            b.emit("vector", lambda E, v=vcol: E.tensor_sub(
                v, v, dump[:, 0:4]), tick=f"{tag}v")
            b.wait("scalar", f"{tag}v", b.semval(f"{tag}v"))
            b.emit("scalar", lambda E, v=vcol: E.activation(
                dump[:, 4:8], v, AF.Sqrt, bias=EPS, scale=1.0),
                tick=f"{tag}s")
            b.wait("vector", f"{tag}s", b.semval(f"{tag}s"))
            b.emit("vector", lambda E: E.reciprocal(dump[:, 8:12],
                                                    dump[:, 4:8]))
            b.link("vector")
            b.emit("vector", lambda E, v=vcol: E.tensor_scalar(
                dump[:, 12:16], v, 1.0, EPS, ALU.mult, ALU.add))
            b.link("vector")
            b.emit("vector", lambda E: E.scalar_tensor_tensor(
                dump[:, 16:20], dump[:, 8:12], 1.0, dump[:, 8:12],
                ALU.bypass, ALU.mult))
            b.link("vector")
            b.emit("vector", lambda E: E.tensor_mul(
                dump[:, 16:20], dump[:, 16:20], dump[:, 12:16]))
            b.link("vector")
            b.emit("vector", lambda E: E.tensor_scalar(
                dump[:, 16:20], dump[:, 16:20], -0.5, 1.5, ALU.mult,
                ALU.add))
            b.link("vector")
            b.emit("vector", lambda E, v=vcol: E.tensor_mul(
                v, dump[:, 8:12], dump[:, 16:20]))
            b.link("vector")
            b.emit("vector", lambda E, mx=mxc, m=mcol: E.tensor_sub(mx, mx, m))
            b.link("vector")
            b.emit("vector", lambda E, mn=mnc, m=mcol: E.tensor_sub(mn, m, mn))
            b.link("vector")
            b.emit("vector", lambda E, mx=mxc, mn=mnc: E.tensor_max(mx, mx, mn))
            b.link("vector")
            b.emit("vector", lambda E, mx=mxc, v=vcol: E.tensor_mul(mx, mx, v))
            b.link("vector")
            b.emit("vector", lambda E, mx=mxc: E.tensor_scalar(
                mx, mx, EPS, None, ALU.max))
            b.link("vector")
            b.emit("vector", lambda E, d=dqg, mx=mxc: E.scalar_tensor_tensor(
                d, mx, 1.0 / 127.0, mu[:, 0:1].broadcast_to([128, 4]),
                ALU.mult, ALU.mult))
            b.emit("vector", lambda E, d=dqu, mx=mxc: E.scalar_tensor_tensor(
                d, mx, 1.0 / 127.0, mu[:, 1:2].broadcast_to([128, 4]),
                ALU.mult, ALU.mult))
            b.link("vector")
            b.emit("vector", lambda E, mx=mxc: E.reciprocal(mx, mx))
            b.link("vector")
            b.emit("vector", lambda E, mx=mxc: E.tensor_scalar(
                mx, mx, 127.0, None, ALU.mult))
            b.link("vector")
            b.emit("vector", lambda E, mx=mxc, v=vcol: E.tensor_mul(
                mx, mx, v))
            b.link("vector")
            b.emit("vector", lambda E, mn=mnc, m=mcol, mx=mxc:
                   E.scalar_tensor_tensor(mn, m, -1.0, mx, ALU.mult,
                                          ALU.mult), tick=f"{tag}v")
            vAB = b.semval(f"{tag}v")
            if xqT_done[0] is not None:
                b.wait("vector", *xqT_done[0])
            for tc in range(TC):
                if tc == 0:
                    b.wait("vector", f"{tag}v", vAB)
                b.emit("vector", lambda E, tc=tc, X=X, mx=mxc, mn=mnc:
                       E.tensor_scalar(vql[:, tc], X[:, tc],
                                       mx[:, tc:tc + 1],
                                       mn[:, tc:tc + 1], ALU.mult, ALU.add),
                       tick=f"{tag}v")
                b.wait("scalar", f"{tag}v", b.semval(f"{tag}v"))
                b.emit("scalar", lambda E, tc=tc: E.activation(
                    q0x[:], vql[:, tc], AF.Copy, bias=256.75, scale=1.0),
                    tick=f"{tag}s")
                b.wait("vector", f"{tag}s", b.semval(f"{tag}s"))
                b.emit("vector", lambda E, tc=tc: E.scalar_tensor_tensor(
                    vql[:, tc], q0x[:], -256.0, vql[:, tc], ALU.add,
                    ALU.subtract))
                b.link("vector")
                b.emit("vector", lambda E, tc=tc: E.scalar_tensor_tensor(
                    vql[:, tc], vql[:, tc], 0.5, q0x[:], ALU.is_le,
                    ALU.add))
                b.link("vector")
                b.emit("vector", lambda E, tc=tc: E.tensor_scalar(
                    vql[:, tc], vql[:, tc], -257.0, None, ALU.add),
                    tick=f"{tag}v")
            return b.semval(f"{tag}v")

        def evac_cb(it, tdown):
            p = it % 2
            X = xt[p]
            dqd = stats[:, 64 + 4 * p:68 + 4 * p]
            b.wait("vector", f"{tag}dmm", tdown)
            for dc in range(2):
                for tc in range(TC):
                    bk = dc * 4 + tc
                    bank_ev[bk] = b.emit(
                        "vector", lambda E, dc=dc, tc=tc, bk=bk, X=X, d=dqd:
                        E.scalar_tensor_tensor(
                            X[:, tc, dc * 512:(dc + 1) * 512], ps[bk][:],
                            d[:, tc:tc + 1],
                            X[:, tc, dc * 512:(dc + 1) * 512],
                            ALU.mult, ALU.add), tick=f"bev{bk}")
            b.wait("sync", f"bev{7}", bank_ev[7])
            dstv = x_dst[it * TT:(it + 1) * TT, :].rearrange(
                "(c p) f -> p c f", p=128)
            b.dma(dstv, X, f"{tag}stx{p}")

        def stage2(it, chain_v):
            p = it % 2
            so = 16 * p
            qo = 32 + 16 * p
            X = xt[p]
            shc = stats[:, qo:qo + 4]
            dqg = stats[:, qo + 4:qo + 8]
            dqu = stats[:, qo + 8:qo + 12]
            dqd = stats[:, 64 + 4 * p:68 + 4 * p]
            xq_ev = []
            b.wait("tensor", f"{tag}v", chain_v)
            for tc in range(TC):
                for g2 in range(2):
                    tb = 4 + g2
                    b.wait("tensor", f"bev{tb}", bank_ev[tb])
                    for j in range(4):
                        ic = g2 * 4 + j
                        b.emit("tensor", lambda E, tc=tc, ic=ic, j=j, tb=tb:
                               E.transpose(
                                   ps[tb][:, j * 128:(j + 1) * 128],
                                   vql[:, tc, ic * 128:(ic + 1) * 128],
                                   idtf[:]))
                    bank_mm[tb] = b.tick_last("tensor", f"bmm{tb}")
                    ev = evac("scalar", lambda E, tc=tc, g2=g2, tb=tb: E.copy(
                        xq_fm[:, g2 * 4:(g2 + 1) * 4,
                              tc * 128:(tc + 1) * 128],
                        ps[tb][:, 0:512].rearrange("p (a t) -> p a t", a=4)),
                        tb)
                xq_ev.append((f"bmm{4 + 1}", bank_mm[5]))
                xq_ev.append((f"bev{tb}", ev))
            xqT_done[0] = (f"bmm{5}", bank_mm[5])
            wfree = wfree_state
            silu_last = None
            up_last = None
            for phase in range(2):
                wsrc = wg_q if phase == 0 else wu_q
                if phase == 1:
                    b.wait("vector", *silu_last)
                for oc in range(8):
                    wp = (phase * 8 + oc) % 2
                    if wfree[wp] is not None:
                        b.wait("sync", *wfree[wp])
                    wl = b.dma(wbuf[:, wp],
                               wsrc[:, oc * 512:(oc + 1) * 512].rearrange(
                                   "(c p) f -> p c f", p=128), f"{tag}wld{wp}")
                    b.wait("tensor", f"{tag}wld{wp}", wl)
                    if phase == 0 and oc == 0:
                        b.wait("tensor", *xq_ev[-1])
                        b.wait("tensor", *xq_ev[-2])
                    for tc in range(TC):
                        bk = ((phase * 8 + oc) % 2) * 4 + tc
                        mms = [
                            (lambda E, ic=ic, tc=tc, wp=wp, bk=bk: E.matmul(
                                ps[bk][:],
                                xq_fm[:, ic, tc * 128:(tc + 1) * 128],
                                wbuf[:, wp, ic], start=(ic == 0),
                                stop=(ic == 7)))
                            for ic in range(8)]
                        mm_group(mms, bk)
                        if phase == 0:
                            evac("scalar",
                                 lambda E, tc=tc, oc=oc, bk=bk, d=dqg:
                                 E.activation(
                                     gu[:, tc, oc * 512:(oc + 1) * 512],
                                     ps[bk][:], AF.Silu,
                                     scale=d[:, tc:tc + 1]), bk)
                        else:
                            evac("vector",
                                 lambda E, tc=tc, oc=oc, bk=bk, d=dqu:
                                 E.scalar_tensor_tensor(
                                     gu[:, tc, oc * 512:(oc + 1) * 512],
                                     ps[bk][:], d[:, tc:tc + 1],
                                     gu[:, tc, oc * 512:(oc + 1) * 512],
                                     ALU.mult, ALU.mult), bk)
                    bklast = ((phase * 8 + oc) % 2) * 4 + 3
                    wfree[wp] = (f"bmm{bklast}", bank_mm[bklast])
                    if phase == 0:
                        silu_last = (f"bev{bklast}", bank_ev[bklast])
                    else:
                        up_last = (f"bev{bklast}", bank_ev[bklast])
            b.wait("vector", *silu_last)
            b.wait("vector", *up_last)
            for tc in range(TC):
                b.emit("vector", lambda E, tc=tc, s=shc: E.tensor_reduce(
                    s[:, tc:tc + 1], gu[:, tc], AX.X, ALU.max,
                    apply_absolute_value=True))
            b.link("vector")
            b.emit("vector", lambda E, s=shc: E.tensor_scalar(
                s, s, EPS, None, ALU.max))
            b.link("vector")
            b.emit("vector", lambda E, d=dqd, s=shc: E.scalar_tensor_tensor(
                d, s, 0.5 / 127.0, mu[:, 2:3].broadcast_to([128, 4]),
                ALU.mult, ALU.mult))
            b.link("vector")
            b.emit("vector", lambda E, s=shc: E.reciprocal(s, s))
            b.link("vector")
            b.emit("vector", lambda E, s=shc: E.tensor_scalar(
                s, s, 127.0, None, ALU.mult), tick=f"{tag}v")
            vsh = b.semval(f"{tag}v")
            scr = xt[1 - p].rearrange("p a d -> p (a d)")
            if it >= 1:
                b.wait("scalar", f"{tag}stx{1 - p}",
                       ((it - 1) // 2 + 1) * 16)
                b.wait("scalar", "bev7", b.semval("bev7"))
            hq_ev = []
            prevG = None
            for tc in range(TC):
                b.wait("scalar", f"{tag}v", vsh)
                if prevG is not None:
                    b.wait("scalar", f"{tag}v", prevG)
                b.emit("scalar", lambda E, tc=tc, s=shc, scr=scr:
                       E.activation(scr, gu[:, tc], AF.Copy, bias=500.0,
                                    scale=s[:, tc:tc + 1]), tick=f"{tag}s")
                bB = b.semval(f"{tag}s")
                b.wait("vector", f"{tag}v", vsh)
                b.emit("vector", lambda E, tc=tc, s=shc: E.tensor_scalar(
                    hv[:], gu[:, tc], s[:, tc:tc + 1], C32M, ALU.mult,
                    ALU.add))
                b.link("vector")
                b.wait("vector", f"{tag}s", bB)
                b.emit("vector", lambda E, tc=tc: E.tensor_scalar(
                    gu[:, tc], hv[:], -C32M, None, ALU.add), tick=f"{tag}v")
                vC = b.semval(f"{tag}v")
                b.wait("scalar", f"{tag}v", vC)
                b.emit("scalar", lambda E, scr=scr: E.activation(
                    hv[:], scr, AF.Copy, bias=C52M, scale=1.0))
                b.link("scalar")
                b.emit("scalar", lambda E, scr=scr: E.activation(
                    scr, hv[:], AF.Copy, bias=-C52M, scale=1.0),
                    tick=f"{tag}s")
                eE = b.semval(f"{tag}s")
                b.link("vector")
                b.emit("vector", lambda E, tc=tc: E.tensor_scalar(
                    gu[:, tc], gu[:, tc], cst[:, 5:6], cst[:, 6:7],
                    ALU.mult, ALU.add))
                b.link("vector")
                b.wait("vector", f"{tag}s", eE)
                b.emit("vector", lambda E, tc=tc, scr=scr:
                       E.scalar_tensor_tensor(
                           gu[:, tc], scr, cst[:, 4:5], gu[:, tc],
                           ALU.mult, ALU.add), tick=f"{tag}v")
                prevG = b.semval(f"{tag}v")
                b.wait("tensor", f"{tag}v", b.semval(f"{tag}v"))
                for g in range(8):
                    tb = 4 + (g % 2)
                    b.wait("tensor", f"bev{tb}", bank_ev[tb])
                    for j in range(4):
                        oc = g * 4 + j
                        b.emit("tensor", lambda E, tc=tc, oc=oc, j=j, tb=tb:
                               E.transpose(
                                   ps[tb][:, j * 128:(j + 1) * 128],
                                   gu[:, tc, oc * 128:(oc + 1) * 128],
                                   idtf[:]))
                    bank_mm[tb] = b.tick_last("tensor", f"bmm{tb}")
                    ev = evac("vector",
                              lambda E, tc=tc, g=g, tb=tb: E.tensor_copy(
                                  hq_fm[:, g * 4:(g + 1) * 4,
                                        tc * 128:(tc + 1) * 128],
                                  ps[tb][:, 0:512].rearrange(
                                      "p (a t) -> p a t", a=4)), tb)
                    hq_ev.append((f"bev{tb}", ev))
                hq_ev.append((f"bmm{4}", bank_mm[4]))
                hq_ev.append((f"bmm{5}", bank_mm[5]))
            lad_guard[0] = b.semval(f"{tag}v")
            b.wait("tensor", *hq_ev[-1])
            b.wait("tensor", *hq_ev[-2])
            for bk in range(8):
                b.wait("tensor", f"bev{bk}", bank_ev[bk])
            for oc in range(32):
                wp = oc % 2
                if oc >= 2:
                    b.wait("sync", f"{tag}dmm", b.semval(f"{tag}dmm") - 1)
                wl = b.dma(wdbuf[:, wp], wd_q[oc * 128:(oc + 1) * 128, :],
                           f"{tag}dld{wp}")
                b.wait("tensor", f"{tag}dld{wp}", wl)
                for dc in range(2):
                    for tc in range(TC):
                        bk = dc * 4 + tc
                        b.emit("tensor",
                               lambda E, oc=oc, dc=dc, tc=tc, wp=wp, bk=bk:
                               E.matmul(ps[bk][:],
                                        hq_fm[:, oc, tc * 128:(tc + 1) * 128],
                                        wdbuf[:, wp, dc * 512:(dc + 1) * 512],
                                        start=(oc == 0), stop=(oc == 31)))
                b.tick_last("tensor", f"{tag}dmm")
            return b.semval(f"{tag}dmm")

        pending = None
        for it in range(NTILE):
            chain_v = stage1(it)
            if pending is not None:
                evac_cb(*pending)
            tdown = stage2(it, chain_v)
            pending = (it, tdown)
        evac_cb(*pending)
        return [(f"{tag}stx0", b.semval(f"{tag}stx0")),
                (f"{tag}stx1", b.semval(f"{tag}stx1"))]


    def attn_core_bufs(scope):
        Ba = SimpleNamespace()
        Ba.o_fm = alloc(scope, [8, SEQ], BF16)
        Ba.v_tm = alloc(scope, [16, D], BF16)
        Ba.k_fm = alloc(scope, [8, SEQ], BF16)
        Ba.q_fm = alloc(scope, [8, SEQ], BF16)
        return Ba

    def attn_ln_bufs(scope):
        Bl = SimpleNamespace()
        Bl.xn_fm = alloc(scope, [8, SEQ], BF16)
        Bl.x1t_a = alloc(scope, [2, D], F32)
        Bl.xn_tm = alloc(scope, [2, D], F32)
        Bl.qkvw = alloc(scope, [2, 8, 128], BF16)
        Bl.vw_r = alloc(scope, [8, D], BF16)
        return Bl

    def attn_head_bufs(scope):
        Bh = SimpleNamespace()
        Bh.p0a = alloc(scope, [SEQ], F32)
        Bh.ppt = [alloc(scope, [SEQ], BF16), alloc(scope, [SEQ], BF16)]
        Bh.sc_B = alloc(scope, [16, NKV], F32)
        Bh.sc_E = alloc(scope, [16, NKV], F32)
        Bh.sc_z = alloc(scope, [16, NKV], F32)
        Bh.sc_m = alloc(scope, [16, NKV], F32)
        Bh.hi16 = alloc(scope, [16, NKV], BF16)
        Bh.lo16 = alloc(scope, [16, NKV], BF16)
        Bh.sc_Mq = alloc(scope, [16, 4], F32)
        Bh.sc_cf = alloc(scope, [16, 4], F32)
        Bh.aughl = alloc(scope, [128], F32)
        Bh.q_aug2 = [alloc(scope, [SEQ], BF16), alloc(scope, [SEQ], BF16)]
        Bh.k_aug2 = [alloc(scope, [SEQ], BF16), alloc(scope, [SEQ], BF16)]
        return Bh

    def attn_wo_bufs(scope):
        Bw = SimpleNamespace()
        Bw.x1t_b = alloc(scope, [2, D], F32)
        Bw.wo_r = alloc(scope, [8, D], BF16)
        return Bw

    def attn_ln_qkv(Ba, Bl, sq, tag):
        o_fm, v_tm, k_fm, q_fm = Ba.o_fm, Ba.v_tm, Ba.k_fm, Ba.q_fm
        xn_fm, x1t_a, xn_tm = Bl.xn_fm, Bl.x1t_a, Bl.xn_tm
        qkvw, vw_r = Bl.qkvw, Bl.vw_r
        row0 = sq * SEQ
        for tc in range(16):
            pp = tc % 2
            so = 48 + 4 * pp
            if tc >= 2:
                b.wait("sync", f"{tag}xnf{pp}", b.semval(f"{tag}xnf{pp}"))
            lv = b.dma(x1t_a[:, pp],
                       x1d[row0 + tc * 128:row0 + (tc + 1) * 128, :],
                       f"{tag}lx{pp}")
            b.wait("vector", f"{tag}lx{pp}", lv)
            b.wait("scalar", f"{tag}lx{pp}", lv)
            XA = x1t_a[:, pp]
            b.wait("scalar", f"{tag}v", b.semval(f"{tag}v"))
            b.emit("vector", lambda E, XA=XA, so=so: E.tensor_reduce(
                stats[:, so:so + 1], XA, AX.X, ALU.add), tick=f"{tag}v")
            b.emit("scalar", lambda E, XA=XA, so=so: E.activation(
                dump[:, :], XA, AF.Square,
                accum_out=stats[:, so + 1:so + 2]), tick=f"{tag}s")
            b.wait("vector", f"{tag}s", b.semval(f"{tag}s"))
            b.wait("vector", f"{tag}v", b.semval(f"{tag}v"))
            b.emit("vector", lambda E, so=so: E.tensor_scalar(
                stats[:, so:so + 2], stats[:, so:so + 2], 1.0 / D, None,
                ALU.mult))
            b.link("vector")
            b.emit("vector", lambda E, so=so: E.scalar_tensor_tensor(
                stats[:, so + 2:so + 3], stats[:, so:so + 1], 1.0,
                stats[:, so:so + 1], ALU.bypass, ALU.mult))
            b.link("vector")
            b.emit("vector", lambda E, so=so: E.tensor_sub(
                stats[:, so + 1:so + 2], stats[:, so + 1:so + 2],
                stats[:, so + 2:so + 3]), tick=f"{tag}v")
            b.wait("scalar", f"{tag}v", b.semval(f"{tag}v"))
            b.emit("scalar", lambda E, so=so: E.activation(
                stats[:, so + 2:so + 3], stats[:, so + 1:so + 2], AF.Sqrt,
                bias=EPS, scale=1.0), tick=f"{tag}s")
            b.wait("vector", f"{tag}s", b.semval(f"{tag}s"))
            b.emit("vector", lambda E, so=so: E.reciprocal(
                stats[:, so + 2:so + 3], stats[:, so + 2:so + 3]))
            b.link("vector")
            b.emit("vector", lambda E, so=so: E.tensor_scalar(
                stats[:, so + 3:so + 4], stats[:, so + 1:so + 2], 1.0, EPS,
                ALU.mult, ALU.add))
            b.link("vector")
            b.emit("vector", lambda E, so=so: E.scalar_tensor_tensor(
                dump[:, 20:21], stats[:, so + 2:so + 3], 1.0,
                stats[:, so + 2:so + 3], ALU.bypass, ALU.mult))
            b.link("vector")
            b.emit("vector", lambda E, so=so: E.tensor_mul(
                dump[:, 20:21], dump[:, 20:21], stats[:, so + 3:so + 4]))
            b.link("vector")
            b.emit("vector", lambda E: E.tensor_scalar(
                dump[:, 20:21], dump[:, 20:21], -0.5, 1.5, ALU.mult,
                ALU.add))
            b.link("vector")
            b.emit("vector", lambda E, so=so: E.tensor_mul(
                stats[:, so + 1:so + 2], stats[:, so + 2:so + 3],
                dump[:, 20:21]))
            b.link("vector")
            b.emit("vector", lambda E, so=so: E.scalar_tensor_tensor(
                stats[:, so:so + 1], stats[:, so:so + 1], -1.0,
                stats[:, so + 1:so + 2], ALU.mult, ALU.mult))
            b.link("vector")
            b.emit("vector", lambda E, XA=XA, pp=pp, so=so: E.tensor_scalar(
                xn_tm[:, pp], XA, stats[:, so + 1:so + 2],
                stats[:, so:so + 1], ALU.mult, ALU.add), tick=f"{tag}v")
            b.wait("tensor", f"{tag}v", b.semval(f"{tag}v"))
            for g2 in range(2):
                tb = 4 + g2
                b.wait("tensor", f"bev{tb}", bank_ev[tb])
                for j in range(4):
                    ic = g2 * 4 + j
                    b.emit("tensor", lambda E, ic=ic, j=j, pp=pp, tb=tb:
                           E.transpose(
                               ps[tb][:, j * 128:(j + 1) * 128],
                               xn_tm[:, pp, ic * 128:(ic + 1) * 128],
                               idtf[:]))
                bank_mm[tb] = b.tick_last("tensor", f"bmm{tb}")
                ev = evac("scalar", lambda E, tc=tc, g2=g2, tb=tb: E.copy(
                    xn_fm[:, g2 * 4:(g2 + 1) * 4, tc * 128:(tc + 1) * 128],
                    ps[tb][:, 0:512].rearrange("p (a t) -> p a t", a=4)), tb)
            b.wait("sync", f"bmm{4}", bank_mm[4])
            b.wait("sync", f"bmm{5}", bank_mm[5])
            b.emit("sync", lambda E: E.nop(), tick=f"{tag}xnf{pp}")
        xn_done = [(f"bev{4}", bank_ev[4]), (f"bev{5}", bank_ev[5])]
        vwl = b.dma(vw_r, wv_b[:, :].rearrange("(c p) f -> p c f", p=128),
                    f"{tag}vwl")
        qk_wfree = {}
        for (wsrcd, dstfm, scl, qt) in ((wq_b, q_fm, 0.125, "q"),
                                        (wk_b, k_fm, 1.0, "k")):
            for dc in range(8):
                wp = dc % 2
                if (qt, wp) in qk_wfree:
                    b.wait("sync", *qk_wfree[(qt, wp)])
                if qt == "k" and dc < 2:
                    b.wait("sync", *qk_wfree[("q", wp)])
                wl = b.dma(qkvw[:, wp],
                           wsrcd[:, dc * 128:(dc + 1) * 128].rearrange(
                               "(c p) f -> p c f", p=128), f"{tag}{qt}wl{wp}")
                b.wait("tensor", f"{tag}{qt}wl{wp}", wl)
                if dc == 0 and qt == "q":
                    for nm, v in xn_done:
                        b.wait("tensor", nm, v)
                for qq in range(4):
                    bk = (dc * 4 + qq) % 4
                    mms = [
                        (lambda E, ic=ic, qq=qq, wp=wp, bk=bk: E.matmul(
                            ps[bk][:], qkvw[:, wp, ic],
                            xn_fm[:, ic, qq * 512:(qq + 1) * 512],
                            start=(ic == 0), stop=(ic == 7)))
                        for ic in range(8)]
                    mm_group(mms, bk)
                    evac("scalar", lambda E, dc=dc, qq=qq, bk=bk, d=dstfm,
                         scl=scl: E.activation(
                             d[:, dc, qq * 512:(qq + 1) * 512], ps[bk][:],
                             AF.Copy, bias=0.0, scale=scl), bk)
                    if qq == 3:
                        qk_wfree[(qt, wp)] = (f"bmm{bk}", bank_mm[bk])
        b.wait("tensor", f"{tag}vwl", vwl)
        for tc in range(16):
            for half in range(2):
                bk = (tc * 2 + half) % 4
                mms = [
                    (lambda E, ic=ic, tc=tc, half=half, bk=bk: E.matmul(
                        ps[bk][:], xn_fm[:, ic, tc * 128:(tc + 1) * 128],
                        vw_r[:, ic, half * 512:(half + 1) * 512],
                        start=(ic == 0), stop=(ic == 7)))
                    for ic in range(8)]
                mm_group(mms, bk)
                evac("scalar", lambda E, tc=tc, half=half, bk=bk: E.copy(
                    v_tm[:, tc, half * 512:(half + 1) * 512], ps[bk][:]), bk)
        vdone = [(f"bev{i}", bank_ev[i]) for i in range(4)]
        return vdone

    def attn_heads(Ba, Bh, sq, tag, vdone):
        o_fm, v_tm, k_fm, q_fm = Ba.o_fm, Ba.v_tm, Ba.k_fm, Ba.q_fm
        p0a, ppt = Bh.p0a, Bh.ppt
        sc_B, sc_E, sc_z, sc_m = Bh.sc_B, Bh.sc_E, Bh.sc_z, Bh.sc_m
        sc_E3 = Bh.sc_E.rearrange("p a b -> p (a b)").rearrange(
            "p (a b c) -> p a b c", a=16, b=4)
        hi16, lo16 = Bh.hi16, Bh.lo16
        sc_Mq, sc_cf, aughl = Bh.sc_Mq, Bh.sc_cf, Bh.aughl
        q_aug2, k_aug2 = Bh.q_aug2, Bh.k_aug2
        row0 = sq * SEQ
        b.dma(k_aug2[0][64:128, :], indic[:], f"{tag}ind")
        b.dma(k_aug2[1][0:64, :], indic[:], f"{tag}ind")
        pB_vals = {}
        augv = {}
        augT_prev = [None]

        def prologue(h):
            pi = h % 2
            ch = h // 2
            qof = 64 * pi
            ka, qa = k_aug2[pi], q_aug2[pi]
            if h >= 2:
                b.wait("vector", f"{tag}pB", pB_vals[h - 2])
            if h == 0:
                for nm, v in vdone:
                    b.wait("vector", nm, v)
            b.emit("vector", lambda E, ch=ch, qof=qof, ka=ka: E.tensor_copy(
                ka[qof:qof + 64, :], k_fm[qof:qof + 64, ch, :]))
            b.emit("vector", lambda E, ch=ch, qof=qof, qa=qa: E.tensor_copy(
                qa[qof:qof + 64, :], q_fm[qof:qof + 64, ch, :]),
                tick=f"{tag}aug")
            augv[h] = b.semval(f"{tag}aug")

        def passA_mm_max(h, qc, kq):
            pi = h % 2
            ch = h // 2
            qof = 64 * pi
            bk = kq % 2
            mm_group([
                (lambda E, qc=qc, kq=kq, bk=bk, ch=ch, qof=qof:
                 E.matmul(ps[bk][:],
                          q_fm[qof:qof + 64, ch,
                               qc * 128:(qc + 1) * 128],
                          k_fm[qof:qof + 64, ch,
                               kq * 512:(kq + 1) * 512],
                          start=True, stop=True,
                          tile_position=(qof, 0)))], bk)
            b.wait("vector", f"bmm{bk}", bank_mm[bk])
            b.emit("vector", lambda E, qc=qc, kq=kq, bk=bk:
                   E.tensor_reduce(
                       sc_B[:, qc, kq * 8:(kq + 1) * 8],
                       ps[bk][:].rearrange("p (a x) -> p a x", a=8),
                       AX.X, ALU.max))
            b.link("vector")
            b.emit("vector", lambda E, qc=qc, kq=kq:
                   E.tensor_reduce(sc_Mq[:, qc, kq:kq + 1],
                                   sc_B[:, qc, 0:(kq + 1) * 8], AX.X,
                                   ALU.max), tick=f"{tag}v")
            nmcol = 56 + bk
            b.wait("scalar", f"{tag}v", b.semval(f"{tag}v"))
            b.emit("scalar", lambda E, qc=qc, kq=kq, nmcol=nmcol:
                   E.activation(stats[:, nmcol:nmcol + 1],
                                sc_Mq[:, qc, kq:kq + 1], AF.Copy,
                                bias=0.0, scale=-1.0))
            b.link("scalar")
            ev = evac("scalar", lambda E, kq=kq, bk=bk, nmcol=nmcol:
                      E.activation(p0a[:, kq * 512:(kq + 1) * 512],
                                   ps[bk][:], AF.Exp,
                                   bias=stats[:, nmcol:nmcol + 1],
                                   scale=1.0), bk)
            return ev

        def passA_sum(qc, kq, ev):
            bk = kq % 2
            b.wait("vector", f"bev{bk}", ev)
            b.emit("vector", lambda E, qc=qc, kq=kq: E.tensor_reduce(
                sc_E[:, qc, kq * 8:(kq + 1) * 8],
                p0a[:, kq * 512:(kq + 1) * 512].rearrange(
                    "p (a x) -> p a x", a=8), AX.X, ALU.add),
                tick=f"{tag}v")

        def smalls_augT(h, tail_cb=None):
            pi = h % 2
            aof = 64 * (1 - pi)
            qa = q_aug2[pi]
            if tail_cb is not None:
                tail_cb()
            vA = b.semval(f"{tag}v")
            b.wait("vector", f"{tag}v", vA)
            b.emit("vector", lambda E: E.tensor_sub(
                sc_cf[:], sc_Mq[:],
                sc_Mq[:, :, 3:4].broadcast_to([128, 16, 4])), tick=f"{tag}v")
            b.wait("scalar", f"{tag}v", b.semval(f"{tag}v"))
            b.emit("scalar", lambda E: E.activation(
                sc_cf[:], sc_cf[:], AF.Exp), tick=f"{tag}s")
            b.wait("vector", f"{tag}s", b.semval(f"{tag}s"))
            b.emit("vector", lambda E: E.tensor_mul(
                sc_E3[:], sc_E3[:],
                sc_cf[:].rearrange("p a x -> p a x 1" if False else
                                   "p a x -> p a x").unsqueeze(3)
                .broadcast_to([128, 16, 4, 8])))
            b.link("vector")
            for qc in range(16):
                b.emit("vector", lambda E, qc=qc: E.tensor_tensor_scan(
                    sc_z[:, qc, :], sc_E[:, qc, :], sc_E[:, qc, :], 0.0,
                    ALU.add, ALU.bypass))
                b.emit("vector", lambda E, qc=qc: E.tensor_tensor_scan(
                    sc_m[:, qc, :], sc_B[:, qc, :], sc_B[:, qc, :], -3.0e38,
                    ALU.max, ALU.bypass))
            b.link("vector")
            vz = b.tick_last("vector", f"{tag}v") if False else None
            b.emit("vector", lambda E: E.nop(), tick=f"{tag}v")
            b.wait("scalar", f"{tag}v", b.semval(f"{tag}v"))
            b.emit("scalar", lambda E: E.activation(
                sc_z[:], sc_z[:], AF.Ln), tick=f"{tag}s")
            b.wait("vector", f"{tag}s", b.semval(f"{tag}s"))
            b.emit("vector", lambda E: E.tensor_sub(
                sc_B[:], sc_Mq[:, :, 3:4].broadcast_to([128, 16, 32]),
                sc_m[:]))
            b.link("vector")
            b.emit("vector", lambda E: E.tensor_add(sc_z[:], sc_z[:],
                                                    sc_B[:]))
            b.link("vector")
            for qc in range(16):
                b.emit("vector", lambda E, qc=qc: E.tensor_tensor_scan(
                    sc_m[:, qc, :], sc_z[:, qc, :], sc_z[:, qc, :], 0.0,
                    ALU.add, ALU.bypass))
            b.link("vector")
            b.emit("vector", lambda E: E.tensor_sub(
                sc_B[:], sc_m[:, :, 31:32].broadcast_to([128, 16, 32]),
                sc_m[:]))
            b.link("vector")
            b.emit("vector", lambda E: E.tensor_add(sc_B[:], sc_B[:],
                                                    sc_z[:]))
            b.link("vector")
            b.emit("vector", lambda E: E.scalar_tensor_tensor(
                sc_E[:], sc_B[:], -1.0,
                sc_Mq[:, :, 3:4].broadcast_to([128, 16, 32]),
                ALU.mult, ALU.subtract))
            b.link("vector")
            b.emit("vector", lambda E: E.tensor_copy(hi16[:], sc_E[:]))
            b.link("vector")
            b.emit("vector", lambda E: E.tensor_sub(lo16[:], sc_E[:],
                                                    hi16[:]),
                   tick=f"{tag}v")
            vhl = b.semval(f"{tag}v")
            b.wait("vector", f"{tag}v", vhl)
            for qc in range(16):
                tb = 2 + qc % 2
                if augT_prev[0] is not None:
                    b.wait("vector", *augT_prev[0])
                b.emit("vector", lambda E, qc=qc, aof=aof: E.tensor_copy(
                    aughl[:, aof:aof + 32], hi16[:, qc, :]))
                b.link("vector")
                b.emit("vector", lambda E, qc=qc, aof=aof: E.tensor_copy(
                    aughl[:, aof + 32:aof + 64], lo16[:, qc, :]),
                    tick=f"{tag}v")
                b.wait("tensor", f"{tag}v", b.semval(f"{tag}v"))
                b.wait("tensor", f"bev{tb}", bank_ev[tb])
                b.emit("tensor", lambda E, tb=tb: E.transpose(
                    ps[tb][:, 0:128], aughl[:], idtf[:]))
                bank_mm[tb] = b.tick_last("tensor", f"bmm{tb}")
                augT_prev[0] = (f"bmm{tb}", bank_mm[tb])
                evac("scalar", lambda E, qc=qc, tb=tb, aof=aof, qa=qa: E.copy(
                    qa[aof:aof + 64, qc * 128:(qc + 1) * 128],
                    ps[tb][aof:aof + 64, 0:128]), tb)

        def passB_pre(h):
            b.wait("tensor", f"{tag}aug", augv[h])
            if h == 0:
                b.wait("tensor", f"{tag}ind", 32)
            for pvb in (4, 5, 6, 7):
                b.wait("tensor", f"bev{pvb}", bank_ev[pvb])

        pend_pv = [None]

        def emit_pv(h, kc, qq, ev):
            pi = h % 2
            sp = kc % 2
            bk = 2 + qq % 2
            pvb = 4 + qq
            b.wait("tensor", f"bev{bk}", ev)
            b.emit("tensor", lambda E, kc=kc, qq=qq, sp=sp,
                   h=h, pi=pi, pvb=pvb: E.matmul(
                       ps[pvb][64 * pi:64 * pi + 64, :],
                       v_tm[:, kc, h * 64:(h + 1) * 64],
                       ppt[sp][:, qq * 512:(qq + 1) * 512],
                       start=(kc == 0), stop=(kc == 15),
                       tile_position=(0, 64 * pi)))
            if qq == 3:
                b.tick_last("tensor", f"{tag}pv")

        def passB_sub(h, kc, qq):
            pi = h % 2
            ka, qa = k_aug2[pi], q_aug2[pi]
            sp = kc % 2
            bk = 2 + qq % 2
            mm_group([
                (lambda E, kc=kc, qq=qq, bk=bk, ka=ka, qa=qa: E.matmul(
                    ps[bk][:], ka[:, kc * 128:(kc + 1) * 128],
                    qa[:, qq * 512:(qq + 1) * 512],
                    start=True, stop=True))], bk)
            if pend_pv[0] is not None:
                emit_pv(*pend_pv[0])
                pend_pv[0] = None
            if qq == 0:
                if kc >= 2:
                    b.wait("scalar", f"{tag}pv", b.semval(f"{tag}pv") - 1)
                else:
                    b.wait("scalar", f"{tag}pv", b.semval(f"{tag}pv"))
            ev = evac("scalar", lambda E, qq=qq, bk=bk, sp=sp:
                      E.activation(
                          ppt[sp][:, qq * 512:(qq + 1) * 512],
                          ps[bk][:], AF.Exp), bk)
            pend_pv[0] = (h, kc, qq, ev)

        def passB_tail(h):
            pi = h % 2
            ch = h // 2
            pvv = b.semval(f"{tag}pv")
            for qq in range(4):
                pvb = 4 + qq
                b.wait("scalar", f"{tag}pv", pvv)
                v2 = b.emit("scalar", lambda E, qq=qq, ch=ch, pi=pi,
                            pvb=pvb: E.copy(
                                o_fm[64 * pi:64 * pi + 64, ch,
                                     qq * 512:(qq + 1) * 512],
                                ps[pvb][64 * pi:64 * pi + 64, :]),
                            tick=f"bev{pvb}")
                bank_ev[pvb] = v2
            b.wait("sync", f"{tag}pv", pvv)
            b.emit("sync", lambda E: E.nop(), tick=f"{tag}pB")
            pB_vals[h] = b.semval(f"{tag}pB")

        def passA_loop(h, hB=None):
            pend = None
            i = 0
            for qc in range(16):
                for kq in range(4):
                    ev = passA_mm_max(h, qc, kq)
                    if pend is not None:
                        passA_sum(*pend)
                    pend = (qc, kq, ev)
                    if hB is not None and i < 56:
                        passB_sub(hB, i // 4, i % 4)
                    i += 1
            passA_sum(*pend)

        def passB_rest(h):
            for j in range(56, 64):
                passB_sub(h, j // 4, j % 4)
            emit_pv(*pend_pv[0])
            pend_pv[0] = None
            passB_tail(h)

        for h in range(H):
            prologue(h)
            if h == 0:
                passA_loop(h)
            else:
                passB_pre(h - 1)
                passA_loop(h, hB=h - 1)
            smalls_augT(h, tail_cb=(
                (lambda hh=h - 1: passB_rest(hh)) if h >= 1 else None))
        passB_pre(H - 1)
        for kc in range(16):
            for qq in range(4):
                passB_sub(H - 1, kc, qq)
        emit_pv(*pend_pv[0])
        pend_pv[0] = None
        passB_tail(H - 1)
        heads_done = [(f"bev{i}", bank_ev[i]) for i in (4, 5, 6, 7)]
        return heads_done

    def attn_wo(Ba, Bw, sq, tag, heads_done):
        o_fm = Ba.o_fm
        x1t_b, wo_r = Bw.x1t_b, Bw.wo_r
        row0 = sq * SEQ
        wol = b.dma(wo_r, wo_b[:, :].rearrange("(c p) f -> p c f", p=128),
                    f"{tag}wol")
        b.wait("tensor", f"{tag}wol", wol)
        for nm, v in heads_done:
            b.wait("tensor", nm, v)
        for qc in range(16):
            pp = qc % 2
            if qc >= 2:
                b.wait("sync", f"{tag}x2{pp}", b.semval(f"{tag}x2{pp}"))
            lv = b.dma(x1t_b[:, pp],
                       x1d[row0 + qc * 128:row0 + (qc + 1) * 128, :],
                       f"{tag}lb{pp}")
            for half in range(2):
                bk = (qc * 2 + half) % 4
                mms = [
                    (lambda E, mc=mc, qc=qc, half=half, bk=bk: E.matmul(
                        ps[bk][:], o_fm[:, mc, qc * 128:(qc + 1) * 128],
                        wo_r[:, mc, half * 512:(half + 1) * 512],
                        start=(mc == 0), stop=(mc == 7)))
                    for mc in range(8)]
                mm_group(mms, bk)
            b.wait("vector", f"{tag}lb{pp}", lv)
            for half in range(2):
                bk = (qc * 2 + half) % 4
                evac("vector", lambda E, pp=pp, half=half, bk=bk:
                     E.tensor_add(x1t_b[:, pp, half * 512:(half + 1) * 512],
                                  ps[bk][:],
                                  x1t_b[:, pp, half * 512:(half + 1) * 512]),
                     bk)
            b.link("vector")
            b.emit("vector", lambda E: E.nop(), tick=f"{tag}ev")
            b.wait("sync", f"{tag}ev", b.semval(f"{tag}ev"))
            b.dma(x2d[row0 + qc * 128:row0 + (qc + 1) * 128, :],
                  x1t_b[:, pp], f"{tag}x2{pp}")
        return [(f"{tag}x2{0}", b.semval(f"{tag}x2{0}")),
                (f"{tag}x2{1}", b.semval(f"{tag}x2{1}"))]

    b.wait("vector", "bev1", bank_ev[1])

    f1_scope = mkscope()
    Bf1 = ffn_bufs(f1_scope)
    st1 = ffn_sublayer(Bf1, x0, x1d, "f1")
    barrier(done=st1)
    f1_scope.close()

    at_scope = mkscope()
    Ba = attn_core_bufs(at_scope)
    for sq, tag in ((0, "a0"), (1, "a1")):
        ln_scope = mkscope()
        Bl = attn_ln_bufs(ln_scope)
        vdone = attn_ln_qkv(Ba, Bl, sq, tag)
        barrier()
        ln_scope.close()
        hd_scope = mkscope()
        Bh = attn_head_bufs(hd_scope)
        hdone = attn_heads(Ba, Bh, sq, tag, vdone)
        barrier()
        hd_scope.close()
        wo_scope = mkscope()
        Bw = attn_wo_bufs(wo_scope)
        sta = attn_wo(Ba, Bw, sq, tag, hdone)
        barrier(done=sta)
        wo_scope.close()
    at_scope.close()

    f3_scope = mkscope()
    Bf3 = ffn_bufs(f3_scope)
    st3 = ffn_sublayer(Bf3, x2d, y, "f3")
    for nm, v in st3:
        b.wait("sync", nm, v)
    f3_scope.close()

    b.run_block()
    b.ctx.close()
    return nc


def _host_inputs(L0_attn_x, Wq, Wk, Wv, Wo, w_gate, w_up, w_down):
    import ml_dtypes
    consts = np.zeros((128, 8), dtype=np.float32)
    consts[:, 0] = 256.0
    consts[:, 1] = EPS
    consts[:, 2] = 0.125
    ind = np.zeros((64, SEQ), dtype=np.float32)
    for t in range(NKV):
        ind[t, t * 64:(t + 1) * 64] = 1.0
        ind[32 + t, t * 64:(t + 1) * 64] = 1.0
    ind = ind.astype(ml_dtypes.bfloat16)
    ident = np.eye(128, dtype=np.float32).astype(ml_dtypes.bfloat16)
    com = {
        "wq": np.ascontiguousarray(Wq), "wk": np.ascontiguousarray(Wk),
        "wv": np.ascontiguousarray(Wv), "wo": np.ascontiguousarray(Wo),
        "wgT": np.ascontiguousarray(w_gate.T),
        "wuT": np.ascontiguousarray(w_up.T),
        "wdT": np.ascontiguousarray(w_down.T),
        "consts": consts, "indic": ind, "ident": ident,
        "identf": np.eye(128, dtype=np.float32),
    }
    xs = np.ascontiguousarray(L0_attn_x).reshape(8, NTOK, D)
    return [dict(com, x0=np.ascontiguousarray(xs[c])) for c in range(8)]


def kernel(L0_attn_x, L0_attn_Wq, L0_attn_Wk, L0_attn_Wv, L0_attn_Wo,
           L0_ffn_w_gate, L0_ffn_w_up, L0_ffn_w_down):
    in_maps = _host_inputs(
        np.asarray(L0_attn_x, dtype=np.float32),
        np.asarray(L0_attn_Wq, dtype=np.float32),
        np.asarray(L0_attn_Wk, dtype=np.float32),
        np.asarray(L0_attn_Wv, dtype=np.float32),
        np.asarray(L0_attn_Wo, dtype=np.float32),
        np.asarray(L0_ffn_w_gate, dtype=np.float32),
        np.asarray(L0_ffn_w_up, dtype=np.float32),
        np.asarray(L0_ffn_w_down, dtype=np.float32))
    nc = build_program()
    res = run_bass_kernel_spmd(nc, in_maps, core_ids=list(range(8)))
    out = np.stack([r["y"] for r in res.results], axis=0)
    return out.reshape(16, SEQ, D)


# version 16
# speedup vs baseline: 1.0187x; 1.0015x over previous
from contextlib import ExitStack

import numpy as np
import concourse.bass as bass
import concourse.mybir as mybir
from concourse.bass_utils import run_bass_kernel_spmd

F32 = mybir.dt.float32
BF16 = mybir.dt.bfloat16
I16 = mybir.dt.int16
U8 = mybir.dt.uint8
AF = mybir.ActivationFunctionType
ALU = mybir.AluOpType
AX = mybir.AxisListType

D = 1024
INNER = 4096
H = 16
DH = 64
SEQ = 2048
NSEQ = 2
NTOK = NSEQ * SEQ
TT = 512
NTILE = NTOK // TT
TC = TT // 128
EPS = 1e-5
NKV = 32
C32M = 12582912.0
C52M = 6755399441055744.0

ENGS = ("sync", "vector", "scalar", "tensor", "gpsimd")

STAGE_FFN1 = True
STAGE_ATTN = True
STAGE_FFN2 = True


class Builder:
    def __init__(self):
        self.nc = bass.Bass("TRN2", target_bir_lowering=False)
        self.ctx = ExitStack()
        self.q = {e: [] for e in ENGS}
        self.sems = {}
        self.semctr = {}
        self.last_tick = {e: None for e in ENGS}

    def sem(self, name):
        if name not in self.sems:
            self.sems[name] = self.ctx.enter_context(self.nc.semaphore(name))
            self.semctr[name] = 0
        return self.sems[name]

    def semval(self, name):
        self.sem(name)
        return self.semctr[name]

    def emit(self, eng, fn, tick=None, tickn=1):
        if tick is None:
            self.q[eng].append(fn)
            self.last_tick[eng] = None
            return None
        h = self.sem(tick)
        self.semctr[tick] += tickn
        val = self.semctr[tick]
        self.q[eng].append(lambda E, f=fn, h=h, n=tickn: f(E).then_inc(h, n))
        self.last_tick[eng] = (tick, val)
        return val

    def tick_last(self, eng, name, n=1):
        if self.last_tick[eng] is not None:
            return self.emit(eng, lambda E: E.nop(), tick=name, tickn=n)
        h = self.sem(name)
        self.semctr[name] += n
        val = self.semctr[name]
        last = self.q[eng][-1]
        assert not isinstance(last, tuple)
        self.q[eng][-1] = (lambda E, f=last, h=h, m=n: f(E).then_inc(h, m))
        self.last_tick[eng] = (name, val)
        return val

    def wait(self, eng, name, val):
        if val <= 0:
            self.sem(name)
            return
        h = self.sem(name)
        self.q[eng].append(("w", h, val))
        self.last_tick[eng] = None

    def link(self, eng):
        if self.last_tick[eng] is not None:
            name, v = self.last_tick[eng]
            self.wait(eng, name, v)
            return
        name = f"_lk_{eng}"
        v = self.tick_last(eng, name)
        self.wait(eng, name, v)

    def dma(self, out, in_, tick, eng="sync"):
        return self.emit(eng, lambda E, o=out, i=in_: E.dma_start(out=o, in_=i),
                         tick=tick, tickn=16)

    def run_block(self):
        block = self.ctx.enter_context(self.nc.Block())

        def mk(engname):
            def body(E):
                for item in self.q[engname]:
                    if isinstance(item, tuple):
                        _, h, val = item
                        E.wait_ge(h, val)
                    else:
                        item(E)
            return body

        block.sync(mk("sync"))
        block.vector(mk("vector"))
        block.scalar(mk("scalar"))
        block.tensor(mk("tensor"))
        block.gpsimd(mk("gpsimd"))


def build_program():
    b = Builder()
    nc = b.nc
    e = b.ctx.enter_context

    x0 = nc.dram_tensor("x0", [NTOK, D], F32, kind="ExternalInput").ap()
    wq_f = nc.dram_tensor("wq", [D, D], F32, kind="ExternalInput").ap()
    wk_f = nc.dram_tensor("wk", [D, D], F32, kind="ExternalInput").ap()
    wv_f = nc.dram_tensor("wv", [D, D], F32, kind="ExternalInput").ap()
    wo_f = nc.dram_tensor("wo", [D, D], F32, kind="ExternalInput").ap()
    wgT = nc.dram_tensor("wgT", [D, INNER], F32, kind="ExternalInput").ap()
    wuT = nc.dram_tensor("wuT", [D, INNER], F32, kind="ExternalInput").ap()
    wdT = nc.dram_tensor("wdT", [INNER, D], F32, kind="ExternalInput").ap()
    consts = nc.dram_tensor("consts", [128, 8], F32, kind="ExternalInput").ap()
    indic = nc.dram_tensor("indic", [64, SEQ], BF16, kind="ExternalInput").ap()
    ident = nc.dram_tensor("ident", [128, 128], BF16, kind="ExternalInput").ap()
    identf = nc.dram_tensor("identf", [128, 128], F32, kind="ExternalInput").ap()
    y = nc.dram_tensor("y", [NTOK, D], F32, kind="ExternalOutput").ap()

    x1d = nc.dram_tensor("x1d", [NTOK, D], F32, kind="Internal").ap()
    x2d = nc.dram_tensor("x2d", [NTOK, D], F32, kind="Internal").ap()
    wg_q = nc.dram_tensor("wg_q", [D, INNER], BF16, kind="Internal").ap()
    wu_q = nc.dram_tensor("wu_q", [D, INNER], BF16, kind="Internal").ap()
    wd_q = nc.dram_tensor("wd_q", [INNER, D], BF16, kind="Internal").ap()
    wq_b = nc.dram_tensor("wq_b", [D, D], BF16, kind="Internal").ap()
    wk_b = nc.dram_tensor("wk_b", [D, D], BF16, kind="Internal").ap()
    wv_b = nc.dram_tensor("wv_b", [D, D], BF16, kind="Internal").ap()
    wo_b = nc.dram_tensor("wo_b", [D, D], BF16, kind="Internal").ap()

    cst = e(nc.sbuf_tensor("cst", [128, 8], F32))
    nc.const_aps.register(F32, 256.0, cst[:, 0:1])
    nc.const_aps.register(F32, EPS, cst[:, 1:2])
    nc.const_aps.register(F32, 0.125, cst[:, 2:3])
    ones_r = e(nc.sbuf_tensor("ones_r", [1, 128], F32))
    ones_c = e(nc.sbuf_tensor("ones_c", [128, 1], F32))
    mu = e(nc.sbuf_tensor("mu", [128, 4], F32))
    idt = e(nc.sbuf_tensor("idt", [128, 128], BF16))
    idtf = e(nc.sbuf_tensor("idtf", [128, 128], F32))
    stats = e(nc.sbuf_tensor("stats", [128, 72], F32))
    dump = e(nc.sbuf_tensor("dump", [128, D], F32))

    from types import SimpleNamespace

    def mkscope():
        return ExitStack()

    _alloc_n = [0]

    def alloc(scope, shape, dt):
        _alloc_n[0] += 1
        t = scope.enter_context(
            nc.sbuf_tensor(f"t{_alloc_n[0]}", [128] + list(shape), dt))
        return t[:]

    def barrier(done=()):
        vals = {}
        for eng in ("sync", "vector", "scalar", "tensor", "gpsimd"):
            vals[eng] = b.emit(eng, lambda E: E.nop(), tick=f"ph_{eng}")
        for eng in ("sync", "vector", "scalar", "tensor", "gpsimd"):
            for oeng, v in vals.items():
                if oeng != eng:
                    b.wait(eng, f"ph_{oeng}", v)
            for nm, v in done:
                b.wait(eng, nm, v)

    ps = [e(nc.psum_tensor(f"psbank{i}", [128, 512], F32)) for i in range(8)]
    psb = [p[:].bitcast(BF16) for p in ps]

    bank_mm = [0] * 8
    bank_ev = [0] * 8

    def mm_group(matmuls, bank):
        b.wait("tensor", f"bev{bank}", bank_ev[bank])
        for m in matmuls[:-1]:
            b.emit("tensor", m)
        v = b.emit("tensor", matmuls[-1], tick=f"bmm{bank}")
        bank_mm[bank] = v
        return v

    def evac(eng, fn, bank):
        b.wait(eng, f"bmm{bank}", bank_mm[bank])
        v = b.emit(eng, fn, tick=f"bev{bank}")
        bank_ev[bank] = v
        return v

    b.dma(cst[:], consts[:], "cld")
    b.dma(idt[:], ident[:], "cld")
    b.dma(idtf[:], identf[:], "cld")
    for eng in ("vector", "scalar", "tensor", "gpsimd"):
        b.wait(eng, "cld", 48)
    b.emit("vector", lambda E: E.memset(ones_r[:], 1.0))
    vo = b.emit("vector", lambda E: E.memset(ones_c[:], 1.0), tick="vones")
    b.wait("tensor", "vones", vo)
    b.emit("vector", lambda E: E.memset(cst[:, 3:4], 0.75))
    b.link("vector")
    b.emit("vector", lambda E: E.tensor_scalar(
        cst[:, 4:5], cst[:, 3:4], C52M, None, ALU.add))
    b.link("vector")
    b.emit("vector", lambda E: E.tensor_scalar(
        cst[:, 4:5], cst[:, 4:5], -C52M, None, ALU.add))
    b.link("vector")
    b.emit("vector", lambda E: E.tensor_scalar(
        cst[:, 5:6], cst[:, 4:5], -1.0, 1.0, ALU.mult, ALU.add))
    b.link("vector")
    b.emit("vector", lambda E: E.tensor_scalar(
        cst[:, 6:7], cst[:, 4:5], -500.0, None, ALU.mult), tick="wcst")

    wq_scope = mkscope()
    awbuf = alloc(wq_scope, [4, D], BF16)
    for (src, dst) in ((wq_f, wq_b), (wk_f, wk_b), (wv_f, wv_b), (wo_f, wo_b)):
        for ch in range(2):
            seg = src[ch * 512:(ch + 1) * 512, :].rearrange("(c p) f -> p c f", p=128)
            lv = b.dma(awbuf, seg, "awld", eng="gpsimd")
            b.wait("sync", "awld", lv)
            dseg = dst[ch * 512:(ch + 1) * 512, :].rearrange("(c p) f -> p c f", p=128)
            sv = b.dma(dseg, awbuf, "awst")
            b.wait("gpsimd", "awst", sv)

    wqin8 = alloc(wq_scope, [8, INNER], F32)
    wqin32 = wqin8.rearrange("p a b -> p (a b)").rearrange(
        "p (a b) -> p a b", a=32)
    wqv = alloc(wq_scope, [INNER], F32)
    wqq0 = alloc(wq_scope, [INNER], I16)
    wqr = alloc(wq_scope, [INNER], BF16)
    wqhi = alloc(wq_scope, [INNER], F32)
    wqleaf = alloc(wq_scope, [1024], F32)

    def wquant(w_src, w_dst, nrow, ncol, mucol):
        nch = nrow // 128
        wqin = wqin8 if nch == 8 else wqin32
        b.wait("vector", "bmm0", bank_mm[0])
        b.wait("vector", "bmm1", bank_mm[1])
        b.wait("sync", "awst", b.semval("awst"))
        b.wait("sync", "wq_v", b.semval("wq_v"))
        b.wait("sync", "wq_s", b.semval("wq_s"))
        lv = b.dma(wqin[:, 0:nch, 0:ncol],
                   w_src[:].rearrange("(c p) f -> p c f", p=128), "wqld")
        b.wait("vector", "wqld", lv)
        b.wait("scalar", "wqld", lv)
        MC = float((1 << 23) + (1 << 22))
        nl = ncol // 128
        b.emit("vector", lambda E: E.memset(stats[:, 43:44], 0.0))
        b.link("vector")
        for c in range(nch):
            b.wait("scalar", "wq_v", b.semval("wq_v"))
            b.emit("scalar", lambda E, c=c, n=ncol: E.activation(
                wqv[:, 0:n], wqin[:, c, 0:n], AF.Abs), tick="wq_s")
            b.wait("vector", "wq_s", b.semval("wq_s"))
            b.emit("vector", lambda E, n=ncol: E.tensor_scalar(
                wqhi[:, 0:n], wqv[:, 0:n], float(2 ** 20), MC, ALU.mult,
                ALU.add))
            b.link("vector")
            b.emit("vector", lambda E, n=ncol: E.tensor_scalar(
                wqhi[:, 0:n], wqhi[:, 0:n], -MC, None, ALU.add))
            b.link("vector")
            b.emit("vector", lambda E, n=ncol: E.scalar_tensor_tensor(
                wqv[:, 0:n], wqv[:, 0:n], float(2 ** 20), wqhi[:, 0:n],
                ALU.mult, ALU.subtract))
            b.link("vector")
            b.emit("vector", lambda E, n=ncol: E.tensor_reduce(
                stats[:, 42:43], wqv[:, 0:n], AX.X, ALU.add))
            b.link("vector")
            b.emit("vector", lambda E: E.tensor_add(
                stats[:, 43:44], stats[:, 43:44], stats[:, 42:43]))
            b.link("vector")
            b.emit("vector", lambda E, c=c, n=ncol, nl=nl: E.tensor_reduce(
                wqleaf[:, c * nl:(c + 1) * nl],
                wqhi[:, 0:n].rearrange("p (a x) -> p a x", x=128),
                AX.X, ALU.add), tick="wq_v")
        nlf = nch * nl
        b.wait("vector", "wq_v", b.semval("wq_v"))
        b.emit("vector", lambda E, m=nlf: E.tensor_scalar(
            wqhi[:, 0:m], wqleaf[:, 0:m], float(2 ** -10), MC, ALU.mult,
            ALU.add))
        b.link("vector")
        b.emit("vector", lambda E, m=nlf: E.tensor_scalar(
            wqhi[:, 0:m], wqhi[:, 0:m], -MC, None, ALU.add))
        b.link("vector")
        b.emit("vector", lambda E, m=nlf: E.scalar_tensor_tensor(
            wqleaf[:, 0:m], wqhi[:, 0:m], float(-(2 ** 10)), wqleaf[:, 0:m],
            ALU.mult, ALU.add))
        b.link("vector")
        b.emit("vector", lambda E, m=nlf: E.tensor_reduce(
            stats[:, 36:37], wqhi[:, 0:m], AX.X, ALU.add))
        b.emit("vector", lambda E, m=nlf: E.tensor_reduce(
            stats[:, 37:38], wqleaf[:, 0:m], AX.X, ALU.add))
        b.link("vector")
        b.emit("vector", lambda E: E.tensor_scalar(
            stats[:, 38:39], stats[:, 36:37], float(2 ** -9), MC, ALU.mult,
            ALU.add))
        b.emit("vector", lambda E: E.tensor_scalar(
            stats[:, 40:41], stats[:, 37:38], float(2 ** -5), MC, ALU.mult,
            ALU.add))
        b.link("vector")
        b.emit("vector", lambda E: E.tensor_scalar(
            stats[:, 38:39], stats[:, 38:39], -MC, None, ALU.add))
        b.emit("vector", lambda E: E.tensor_scalar(
            stats[:, 40:41], stats[:, 40:41], -MC, None, ALU.add))
        b.link("vector")
        b.emit("vector", lambda E: E.scalar_tensor_tensor(
            stats[:, 39:40], stats[:, 38:39], float(-(2 ** 9)),
            stats[:, 36:37], ALU.mult, ALU.add))
        b.emit("vector", lambda E: E.scalar_tensor_tensor(
            stats[:, 41:42], stats[:, 40:41], float(-(2 ** 5)),
            stats[:, 37:38], ALU.mult, ALU.add), tick="wq_v")
        b.wait("tensor", "wq_v", b.semval("wq_v"))
        mm_group([lambda E: E.matmul(ps[0][0:1, 0:6], ones_c[:, 0:1],
                                     stats[:, 38:44], start=True,
                                     stop=True)], 0)
        evac("vector", lambda E: E.tensor_copy(stats[0:1, 38:44],
                                               ps[0][0:1, 0:6]), 0)
        b.link("vector")
        b.emit("vector", lambda E: E.scalar_tensor_tensor(
            stats[0:1, 38:39], stats[0:1, 38:39], float(2 ** 9),
            stats[0:1, 39:40], ALU.mult, ALU.add))
        b.emit("vector", lambda E: E.scalar_tensor_tensor(
            stats[0:1, 40:41], stats[0:1, 40:41], float(2 ** 5),
            stats[0:1, 41:42], ALU.mult, ALU.add))
        b.link("vector")
        b.emit("vector", lambda E: E.scalar_tensor_tensor(
            stats[0:1, 38:39], stats[0:1, 38:39], float(2 ** 10),
            stats[0:1, 40:41], ALU.mult, ALU.add))
        b.link("vector")
        b.emit("vector", lambda E: E.tensor_add(
            stats[0:1, 38:39], stats[0:1, 38:39], stats[0:1, 43:44]))
        b.link("vector")
        b.emit("vector", lambda E, n=nrow * ncol: E.tensor_scalar(
            stats[0:1, 33:34], stats[0:1, 38:39], float(2.0 ** -20 / n), EPS,
            ALU.mult, ALU.max), tick="wqv")
        b.wait("tensor", "wqv", b.semval("wqv"))
        b.wait("tensor", "bev0", bank_ev[0])
        mm_group([lambda E: E.matmul(ps[1][:, 0:1], ones_r[0:1, :],
                                     stats[0:1, 33:34], start=True,
                                     stop=True)], 1)
        evac("vector", lambda E, mc=mucol: E.tensor_copy(
            mu[:, mc:mc + 1], ps[1][:, 0:1]), 1)
        b.link("vector")
        b.emit("vector", lambda E, mc=mucol: E.reciprocal(
            stats[:, 34:35], mu[:, mc:mc + 1]))
        b.link("vector")
        b.emit("vector", lambda E, mc=mucol: E.scalar_tensor_tensor(
            stats[:, 35:36], stats[:, 34:35], -1.0, mu[:, mc:mc + 1],
            ALU.mult, ALU.mult))
        b.link("vector")
        b.emit("vector", lambda E: E.tensor_scalar(
            stats[:, 35:36], stats[:, 35:36], 2.0, None, ALU.add))
        b.link("vector")
        b.emit("vector", lambda E: E.tensor_mul(
            stats[:, 34:35], stats[:, 34:35], stats[:, 35:36]))
        b.link("vector")
        for c in range(nch):
            b.emit("vector", lambda E, c=c, n=ncol: E.tensor_scalar(
                wqv[:, 0:n], wqin[:, c, 0:n], stats[:, 34:35], None,
                ALU.mult), tick="wq_v")
            b.wait("scalar", "wq_v", b.semval("wq_v"))
            b.emit("scalar", lambda E, n=ncol: E.activation(
                wqq0[:, 0:n], wqv[:, 0:n], AF.Copy, bias=256.75, scale=1.0),
                tick="wq_s")
            b.wait("vector", "wq_s", b.semval("wq_s"))
            b.emit("vector", lambda E, n=ncol: E.scalar_tensor_tensor(
                wqv[:, 0:n], wqq0[:, 0:n], -256.0, wqv[:, 0:n], ALU.add,
                ALU.subtract))
            b.link("vector")
            b.emit("vector", lambda E, n=ncol: E.scalar_tensor_tensor(
                wqv[:, 0:n], wqv[:, 0:n], 0.5, wqq0[:, 0:n], ALU.is_le,
                ALU.add))
            b.link("vector")
            b.emit("vector", lambda E, n=ncol: E.tensor_scalar(
                wqr[:, 0:n], wqv[:, 0:n], -257.0, 1.0, ALU.add, ALU.min))
            b.link("vector")
            b.emit("vector", lambda E, n=ncol: E.tensor_scalar(
                wqr[:, 0:n], wqr[:, 0:n], -1.0, None, ALU.max), tick="wq_v")
            b.wait("sync", "wq_v", b.semval("wq_v"))
            sv = b.dma(w_dst[c * 128:(c + 1) * 128, 0:ncol],
                       wqr[:, 0:ncol], "wq_st")
            b.wait("vector", "wq_st", sv)
            b.wait("scalar", "wq_st", sv)

    wquant(wgT, wg_q, D, INNER, 0)
    wquant(wuT, wu_q, D, INNER, 1)
    wquant(wdT, wd_q, INNER, D, 2)
    barrier(done=[("wq_st", b.semval("wq_st")), ("awst", b.semval("awst"))])
    wq_scope.close()

    def ffn_bufs(scope):
        Bf = SimpleNamespace()
        Bf.xt = [alloc(scope, [TC, D], F32), alloc(scope, [TC, D], F32)]
        Bf.vql = alloc(scope, [TC, D], F32)
        Bf.q0x = alloc(scope, [D], I16)
        Bf.xq_fm = alloc(scope, [8, TT], BF16)
        Bf.gu = alloc(scope, [TC, INNER], F32)
        Bf.hv = alloc(scope, [INNER], F32)
        Bf.hq_fm = alloc(scope, [32, TT], BF16)
        Bf.wbuf = alloc(scope, [2, 8, 512], BF16)
        Bf.wdbuf = alloc(scope, [2, D], BF16)
        return Bf

    def ffn_sublayer(Bf, x_src, x_dst, tag):
        xt, vql, q0x, xq_fm = Bf.xt, Bf.vql, Bf.q0x, Bf.xq_fm
        gu, hv, hq_fm, wbuf, wdbuf = (Bf.gu, Bf.hv, Bf.hq_fm, Bf.wbuf,
                                      Bf.wdbuf)
        wfree_state = [None, None]
        lad_guard = [None]
        xqT_done = [None]

        def stage1(it):
            p = it % 2
            so = 16 * p
            qo = 32 + 16 * p
            X = xt[p]
            if it >= 2:
                b.wait("sync", f"{tag}stx{p}", (it // 2) * 16)
            if lad_guard[0] is not None:
                b.wait("sync", f"{tag}v", lad_guard[0])
            src = x_src[it * TT:(it + 1) * TT, :].rearrange(
                "(c p) f -> p c f", p=128)
            lv = b.dma(X, src, f"{tag}ldx{p}")
            b.wait("vector", f"{tag}ldx{p}", lv)
            b.wait("scalar", f"{tag}ldx{p}", lv)
            b.wait("scalar", f"{tag}v", b.semval(f"{tag}v"))
            for tc in range(TC):
                b.emit("vector", lambda E, tc=tc, X=X, so=so: E.tensor_reduce(
                    stats[:, so + tc:so + tc + 1], X[:, tc], AX.X, ALU.add))
                b.emit("vector", lambda E, tc=tc, X=X, so=so: E.tensor_reduce(
                    stats[:, so + 8 + tc:so + 9 + tc], X[:, tc], AX.X, ALU.max))
                b.emit("vector", lambda E, tc=tc, X=X, so=so: E.tensor_reduce(
                    stats[:, so + 12 + tc:so + 13 + tc], X[:, tc], AX.X,
                    ALU.min))
            b.link("vector")
            for tc in range(TC):
                if tc > 0:
                    b.link("scalar")
                b.emit("scalar", lambda E, tc=tc, X=X, so=so: E.activation(
                    dump[:, :], X[:, tc], AF.Square,
                    accum_out=stats[:, so + 4 + tc:so + 5 + tc]),
                    tick=f"{tag}s")
            ssq = b.semval(f"{tag}s")
            mcol = stats[:, so:so + 4]
            vcol = stats[:, so + 4:so + 8]
            mxc = stats[:, so + 8:so + 12]
            mnc = stats[:, so + 12:so + 16]
            shc = stats[:, qo:qo + 4]
            dqg = stats[:, qo + 4:qo + 8]
            dqu = stats[:, qo + 8:qo + 12]
            dqd = stats[:, 64 + 4 * p:68 + 4 * p]
            b.wait("vector", f"{tag}s", ssq)
            b.emit("vector", lambda E, m=mcol: E.tensor_scalar(
                m, m, 1.0 / D, None, ALU.mult))
            b.link("vector")
            b.emit("vector", lambda E, v=vcol: E.tensor_scalar(
                v, v, 1.0 / D, None, ALU.mult))
            b.link("vector")
            b.emit("vector", lambda E, m=mcol: E.scalar_tensor_tensor(
                dump[:, 0:4], m, 1.0, m, ALU.bypass, ALU.mult))
            b.link("vector")
            b.emit("vector", lambda E, v=vcol: E.tensor_sub(
                v, v, dump[:, 0:4]), tick=f"{tag}v")
            b.wait("scalar", f"{tag}v", b.semval(f"{tag}v"))
            b.emit("scalar", lambda E, v=vcol: E.activation(
                dump[:, 4:8], v, AF.Sqrt, bias=EPS, scale=1.0),
                tick=f"{tag}s")
            b.wait("vector", f"{tag}s", b.semval(f"{tag}s"))
            b.emit("vector", lambda E: E.reciprocal(dump[:, 8:12],
                                                    dump[:, 4:8]))
            b.link("vector")
            b.emit("vector", lambda E, v=vcol: E.tensor_scalar(
                dump[:, 12:16], v, 1.0, EPS, ALU.mult, ALU.add))
            b.link("vector")
            b.emit("vector", lambda E: E.scalar_tensor_tensor(
                dump[:, 16:20], dump[:, 8:12], 1.0, dump[:, 8:12],
                ALU.bypass, ALU.mult))
            b.link("vector")
            b.emit("vector", lambda E: E.tensor_mul(
                dump[:, 16:20], dump[:, 16:20], dump[:, 12:16]))
            b.link("vector")
            b.emit("vector", lambda E: E.tensor_scalar(
                dump[:, 16:20], dump[:, 16:20], -0.5, 1.5, ALU.mult,
                ALU.add))
            b.link("vector")
            b.emit("vector", lambda E, v=vcol: E.tensor_mul(
                v, dump[:, 8:12], dump[:, 16:20]))
            b.link("vector")
            b.emit("vector", lambda E, mx=mxc, m=mcol: E.tensor_sub(mx, mx, m))
            b.link("vector")
            b.emit("vector", lambda E, mn=mnc, m=mcol: E.tensor_sub(mn, m, mn))
            b.link("vector")
            b.emit("vector", lambda E, mx=mxc, mn=mnc: E.tensor_max(mx, mx, mn))
            b.link("vector")
            b.emit("vector", lambda E, mx=mxc, v=vcol: E.tensor_mul(mx, mx, v))
            b.link("vector")
            b.emit("vector", lambda E, mx=mxc: E.tensor_scalar(
                mx, mx, EPS, None, ALU.max))
            b.link("vector")
            b.emit("vector", lambda E, d=dqg, mx=mxc: E.scalar_tensor_tensor(
                d, mx, 1.0 / 127.0, mu[:, 0:1].broadcast_to([128, 4]),
                ALU.mult, ALU.mult))
            b.emit("vector", lambda E, d=dqu, mx=mxc: E.scalar_tensor_tensor(
                d, mx, 1.0 / 127.0, mu[:, 1:2].broadcast_to([128, 4]),
                ALU.mult, ALU.mult))
            b.link("vector")
            b.emit("vector", lambda E, mx=mxc: E.reciprocal(mx, mx))
            b.link("vector")
            b.emit("vector", lambda E, mx=mxc: E.tensor_scalar(
                mx, mx, 127.0, None, ALU.mult))
            b.link("vector")
            b.emit("vector", lambda E, mx=mxc, v=vcol: E.tensor_mul(
                mx, mx, v))
            b.link("vector")
            b.emit("vector", lambda E, mn=mnc, m=mcol, mx=mxc:
                   E.scalar_tensor_tensor(mn, m, -1.0, mx, ALU.mult,
                                          ALU.mult), tick=f"{tag}v")
            vAB = b.semval(f"{tag}v")
            if xqT_done[0] is not None:
                b.wait("vector", *xqT_done[0])
            for tc in range(TC):
                if tc == 0:
                    b.wait("vector", f"{tag}v", vAB)
                b.emit("vector", lambda E, tc=tc, X=X, mx=mxc, mn=mnc:
                       E.tensor_scalar(vql[:, tc], X[:, tc],
                                       mx[:, tc:tc + 1],
                                       mn[:, tc:tc + 1], ALU.mult, ALU.add),
                       tick=f"{tag}v")
                b.wait("scalar", f"{tag}v", b.semval(f"{tag}v"))
                b.emit("scalar", lambda E, tc=tc: E.activation(
                    q0x[:], vql[:, tc], AF.Copy, bias=256.75, scale=1.0),
                    tick=f"{tag}s")
                b.wait("vector", f"{tag}s", b.semval(f"{tag}s"))
                b.emit("vector", lambda E, tc=tc: E.scalar_tensor_tensor(
                    vql[:, tc], q0x[:], -256.0, vql[:, tc], ALU.add,
                    ALU.subtract))
                b.link("vector")
                b.emit("vector", lambda E, tc=tc: E.scalar_tensor_tensor(
                    vql[:, tc], vql[:, tc], 0.5, q0x[:], ALU.is_le,
                    ALU.add))
                b.link("vector")
                b.emit("vector", lambda E, tc=tc: E.tensor_scalar(
                    vql[:, tc], vql[:, tc], -257.0, None, ALU.add),
                    tick=f"{tag}v")
            return b.semval(f"{tag}v")

        def evac_cb(it, tdown):
            p = it % 2
            X = xt[p]
            dqd = stats[:, 64 + 4 * p:68 + 4 * p]
            b.wait("vector", f"{tag}dmm", tdown)
            for dc in range(2):
                for tc in range(TC):
                    bk = dc * 4 + tc
                    bank_ev[bk] = b.emit(
                        "vector", lambda E, dc=dc, tc=tc, bk=bk, X=X, d=dqd:
                        E.scalar_tensor_tensor(
                            X[:, tc, dc * 512:(dc + 1) * 512], ps[bk][:],
                            d[:, tc:tc + 1],
                            X[:, tc, dc * 512:(dc + 1) * 512],
                            ALU.mult, ALU.add), tick=f"bev{bk}")
            b.wait("sync", f"bev{7}", bank_ev[7])
            dstv = x_dst[it * TT:(it + 1) * TT, :].rearrange(
                "(c p) f -> p c f", p=128)
            b.dma(dstv, X, f"{tag}stx{p}")

        def stage2(it, chain_v):
            p = it % 2
            so = 16 * p
            qo = 32 + 16 * p
            X = xt[p]
            shc = stats[:, qo:qo + 4]
            dqg = stats[:, qo + 4:qo + 8]
            dqu = stats[:, qo + 8:qo + 12]
            dqd = stats[:, 64 + 4 * p:68 + 4 * p]
            xq_ev = []
            b.wait("tensor", f"{tag}v", chain_v)
            for tc in range(TC):
                for g2 in range(2):
                    tb = 4 + g2
                    b.wait("tensor", f"bev{tb}", bank_ev[tb])
                    for j in range(4):
                        ic = g2 * 4 + j
                        b.emit("tensor", lambda E, tc=tc, ic=ic, j=j, tb=tb:
                               E.transpose(
                                   ps[tb][:, j * 128:(j + 1) * 128],
                                   vql[:, tc, ic * 128:(ic + 1) * 128],
                                   idtf[:]))
                    bank_mm[tb] = b.tick_last("tensor", f"bmm{tb}")
                    ev = evac("scalar", lambda E, tc=tc, g2=g2, tb=tb: E.copy(
                        xq_fm[:, g2 * 4:(g2 + 1) * 4,
                              tc * 128:(tc + 1) * 128],
                        ps[tb][:, 0:512].rearrange("p (a t) -> p a t", a=4)),
                        tb)
                xq_ev.append((f"bmm{4 + 1}", bank_mm[5]))
                xq_ev.append((f"bev{tb}", ev))
            xqT_done[0] = (f"bmm{5}", bank_mm[5])
            wfree = wfree_state
            silu_last = None
            up_last = None
            for phase in range(2):
                wsrc = wg_q if phase == 0 else wu_q
                if phase == 1:
                    b.wait("vector", *silu_last)
                for oc in range(8):
                    wp = (phase * 8 + oc) % 2
                    if wfree[wp] is not None:
                        b.wait("sync", *wfree[wp])
                    wl = b.dma(wbuf[:, wp],
                               wsrc[:, oc * 512:(oc + 1) * 512].rearrange(
                                   "(c p) f -> p c f", p=128), f"{tag}wld{wp}")
                    b.wait("tensor", f"{tag}wld{wp}", wl)
                    if phase == 0 and oc == 0:
                        b.wait("tensor", *xq_ev[-1])
                        b.wait("tensor", *xq_ev[-2])
                    for tc in range(TC):
                        bk = ((phase * 8 + oc) % 2) * 2 + (tc % 2)
                        mms = [
                            (lambda E, ic=ic, tc=tc, wp=wp, bk=bk: E.matmul(
                                ps[bk][:],
                                xq_fm[:, ic, tc * 128:(tc + 1) * 128],
                                wbuf[:, wp, ic], start=(ic == 0),
                                stop=(ic == 7)))
                            for ic in range(8)]
                        mm_group(mms, bk)
                        if phase == 0:
                            evac("scalar",
                                 lambda E, tc=tc, oc=oc, bk=bk, d=dqg:
                                 E.activation(
                                     gu[:, tc, oc * 512:(oc + 1) * 512],
                                     ps[bk][:], AF.Silu,
                                     scale=d[:, tc:tc + 1]), bk)
                        else:
                            evac("vector",
                                 lambda E, tc=tc, oc=oc, bk=bk, d=dqu:
                                 E.scalar_tensor_tensor(
                                     gu[:, tc, oc * 512:(oc + 1) * 512],
                                     ps[bk][:], d[:, tc:tc + 1],
                                     gu[:, tc, oc * 512:(oc + 1) * 512],
                                     ALU.mult, ALU.mult), bk)
                    bklast = ((phase * 8 + oc) % 2) * 2 + 1
                    wfree[wp] = (f"bmm{bklast}", bank_mm[bklast])
                    if phase == 0:
                        silu_last = (f"bev{bklast}", bank_ev[bklast])
                    else:
                        up_last = (f"bev{bklast}", bank_ev[bklast])
            b.wait("vector", *silu_last)
            b.wait("vector", *up_last)
            for tc in range(TC):
                b.emit("vector", lambda E, tc=tc, s=shc: E.tensor_reduce(
                    s[:, tc:tc + 1], gu[:, tc], AX.X, ALU.max,
                    apply_absolute_value=True))
            b.link("vector")
            b.emit("vector", lambda E, s=shc: E.tensor_scalar(
                s, s, EPS, None, ALU.max))
            b.link("vector")
            b.emit("vector", lambda E, d=dqd, s=shc: E.scalar_tensor_tensor(
                d, s, 0.5 / 127.0, mu[:, 2:3].broadcast_to([128, 4]),
                ALU.mult, ALU.mult))
            b.link("vector")
            b.emit("vector", lambda E, s=shc: E.reciprocal(s, s))
            b.link("vector")
            b.emit("vector", lambda E, s=shc: E.tensor_scalar(
                s, s, 127.0, None, ALU.mult), tick=f"{tag}v")
            vsh = b.semval(f"{tag}v")
            scr = xt[1 - p].rearrange("p a d -> p (a d)")
            if it >= 1:
                b.wait("scalar", f"{tag}stx{1 - p}",
                       ((it - 1) // 2 + 1) * 16)
                b.wait("scalar", "bev7", b.semval("bev7"))
            hq_ev = []
            prevG = None
            for tc in range(TC):
                b.wait("scalar", f"{tag}v", vsh)
                if prevG is not None:
                    b.wait("scalar", f"{tag}v", prevG)
                b.emit("scalar", lambda E, tc=tc, s=shc, scr=scr:
                       E.activation(scr, gu[:, tc], AF.Copy, bias=500.0,
                                    scale=s[:, tc:tc + 1]), tick=f"{tag}s")
                bB = b.semval(f"{tag}s")
                b.wait("vector", f"{tag}v", vsh)
                b.emit("vector", lambda E, tc=tc, s=shc: E.tensor_scalar(
                    hv[:], gu[:, tc], s[:, tc:tc + 1], C32M, ALU.mult,
                    ALU.add))
                b.link("vector")
                b.wait("vector", f"{tag}s", bB)
                b.emit("vector", lambda E, tc=tc: E.tensor_scalar(
                    gu[:, tc], hv[:], -C32M, None, ALU.add), tick=f"{tag}v")
                vC = b.semval(f"{tag}v")
                b.wait("scalar", f"{tag}v", vC)
                b.emit("scalar", lambda E, scr=scr: E.activation(
                    hv[:], scr, AF.Copy, bias=C52M, scale=1.0))
                b.link("scalar")
                b.emit("scalar", lambda E, scr=scr: E.activation(
                    scr, hv[:], AF.Copy, bias=-C52M, scale=1.0),
                    tick=f"{tag}s")
                eE = b.semval(f"{tag}s")
                b.link("vector")
                b.emit("vector", lambda E, tc=tc: E.tensor_scalar(
                    gu[:, tc], gu[:, tc], cst[:, 5:6], cst[:, 6:7],
                    ALU.mult, ALU.add))
                b.link("vector")
                b.wait("vector", f"{tag}s", eE)
                b.emit("vector", lambda E, tc=tc, scr=scr:
                       E.scalar_tensor_tensor(
                           gu[:, tc], scr, cst[:, 4:5], gu[:, tc],
                           ALU.mult, ALU.add), tick=f"{tag}v")
                prevG = b.semval(f"{tag}v")
                b.wait("tensor", f"{tag}v", b.semval(f"{tag}v"))
                for g in range(8):
                    tb = 4 + (g % 2)
                    b.wait("tensor", f"bev{tb}", bank_ev[tb])
                    for j in range(4):
                        oc = g * 4 + j
                        b.emit("tensor", lambda E, tc=tc, oc=oc, j=j, tb=tb:
                               E.transpose(
                                   ps[tb][:, j * 128:(j + 1) * 128],
                                   gu[:, tc, oc * 128:(oc + 1) * 128],
                                   idtf[:]))
                    bank_mm[tb] = b.tick_last("tensor", f"bmm{tb}")
                    ev = evac("vector",
                              lambda E, tc=tc, g=g, tb=tb: E.tensor_copy(
                                  hq_fm[:, g * 4:(g + 1) * 4,
                                        tc * 128:(tc + 1) * 128],
                                  ps[tb][:, 0:512].rearrange(
                                      "p (a t) -> p a t", a=4)), tb)
                    hq_ev.append((f"bev{tb}", ev))
                hq_ev.append((f"bmm{4}", bank_mm[4]))
                hq_ev.append((f"bmm{5}", bank_mm[5]))
            lad_guard[0] = b.semval(f"{tag}v")
            b.wait("tensor", *hq_ev[-1])
            b.wait("tensor", *hq_ev[-2])
            for bk in range(8):
                b.wait("tensor", f"bev{bk}", bank_ev[bk])
            for oc in range(32):
                wp = oc % 2
                if oc >= 2:
                    b.wait("sync", f"{tag}dmm", b.semval(f"{tag}dmm") - 1)
                wl = b.dma(wdbuf[:, wp], wd_q[oc * 128:(oc + 1) * 128, :],
                           f"{tag}dld{wp}")
                b.wait("tensor", f"{tag}dld{wp}", wl)
                for dc in range(2):
                    for tc in range(TC):
                        bk = dc * 4 + tc
                        b.emit("tensor",
                               lambda E, oc=oc, dc=dc, tc=tc, wp=wp, bk=bk:
                               E.matmul(ps[bk][:],
                                        hq_fm[:, oc, tc * 128:(tc + 1) * 128],
                                        wdbuf[:, wp, dc * 512:(dc + 1) * 512],
                                        start=(oc == 0), stop=(oc == 31)))
                b.tick_last("tensor", f"{tag}dmm")
            return b.semval(f"{tag}dmm")

        pending = None
        for it in range(NTILE):
            chain_v = stage1(it)
            if pending is not None:
                evac_cb(*pending)
            tdown = stage2(it, chain_v)
            pending = (it, tdown)
        evac_cb(*pending)
        return [(f"{tag}stx0", b.semval(f"{tag}stx0")),
                (f"{tag}stx1", b.semval(f"{tag}stx1"))]


    def attn_core_bufs(scope):
        Ba = SimpleNamespace()
        Ba.o_fm = alloc(scope, [8, SEQ], BF16)
        Ba.v_tm = alloc(scope, [16, D], BF16)
        Ba.k_fm = alloc(scope, [8, SEQ], BF16)
        Ba.q_fm = alloc(scope, [8, SEQ], BF16)
        return Ba

    def attn_ln_bufs(scope):
        Bl = SimpleNamespace()
        Bl.xn_fm = alloc(scope, [8, SEQ], BF16)
        Bl.x1t_a = alloc(scope, [2, D], F32)
        Bl.xn_tm = alloc(scope, [2, D], F32)
        Bl.qkvw = alloc(scope, [2, 8, 128], BF16)
        Bl.vw_r = alloc(scope, [8, D], BF16)
        return Bl

    def attn_head_bufs(scope):
        Bh = SimpleNamespace()
        Bh.p0a = alloc(scope, [SEQ], F32)
        Bh.ppt = [alloc(scope, [SEQ], BF16), alloc(scope, [SEQ], BF16)]
        Bh.sc_B = alloc(scope, [16, NKV], F32)
        Bh.sc_E = alloc(scope, [16, NKV], F32)
        Bh.sc_z = alloc(scope, [16, NKV], F32)
        Bh.sc_m = alloc(scope, [16, NKV], F32)
        Bh.hi16 = alloc(scope, [16, NKV], BF16)
        Bh.lo16 = alloc(scope, [16, NKV], BF16)
        Bh.sc_Mq = alloc(scope, [16, 4], F32)
        Bh.sc_cf = alloc(scope, [16, 4], F32)
        Bh.aughl = alloc(scope, [128], F32)
        Bh.q_aug2 = [alloc(scope, [SEQ], BF16), alloc(scope, [SEQ], BF16)]
        Bh.k_aug2 = [alloc(scope, [SEQ], BF16), alloc(scope, [SEQ], BF16)]
        return Bh

    def attn_wo_bufs(scope):
        Bw = SimpleNamespace()
        Bw.x1t_b = alloc(scope, [2, D], F32)
        Bw.wo_r = alloc(scope, [8, D], BF16)
        return Bw

    def attn_ln_qkv(Ba, Bl, sq, tag):
        o_fm, v_tm, k_fm, q_fm = Ba.o_fm, Ba.v_tm, Ba.k_fm, Ba.q_fm
        xn_fm, x1t_a, xn_tm = Bl.xn_fm, Bl.x1t_a, Bl.xn_tm
        qkvw, vw_r = Bl.qkvw, Bl.vw_r
        row0 = sq * SEQ
        for tc in range(16):
            pp = tc % 2
            so = 48 + 4 * pp
            if tc >= 2:
                b.wait("sync", f"{tag}xnf{pp}", b.semval(f"{tag}xnf{pp}"))
            lv = b.dma(x1t_a[:, pp],
                       x1d[row0 + tc * 128:row0 + (tc + 1) * 128, :],
                       f"{tag}lx{pp}")
            b.wait("vector", f"{tag}lx{pp}", lv)
            b.wait("scalar", f"{tag}lx{pp}", lv)
            XA = x1t_a[:, pp]
            b.wait("scalar", f"{tag}v", b.semval(f"{tag}v"))
            b.emit("vector", lambda E, XA=XA, so=so: E.tensor_reduce(
                stats[:, so:so + 1], XA, AX.X, ALU.add), tick=f"{tag}v")
            b.emit("scalar", lambda E, XA=XA, so=so: E.activation(
                dump[:, :], XA, AF.Square,
                accum_out=stats[:, so + 1:so + 2]), tick=f"{tag}s")
            b.wait("vector", f"{tag}s", b.semval(f"{tag}s"))
            b.wait("vector", f"{tag}v", b.semval(f"{tag}v"))
            b.emit("vector", lambda E, so=so: E.tensor_scalar(
                stats[:, so:so + 2], stats[:, so:so + 2], 1.0 / D, None,
                ALU.mult))
            b.link("vector")
            b.emit("vector", lambda E, so=so: E.scalar_tensor_tensor(
                stats[:, so + 2:so + 3], stats[:, so:so + 1], 1.0,
                stats[:, so:so + 1], ALU.bypass, ALU.mult))
            b.link("vector")
            b.emit("vector", lambda E, so=so: E.tensor_sub(
                stats[:, so + 1:so + 2], stats[:, so + 1:so + 2],
                stats[:, so + 2:so + 3]), tick=f"{tag}v")
            b.wait("scalar", f"{tag}v", b.semval(f"{tag}v"))
            b.emit("scalar", lambda E, so=so: E.activation(
                stats[:, so + 2:so + 3], stats[:, so + 1:so + 2], AF.Sqrt,
                bias=EPS, scale=1.0), tick=f"{tag}s")
            b.wait("vector", f"{tag}s", b.semval(f"{tag}s"))
            b.emit("vector", lambda E, so=so: E.reciprocal(
                stats[:, so + 2:so + 3], stats[:, so + 2:so + 3]))
            b.link("vector")
            b.emit("vector", lambda E, so=so: E.tensor_scalar(
                stats[:, so + 3:so + 4], stats[:, so + 1:so + 2], 1.0, EPS,
                ALU.mult, ALU.add))
            b.link("vector")
            b.emit("vector", lambda E, so=so: E.scalar_tensor_tensor(
                dump[:, 20:21], stats[:, so + 2:so + 3], 1.0,
                stats[:, so + 2:so + 3], ALU.bypass, ALU.mult))
            b.link("vector")
            b.emit("vector", lambda E, so=so: E.tensor_mul(
                dump[:, 20:21], dump[:, 20:21], stats[:, so + 3:so + 4]))
            b.link("vector")
            b.emit("vector", lambda E: E.tensor_scalar(
                dump[:, 20:21], dump[:, 20:21], -0.5, 1.5, ALU.mult,
                ALU.add))
            b.link("vector")
            b.emit("vector", lambda E, so=so: E.tensor_mul(
                stats[:, so + 1:so + 2], stats[:, so + 2:so + 3],
                dump[:, 20:21]))
            b.link("vector")
            b.emit("vector", lambda E, so=so: E.scalar_tensor_tensor(
                stats[:, so:so + 1], stats[:, so:so + 1], -1.0,
                stats[:, so + 1:so + 2], ALU.mult, ALU.mult))
            b.link("vector")
            b.emit("vector", lambda E, XA=XA, pp=pp, so=so: E.tensor_scalar(
                xn_tm[:, pp], XA, stats[:, so + 1:so + 2],
                stats[:, so:so + 1], ALU.mult, ALU.add), tick=f"{tag}v")
            b.wait("tensor", f"{tag}v", b.semval(f"{tag}v"))
            for g2 in range(2):
                tb = 4 + g2
                b.wait("tensor", f"bev{tb}", bank_ev[tb])
                for j in range(4):
                    ic = g2 * 4 + j
                    b.emit("tensor", lambda E, ic=ic, j=j, pp=pp, tb=tb:
                           E.transpose(
                               ps[tb][:, j * 128:(j + 1) * 128],
                               xn_tm[:, pp, ic * 128:(ic + 1) * 128],
                               idtf[:]))
                bank_mm[tb] = b.tick_last("tensor", f"bmm{tb}")
                ev = evac("scalar", lambda E, tc=tc, g2=g2, tb=tb: E.copy(
                    xn_fm[:, g2 * 4:(g2 + 1) * 4, tc * 128:(tc + 1) * 128],
                    ps[tb][:, 0:512].rearrange("p (a t) -> p a t", a=4)), tb)
            b.wait("sync", f"bmm{4}", bank_mm[4])
            b.wait("sync", f"bmm{5}", bank_mm[5])
            b.emit("sync", lambda E: E.nop(), tick=f"{tag}xnf{pp}")
        xn_done = [(f"bev{4}", bank_ev[4]), (f"bev{5}", bank_ev[5])]
        vwl = b.dma(vw_r, wv_b[:, :].rearrange("(c p) f -> p c f", p=128),
                    f"{tag}vwl")
        qk_wfree = {}
        for (wsrcd, dstfm, scl, qt) in ((wq_b, q_fm, 0.125, "q"),
                                        (wk_b, k_fm, 1.0, "k")):
            for dc in range(8):
                wp = dc % 2
                if (qt, wp) in qk_wfree:
                    b.wait("sync", *qk_wfree[(qt, wp)])
                if qt == "k" and dc < 2:
                    b.wait("sync", *qk_wfree[("q", wp)])
                wl = b.dma(qkvw[:, wp],
                           wsrcd[:, dc * 128:(dc + 1) * 128].rearrange(
                               "(c p) f -> p c f", p=128), f"{tag}{qt}wl{wp}")
                b.wait("tensor", f"{tag}{qt}wl{wp}", wl)
                if dc == 0 and qt == "q":
                    for nm, v in xn_done:
                        b.wait("tensor", nm, v)
                for qq in range(4):
                    bk = (dc * 4 + qq) % 4
                    mms = [
                        (lambda E, ic=ic, qq=qq, wp=wp, bk=bk: E.matmul(
                            ps[bk][:], qkvw[:, wp, ic],
                            xn_fm[:, ic, qq * 512:(qq + 1) * 512],
                            start=(ic == 0), stop=(ic == 7)))
                        for ic in range(8)]
                    mm_group(mms, bk)
                    evac("scalar", lambda E, dc=dc, qq=qq, bk=bk, d=dstfm,
                         scl=scl: E.activation(
                             d[:, dc, qq * 512:(qq + 1) * 512], ps[bk][:],
                             AF.Copy, bias=0.0, scale=scl), bk)
                    if qq == 3:
                        qk_wfree[(qt, wp)] = (f"bmm{bk}", bank_mm[bk])
        b.wait("tensor", f"{tag}vwl", vwl)
        for tc in range(16):
            for half in range(2):
                bk = (tc * 2 + half) % 4
                mms = [
                    (lambda E, ic=ic, tc=tc, half=half, bk=bk: E.matmul(
                        ps[bk][:], xn_fm[:, ic, tc * 128:(tc + 1) * 128],
                        vw_r[:, ic, half * 512:(half + 1) * 512],
                        start=(ic == 0), stop=(ic == 7)))
                    for ic in range(8)]
                mm_group(mms, bk)
                evac("scalar", lambda E, tc=tc, half=half, bk=bk: E.copy(
                    v_tm[:, tc, half * 512:(half + 1) * 512], ps[bk][:]), bk)
        vdone = [(f"bev{i}", bank_ev[i]) for i in range(4)]
        return vdone

    def attn_heads(Ba, Bh, sq, tag, vdone):
        o_fm, v_tm, k_fm, q_fm = Ba.o_fm, Ba.v_tm, Ba.k_fm, Ba.q_fm
        p0a, ppt = Bh.p0a, Bh.ppt
        sc_B, sc_E, sc_z, sc_m = Bh.sc_B, Bh.sc_E, Bh.sc_z, Bh.sc_m
        sc_E3 = Bh.sc_E.rearrange("p a b -> p (a b)").rearrange(
            "p (a b c) -> p a b c", a=16, b=4)
        hi16, lo16 = Bh.hi16, Bh.lo16
        sc_Mq, sc_cf, aughl = Bh.sc_Mq, Bh.sc_cf, Bh.aughl
        q_aug2, k_aug2 = Bh.q_aug2, Bh.k_aug2
        row0 = sq * SEQ
        b.dma(k_aug2[0][64:128, :], indic[:], f"{tag}ind")
        b.dma(k_aug2[1][0:64, :], indic[:], f"{tag}ind")
        pB_vals = {}
        augv = {}
        augT_prev = [None]

        def prologue(h):
            pi = h % 2
            ch = h // 2
            qof = 64 * pi
            ka, qa = k_aug2[pi], q_aug2[pi]
            if h >= 2:
                b.wait("vector", f"{tag}pB", pB_vals[h - 2])
            if h == 0:
                for nm, v in vdone:
                    b.wait("vector", nm, v)
            b.emit("vector", lambda E, ch=ch, qof=qof, ka=ka: E.tensor_copy(
                ka[qof:qof + 64, :], k_fm[qof:qof + 64, ch, :]))
            b.emit("vector", lambda E, ch=ch, qof=qof, qa=qa: E.tensor_copy(
                qa[qof:qof + 64, :], q_fm[qof:qof + 64, ch, :]),
                tick=f"{tag}aug")
            augv[h] = b.semval(f"{tag}aug")

        def passA_mm_max(h, qc, kq):
            pi = h % 2
            ch = h // 2
            qof = 64 * pi
            bk = kq % 2
            mm_group([
                (lambda E, qc=qc, kq=kq, bk=bk, ch=ch, qof=qof:
                 E.matmul(ps[bk][:],
                          q_fm[qof:qof + 64, ch,
                               qc * 128:(qc + 1) * 128],
                          k_fm[qof:qof + 64, ch,
                               kq * 512:(kq + 1) * 512],
                          start=True, stop=True,
                          tile_position=(qof, 0)))], bk)
            b.wait("vector", f"bmm{bk}", bank_mm[bk])
            b.emit("vector", lambda E, qc=qc, kq=kq, bk=bk:
                   E.tensor_reduce(
                       sc_B[:, qc, kq * 8:(kq + 1) * 8],
                       ps[bk][:].rearrange("p (a x) -> p a x", a=8),
                       AX.X, ALU.max))
            b.link("vector")
            b.emit("vector", lambda E, qc=qc, kq=kq:
                   E.tensor_reduce(sc_Mq[:, qc, kq:kq + 1],
                                   sc_B[:, qc, 0:(kq + 1) * 8], AX.X,
                                   ALU.max), tick=f"{tag}v")
            nmcol = 56 + bk
            b.wait("scalar", f"{tag}v", b.semval(f"{tag}v"))
            b.emit("scalar", lambda E, qc=qc, kq=kq, nmcol=nmcol:
                   E.activation(stats[:, nmcol:nmcol + 1],
                                sc_Mq[:, qc, kq:kq + 1], AF.Copy,
                                bias=0.0, scale=-1.0))
            b.link("scalar")
            ev = evac("scalar", lambda E, kq=kq, bk=bk, nmcol=nmcol:
                      E.activation(p0a[:, kq * 512:(kq + 1) * 512],
                                   ps[bk][:], AF.Exp,
                                   bias=stats[:, nmcol:nmcol + 1],
                                   scale=1.0), bk)
            return ev

        def passA_sum(qc, kq, ev):
            bk = kq % 2
            b.wait("vector", f"bev{bk}", ev)
            b.emit("vector", lambda E, qc=qc, kq=kq: E.tensor_reduce(
                sc_E[:, qc, kq * 8:(kq + 1) * 8],
                p0a[:, kq * 512:(kq + 1) * 512].rearrange(
                    "p (a x) -> p a x", a=8), AX.X, ALU.add),
                tick=f"{tag}v")

        def smalls_augT(h, tail_cb=None):
            pi = h % 2
            aof = 64 * (1 - pi)
            qa = q_aug2[pi]
            if tail_cb is not None:
                tail_cb()
            vA = b.semval(f"{tag}v")
            b.wait("vector", f"{tag}v", vA)
            b.emit("vector", lambda E: E.tensor_sub(
                sc_cf[:], sc_Mq[:],
                sc_Mq[:, :, 3:4].broadcast_to([128, 16, 4])), tick=f"{tag}v")
            b.wait("scalar", f"{tag}v", b.semval(f"{tag}v"))
            b.emit("scalar", lambda E: E.activation(
                sc_cf[:], sc_cf[:], AF.Exp), tick=f"{tag}s")
            b.wait("vector", f"{tag}s", b.semval(f"{tag}s"))
            b.emit("vector", lambda E: E.tensor_mul(
                sc_E3[:], sc_E3[:],
                sc_cf[:].rearrange("p a x -> p a x 1" if False else
                                   "p a x -> p a x").unsqueeze(3)
                .broadcast_to([128, 16, 4, 8])))
            b.link("vector")
            for qc in range(16):
                b.emit("vector", lambda E, qc=qc: E.tensor_tensor_scan(
                    sc_z[:, qc, :], sc_E[:, qc, :], sc_E[:, qc, :], 0.0,
                    ALU.add, ALU.bypass))
                b.emit("vector", lambda E, qc=qc: E.tensor_tensor_scan(
                    sc_m[:, qc, :], sc_B[:, qc, :], sc_B[:, qc, :], -3.0e38,
                    ALU.max, ALU.bypass))
            b.link("vector")
            vz = b.tick_last("vector", f"{tag}v") if False else None
            b.emit("vector", lambda E: E.nop(), tick=f"{tag}v")
            b.wait("scalar", f"{tag}v", b.semval(f"{tag}v"))
            b.emit("scalar", lambda E: E.activation(
                sc_z[:], sc_z[:], AF.Ln), tick=f"{tag}s")
            b.wait("vector", f"{tag}s", b.semval(f"{tag}s"))
            b.emit("vector", lambda E: E.tensor_sub(
                sc_B[:], sc_Mq[:, :, 3:4].broadcast_to([128, 16, 32]),
                sc_m[:]))
            b.link("vector")
            b.emit("vector", lambda E: E.tensor_add(sc_z[:], sc_z[:],
                                                    sc_B[:]))
            b.link("vector")
            for qc in range(16):
                b.emit("vector", lambda E, qc=qc: E.tensor_tensor_scan(
                    sc_m[:, qc, :], sc_z[:, qc, :], sc_z[:, qc, :], 0.0,
                    ALU.add, ALU.bypass))
            b.link("vector")
            b.emit("vector", lambda E: E.tensor_sub(
                sc_B[:], sc_m[:, :, 31:32].broadcast_to([128, 16, 32]),
                sc_m[:]))
            b.link("vector")
            b.emit("vector", lambda E: E.tensor_add(sc_B[:], sc_B[:],
                                                    sc_z[:]))
            b.link("vector")
            b.emit("vector", lambda E: E.scalar_tensor_tensor(
                sc_E[:], sc_B[:], -1.0,
                sc_Mq[:, :, 3:4].broadcast_to([128, 16, 32]),
                ALU.mult, ALU.subtract))
            b.link("vector")
            b.emit("vector", lambda E: E.tensor_copy(hi16[:], sc_E[:]))
            b.link("vector")
            b.emit("vector", lambda E: E.tensor_sub(lo16[:], sc_E[:],
                                                    hi16[:]),
                   tick=f"{tag}v")
            vhl = b.semval(f"{tag}v")
            b.wait("vector", f"{tag}v", vhl)
            for qc in range(16):
                tb = 2 + qc % 2
                if augT_prev[0] is not None:
                    b.wait("vector", *augT_prev[0])
                b.emit("vector", lambda E, qc=qc, aof=aof: E.tensor_copy(
                    aughl[:, aof:aof + 32], hi16[:, qc, :]))
                b.link("vector")
                b.emit("vector", lambda E, qc=qc, aof=aof: E.tensor_copy(
                    aughl[:, aof + 32:aof + 64], lo16[:, qc, :]),
                    tick=f"{tag}v")
                b.wait("tensor", f"{tag}v", b.semval(f"{tag}v"))
                b.wait("tensor", f"bev{tb}", bank_ev[tb])
                b.emit("tensor", lambda E, tb=tb: E.transpose(
                    ps[tb][:, 0:128], aughl[:], idtf[:]))
                bank_mm[tb] = b.tick_last("tensor", f"bmm{tb}")
                augT_prev[0] = (f"bmm{tb}", bank_mm[tb])
                evac("scalar", lambda E, qc=qc, tb=tb, aof=aof, qa=qa: E.copy(
                    qa[aof:aof + 64, qc * 128:(qc + 1) * 128],
                    ps[tb][aof:aof + 64, 0:128]), tb)

        def passB_pre(h):
            b.wait("tensor", f"{tag}aug", augv[h])
            if h == 0:
                b.wait("tensor", f"{tag}ind", 32)
            for pvb in (4, 5, 6, 7):
                b.wait("tensor", f"bev{pvb}", bank_ev[pvb])

        pend_pv = [None]

        def emit_pv(h, kc, qq, ev):
            pi = h % 2
            sp = kc % 2
            bk = 2 + qq % 2
            pvb = 4 + qq
            b.wait("tensor", f"bev{bk}", ev)
            b.emit("tensor", lambda E, kc=kc, qq=qq, sp=sp,
                   h=h, pi=pi, pvb=pvb: E.matmul(
                       ps[pvb][64 * pi:64 * pi + 64, :],
                       v_tm[:, kc, h * 64:(h + 1) * 64],
                       ppt[sp][:, qq * 512:(qq + 1) * 512],
                       start=(kc == 0), stop=(kc == 15),
                       tile_position=(0, 64 * pi)))
            if qq == 3:
                b.tick_last("tensor", f"{tag}pv")

        def passB_sub(h, kc, qq):
            pi = h % 2
            ka, qa = k_aug2[pi], q_aug2[pi]
            sp = kc % 2
            bk = 2 + qq % 2
            mm_group([
                (lambda E, kc=kc, qq=qq, bk=bk, ka=ka, qa=qa: E.matmul(
                    ps[bk][:], ka[:, kc * 128:(kc + 1) * 128],
                    qa[:, qq * 512:(qq + 1) * 512],
                    start=True, stop=True))], bk)
            if pend_pv[0] is not None:
                emit_pv(*pend_pv[0])
                pend_pv[0] = None
            if qq == 0:
                if kc >= 2:
                    b.wait("scalar", f"{tag}pv", b.semval(f"{tag}pv") - 1)
                else:
                    b.wait("scalar", f"{tag}pv", b.semval(f"{tag}pv"))
            ev = evac("scalar", lambda E, qq=qq, bk=bk, sp=sp:
                      E.activation(
                          ppt[sp][:, qq * 512:(qq + 1) * 512],
                          ps[bk][:], AF.Exp), bk)
            pend_pv[0] = (h, kc, qq, ev)

        def passB_tail(h):
            pi = h % 2
            ch = h // 2
            pvv = b.semval(f"{tag}pv")
            for qq in range(4):
                pvb = 4 + qq
                b.wait("scalar", f"{tag}pv", pvv)
                v2 = b.emit("scalar", lambda E, qq=qq, ch=ch, pi=pi,
                            pvb=pvb: E.copy(
                                o_fm[64 * pi:64 * pi + 64, ch,
                                     qq * 512:(qq + 1) * 512],
                                ps[pvb][64 * pi:64 * pi + 64, :]),
                            tick=f"bev{pvb}")
                bank_ev[pvb] = v2
            b.wait("sync", f"{tag}pv", pvv)
            b.emit("sync", lambda E: E.nop(), tick=f"{tag}pB")
            pB_vals[h] = b.semval(f"{tag}pB")

        def passA_loop(h, hB=None):
            pend = []
            i = 0
            for qc in range(16):
                for kq in range(4):
                    ev = passA_mm_max(h, qc, kq)
                    if len(pend) >= 2:
                        passA_sum(*pend.pop(0))
                    pend.append((qc, kq, ev))
                    if hB is not None and i < 56:
                        passB_sub(hB, i // 4, i % 4)
                    i += 1
            for pp in pend:
                passA_sum(*pp)

        def passB_rest(h):
            for j in range(56, 64):
                passB_sub(h, j // 4, j % 4)
            emit_pv(*pend_pv[0])
            pend_pv[0] = None
            passB_tail(h)

        for h in range(H):
            prologue(h)
            if h == 0:
                passA_loop(h)
            else:
                passB_pre(h - 1)
                passA_loop(h, hB=h - 1)
            smalls_augT(h, tail_cb=(
                (lambda hh=h - 1: passB_rest(hh)) if h >= 1 else None))
        passB_pre(H - 1)
        for kc in range(16):
            for qq in range(4):
                passB_sub(H - 1, kc, qq)
        emit_pv(*pend_pv[0])
        pend_pv[0] = None
        passB_tail(H - 1)
        heads_done = [(f"bev{i}", bank_ev[i]) for i in (4, 5, 6, 7)]
        return heads_done

    def attn_wo(Ba, Bw, sq, tag, heads_done):
        o_fm = Ba.o_fm
        x1t_b, wo_r = Bw.x1t_b, Bw.wo_r
        row0 = sq * SEQ
        wol = b.dma(wo_r, wo_b[:, :].rearrange("(c p) f -> p c f", p=128),
                    f"{tag}wol")
        b.wait("tensor", f"{tag}wol", wol)
        for nm, v in heads_done:
            b.wait("tensor", nm, v)
        for qc in range(16):
            pp = qc % 2
            if qc >= 2:
                b.wait("sync", f"{tag}x2{pp}", b.semval(f"{tag}x2{pp}"))
            lv = b.dma(x1t_b[:, pp],
                       x1d[row0 + qc * 128:row0 + (qc + 1) * 128, :],
                       f"{tag}lb{pp}")
            for half in range(2):
                bk = (qc * 2 + half) % 4
                mms = [
                    (lambda E, mc=mc, qc=qc, half=half, bk=bk: E.matmul(
                        ps[bk][:], o_fm[:, mc, qc * 128:(qc + 1) * 128],
                        wo_r[:, mc, half * 512:(half + 1) * 512],
                        start=(mc == 0), stop=(mc == 7)))
                    for mc in range(8)]
                mm_group(mms, bk)
            b.wait("vector", f"{tag}lb{pp}", lv)
            for half in range(2):
                bk = (qc * 2 + half) % 4
                evac("vector", lambda E, pp=pp, half=half, bk=bk:
                     E.tensor_add(x1t_b[:, pp, half * 512:(half + 1) * 512],
                                  ps[bk][:],
                                  x1t_b[:, pp, half * 512:(half + 1) * 512]),
                     bk)
            b.link("vector")
            b.emit("vector", lambda E: E.nop(), tick=f"{tag}ev")
            b.wait("sync", f"{tag}ev", b.semval(f"{tag}ev"))
            b.dma(x2d[row0 + qc * 128:row0 + (qc + 1) * 128, :],
                  x1t_b[:, pp], f"{tag}x2{pp}")
        return [(f"{tag}x2{0}", b.semval(f"{tag}x2{0}")),
                (f"{tag}x2{1}", b.semval(f"{tag}x2{1}"))]

    b.wait("vector", "bev1", bank_ev[1])

    f1_scope = mkscope()
    Bf1 = ffn_bufs(f1_scope)
    st1 = ffn_sublayer(Bf1, x0, x1d, "f1")
    barrier(done=st1)
    f1_scope.close()

    at_scope = mkscope()
    Ba = attn_core_bufs(at_scope)
    for sq, tag in ((0, "a0"), (1, "a1")):
        ln_scope = mkscope()
        Bl = attn_ln_bufs(ln_scope)
        vdone = attn_ln_qkv(Ba, Bl, sq, tag)
        barrier()
        ln_scope.close()
        hd_scope = mkscope()
        Bh = attn_head_bufs(hd_scope)
        hdone = attn_heads(Ba, Bh, sq, tag, vdone)
        barrier()
        hd_scope.close()
        wo_scope = mkscope()
        Bw = attn_wo_bufs(wo_scope)
        sta = attn_wo(Ba, Bw, sq, tag, hdone)
        barrier(done=sta)
        wo_scope.close()
    at_scope.close()

    f3_scope = mkscope()
    Bf3 = ffn_bufs(f3_scope)
    st3 = ffn_sublayer(Bf3, x2d, y, "f3")
    for nm, v in st3:
        b.wait("sync", nm, v)
    f3_scope.close()

    b.run_block()
    b.ctx.close()
    return nc


def _host_inputs(L0_attn_x, Wq, Wk, Wv, Wo, w_gate, w_up, w_down):
    import ml_dtypes
    consts = np.zeros((128, 8), dtype=np.float32)
    consts[:, 0] = 256.0
    consts[:, 1] = EPS
    consts[:, 2] = 0.125
    ind = np.zeros((64, SEQ), dtype=np.float32)
    for t in range(NKV):
        ind[t, t * 64:(t + 1) * 64] = 1.0
        ind[32 + t, t * 64:(t + 1) * 64] = 1.0
    ind = ind.astype(ml_dtypes.bfloat16)
    ident = np.eye(128, dtype=np.float32).astype(ml_dtypes.bfloat16)
    com = {
        "wq": np.ascontiguousarray(Wq), "wk": np.ascontiguousarray(Wk),
        "wv": np.ascontiguousarray(Wv), "wo": np.ascontiguousarray(Wo),
        "wgT": np.ascontiguousarray(w_gate.T),
        "wuT": np.ascontiguousarray(w_up.T),
        "wdT": np.ascontiguousarray(w_down.T),
        "consts": consts, "indic": ind, "ident": ident,
        "identf": np.eye(128, dtype=np.float32),
    }
    xs = np.ascontiguousarray(L0_attn_x).reshape(8, NTOK, D)
    return [dict(com, x0=np.ascontiguousarray(xs[c])) for c in range(8)]


def kernel(L0_attn_x, L0_attn_Wq, L0_attn_Wk, L0_attn_Wv, L0_attn_Wo,
           L0_ffn_w_gate, L0_ffn_w_up, L0_ffn_w_down):
    in_maps = _host_inputs(
        np.asarray(L0_attn_x, dtype=np.float32),
        np.asarray(L0_attn_Wq, dtype=np.float32),
        np.asarray(L0_attn_Wk, dtype=np.float32),
        np.asarray(L0_attn_Wv, dtype=np.float32),
        np.asarray(L0_attn_Wo, dtype=np.float32),
        np.asarray(L0_ffn_w_gate, dtype=np.float32),
        np.asarray(L0_ffn_w_up, dtype=np.float32),
        np.asarray(L0_ffn_w_down, dtype=np.float32))
    nc = build_program()
    res = run_bass_kernel_spmd(nc, in_maps, core_ids=list(range(8)))
    out = np.stack([r["y"] for r in res.results], axis=0)
    return out.reshape(16, SEQ, D)


# version 17 (final)
# speedup vs baseline: 1.0278x; 1.0089x over previous
from contextlib import ExitStack

import numpy as np
import concourse.bass as bass
import concourse.mybir as mybir
from concourse.bass_utils import run_bass_kernel_spmd

F32 = mybir.dt.float32
BF16 = mybir.dt.bfloat16
I16 = mybir.dt.int16
U8 = mybir.dt.uint8
AF = mybir.ActivationFunctionType
ALU = mybir.AluOpType
AX = mybir.AxisListType

D = 1024
INNER = 4096
H = 16
DH = 64
SEQ = 2048
NSEQ = 2
NTOK = NSEQ * SEQ
TT = 512
NTILE = NTOK // TT
TC = TT // 128
EPS = 1e-5
NKV = 32
C32M = 12582912.0
C52M = 6755399441055744.0

ENGS = ("sync", "vector", "scalar", "tensor", "gpsimd")

STAGE_FFN1 = True
STAGE_ATTN = True
STAGE_FFN2 = True


class Builder:
    def __init__(self):
        self.nc = bass.Bass("TRN2", target_bir_lowering=False)
        self.ctx = ExitStack()
        self.q = {e: [] for e in ENGS}
        self.sems = {}
        self.semctr = {}
        self.last_tick = {e: None for e in ENGS}

    def sem(self, name):
        if name not in self.sems:
            self.sems[name] = self.ctx.enter_context(self.nc.semaphore(name))
            self.semctr[name] = 0
        return self.sems[name]

    def semval(self, name):
        self.sem(name)
        return self.semctr[name]

    def emit(self, eng, fn, tick=None, tickn=1):
        if tick is None:
            self.q[eng].append(fn)
            self.last_tick[eng] = None
            return None
        h = self.sem(tick)
        self.semctr[tick] += tickn
        val = self.semctr[tick]
        self.q[eng].append(lambda E, f=fn, h=h, n=tickn: f(E).then_inc(h, n))
        self.last_tick[eng] = (tick, val)
        return val

    def tick_last(self, eng, name, n=1):
        if self.last_tick[eng] is not None:
            return self.emit(eng, lambda E: E.nop(), tick=name, tickn=n)
        h = self.sem(name)
        self.semctr[name] += n
        val = self.semctr[name]
        last = self.q[eng][-1]
        assert not isinstance(last, tuple)
        self.q[eng][-1] = (lambda E, f=last, h=h, m=n: f(E).then_inc(h, m))
        self.last_tick[eng] = (name, val)
        return val

    def wait(self, eng, name, val):
        if val <= 0:
            self.sem(name)
            return
        h = self.sem(name)
        self.q[eng].append(("w", h, val))
        self.last_tick[eng] = None

    def link(self, eng):
        if self.last_tick[eng] is not None:
            name, v = self.last_tick[eng]
            self.wait(eng, name, v)
            return
        name = f"_lk_{eng}"
        v = self.tick_last(eng, name)
        self.wait(eng, name, v)

    def dma(self, out, in_, tick, eng="sync"):
        return self.emit(eng, lambda E, o=out, i=in_: E.dma_start(out=o, in_=i),
                         tick=tick, tickn=16)

    def run_block(self):
        block = self.ctx.enter_context(self.nc.Block())

        def mk(engname):
            def body(E):
                for item in self.q[engname]:
                    if isinstance(item, tuple):
                        _, h, val = item
                        E.wait_ge(h, val)
                    else:
                        item(E)
            return body

        block.sync(mk("sync"))
        block.vector(mk("vector"))
        block.scalar(mk("scalar"))
        block.tensor(mk("tensor"))
        block.gpsimd(mk("gpsimd"))


def build_program():
    b = Builder()
    nc = b.nc
    e = b.ctx.enter_context

    x0 = nc.dram_tensor("x0", [NTOK, D], F32, kind="ExternalInput").ap()
    wq_f = nc.dram_tensor("wq", [D, D], F32, kind="ExternalInput").ap()
    wk_f = nc.dram_tensor("wk", [D, D], F32, kind="ExternalInput").ap()
    wv_f = nc.dram_tensor("wv", [D, D], F32, kind="ExternalInput").ap()
    wo_f = nc.dram_tensor("wo", [D, D], F32, kind="ExternalInput").ap()
    wgT = nc.dram_tensor("wgT", [D, INNER], F32, kind="ExternalInput").ap()
    wuT = nc.dram_tensor("wuT", [D, INNER], F32, kind="ExternalInput").ap()
    wdT = nc.dram_tensor("wdT", [INNER, D], F32, kind="ExternalInput").ap()
    consts = nc.dram_tensor("consts", [128, 8], F32, kind="ExternalInput").ap()
    indic = nc.dram_tensor("indic", [64, SEQ], BF16, kind="ExternalInput").ap()
    ident = nc.dram_tensor("ident", [128, 128], BF16, kind="ExternalInput").ap()
    identf = nc.dram_tensor("identf", [128, 128], F32, kind="ExternalInput").ap()
    y = nc.dram_tensor("y", [NTOK, D], F32, kind="ExternalOutput").ap()

    x1d = nc.dram_tensor("x1d", [NTOK, D], F32, kind="Internal").ap()
    x2d = nc.dram_tensor("x2d", [NTOK, D], F32, kind="Internal").ap()
    wg_q = nc.dram_tensor("wg_q", [D, INNER], BF16, kind="Internal").ap()
    wu_q = nc.dram_tensor("wu_q", [D, INNER], BF16, kind="Internal").ap()
    wd_q = nc.dram_tensor("wd_q", [INNER, D], BF16, kind="Internal").ap()
    wq_b = nc.dram_tensor("wq_b", [D, D], BF16, kind="Internal").ap()
    wk_b = nc.dram_tensor("wk_b", [D, D], BF16, kind="Internal").ap()
    wv_b = nc.dram_tensor("wv_b", [D, D], BF16, kind="Internal").ap()
    wo_b = nc.dram_tensor("wo_b", [D, D], BF16, kind="Internal").ap()

    cst = e(nc.sbuf_tensor("cst", [128, 8], F32))
    nc.const_aps.register(F32, 256.0, cst[:, 0:1])
    nc.const_aps.register(F32, EPS, cst[:, 1:2])
    nc.const_aps.register(F32, 0.125, cst[:, 2:3])
    ones_r = e(nc.sbuf_tensor("ones_r", [1, 128], F32))
    ones_c = e(nc.sbuf_tensor("ones_c", [128, 1], F32))
    mu = e(nc.sbuf_tensor("mu", [128, 4], F32))
    idt = e(nc.sbuf_tensor("idt", [128, 128], BF16))
    idtf = e(nc.sbuf_tensor("idtf", [128, 128], F32))
    stats = e(nc.sbuf_tensor("stats", [128, 72], F32))
    dump = e(nc.sbuf_tensor("dump", [128, D], F32))

    from types import SimpleNamespace

    def mkscope():
        return ExitStack()

    _alloc_n = [0]

    def alloc(scope, shape, dt):
        _alloc_n[0] += 1
        t = scope.enter_context(
            nc.sbuf_tensor(f"t{_alloc_n[0]}", [128] + list(shape), dt))
        return t[:]

    def barrier(done=()):
        vals = {}
        for eng in ("sync", "vector", "scalar", "tensor", "gpsimd"):
            vals[eng] = b.emit(eng, lambda E: E.nop(), tick=f"ph_{eng}")
        for eng in ("sync", "vector", "scalar", "tensor", "gpsimd"):
            for oeng, v in vals.items():
                if oeng != eng:
                    b.wait(eng, f"ph_{oeng}", v)
            for nm, v in done:
                b.wait(eng, nm, v)

    ps = [e(nc.psum_tensor(f"psbank{i}", [128, 512], F32)) for i in range(8)]
    psb = [p[:].bitcast(BF16) for p in ps]

    bank_mm = [0] * 8
    bank_ev = [0] * 8

    def mm_group(matmuls, bank):
        b.wait("tensor", f"bev{bank}", bank_ev[bank])
        for m in matmuls[:-1]:
            b.emit("tensor", m)
        v = b.emit("tensor", matmuls[-1], tick=f"bmm{bank}")
        bank_mm[bank] = v
        return v

    def evac(eng, fn, bank):
        b.wait(eng, f"bmm{bank}", bank_mm[bank])
        v = b.emit(eng, fn, tick=f"bev{bank}")
        bank_ev[bank] = v
        return v

    b.dma(cst[:], consts[:], "cld")
    b.dma(idt[:], ident[:], "cld")
    b.dma(idtf[:], identf[:], "cld")
    for eng in ("vector", "scalar", "tensor", "gpsimd"):
        b.wait(eng, "cld", 48)
    b.emit("vector", lambda E: E.memset(ones_r[:], 1.0))
    vo = b.emit("vector", lambda E: E.memset(ones_c[:], 1.0), tick="vones")
    b.wait("tensor", "vones", vo)
    b.emit("vector", lambda E: E.memset(cst[:, 3:4], 0.75))
    b.link("vector")
    b.emit("vector", lambda E: E.tensor_scalar(
        cst[:, 4:5], cst[:, 3:4], C52M, None, ALU.add))
    b.link("vector")
    b.emit("vector", lambda E: E.tensor_scalar(
        cst[:, 4:5], cst[:, 4:5], -C52M, None, ALU.add))
    b.link("vector")
    b.emit("vector", lambda E: E.tensor_scalar(
        cst[:, 5:6], cst[:, 4:5], -1.0, 1.0, ALU.mult, ALU.add))
    b.link("vector")
    b.emit("vector", lambda E: E.tensor_scalar(
        cst[:, 6:7], cst[:, 4:5], -500.0, None, ALU.mult), tick="wcst")

    wq_scope = mkscope()
    awbuf = alloc(wq_scope, [4, D], BF16)
    for (src, dst) in ((wq_f, wq_b), (wk_f, wk_b), (wv_f, wv_b), (wo_f, wo_b)):
        for ch in range(2):
            seg = src[ch * 512:(ch + 1) * 512, :].rearrange("(c p) f -> p c f", p=128)
            lv = b.dma(awbuf, seg, "awld", eng="gpsimd")
            b.wait("sync", "awld", lv)
            dseg = dst[ch * 512:(ch + 1) * 512, :].rearrange("(c p) f -> p c f", p=128)
            sv = b.dma(dseg, awbuf, "awst")
            b.wait("gpsimd", "awst", sv)

    wqin8 = alloc(wq_scope, [8, INNER], F32)
    wqin32 = wqin8.rearrange("p a b -> p (a b)").rearrange(
        "p (a b) -> p a b", a=32)
    wqv = alloc(wq_scope, [INNER], F32)
    wqq0 = alloc(wq_scope, [INNER], I16)
    wqr = alloc(wq_scope, [2, INNER], BF16)
    wqhi = alloc(wq_scope, [INNER], F32)
    wqleaf = alloc(wq_scope, [1024], F32)

    wq_free = [None, None]
    st_vals = [None, None]

    def wquant(w_src, w_dst, nrow, ncol, mucol):
        nch = nrow // 128
        wqin = wqin8 if nch == 8 else wqin32
        hch = nch // 2
        b.wait("vector", "bmm0", bank_mm[0])
        b.wait("vector", "bmm1", bank_mm[1])
        b.wait("sync", "awst", b.semval("awst"))
        MC = float((1 << 23) + (1 << 22))
        nl = ncol // 128
        lvh = [0, 0]
        for hf in range(2):
            if wq_free[hf] is not None:
                b.wait("sync", *wq_free[hf])
            lvh[hf] = b.dma(
                wqin[:, hf * hch:(hf + 1) * hch, 0:ncol],
                w_src[hf * 128 * hch:(hf + 1) * 128 * hch, :].rearrange(
                    "(c p) f -> p c f", p=128), f"wqld{hf}")
        b.emit("vector", lambda E: E.memset(stats[:, 43:44], 0.0))
        b.link("vector")
        for c in range(nch):
            if c % hch == 0:
                b.wait("vector", f"wqld{c // hch}", lvh[c // hch])
                b.wait("scalar", f"wqld{c // hch}", lvh[c // hch])
            b.wait("scalar", "wq_v", b.semval("wq_v"))
            b.emit("scalar", lambda E, c=c, n=ncol: E.activation(
                wqv[:, 0:n], wqin[:, c, 0:n], AF.Abs), tick="wq_s")
            b.wait("vector", "wq_s", b.semval("wq_s"))
            b.emit("vector", lambda E, n=ncol: E.tensor_scalar(
                wqhi[:, 0:n], wqv[:, 0:n], float(2 ** 20), MC, ALU.mult,
                ALU.add))
            b.link("vector")
            b.emit("vector", lambda E, n=ncol: E.tensor_scalar(
                wqhi[:, 0:n], wqhi[:, 0:n], -MC, None, ALU.add))
            b.link("vector")
            b.emit("vector", lambda E, n=ncol: E.scalar_tensor_tensor(
                wqv[:, 0:n], wqv[:, 0:n], float(2 ** 20), wqhi[:, 0:n],
                ALU.mult, ALU.subtract))
            b.link("vector")
            b.emit("vector", lambda E, n=ncol: E.tensor_reduce(
                stats[:, 42:43], wqv[:, 0:n], AX.X, ALU.add))
            b.link("vector")
            b.emit("vector", lambda E: E.tensor_add(
                stats[:, 43:44], stats[:, 43:44], stats[:, 42:43]))
            b.link("vector")
            b.emit("vector", lambda E, c=c, n=ncol, nl=nl: E.tensor_reduce(
                wqleaf[:, c * nl:(c + 1) * nl],
                wqhi[:, 0:n].rearrange("p (a x) -> p a x", x=128),
                AX.X, ALU.add), tick="wq_v")
        nlf = nch * nl
        b.wait("vector", "wq_v", b.semval("wq_v"))
        b.emit("vector", lambda E, m=nlf: E.tensor_scalar(
            wqhi[:, 0:m], wqleaf[:, 0:m], float(2 ** -10), MC, ALU.mult,
            ALU.add))
        b.link("vector")
        b.emit("vector", lambda E, m=nlf: E.tensor_scalar(
            wqhi[:, 0:m], wqhi[:, 0:m], -MC, None, ALU.add))
        b.link("vector")
        b.emit("vector", lambda E, m=nlf: E.scalar_tensor_tensor(
            wqleaf[:, 0:m], wqhi[:, 0:m], float(-(2 ** 10)), wqleaf[:, 0:m],
            ALU.mult, ALU.add))
        b.link("vector")
        b.emit("vector", lambda E, m=nlf: E.tensor_reduce(
            stats[:, 36:37], wqhi[:, 0:m], AX.X, ALU.add))
        b.emit("vector", lambda E, m=nlf: E.tensor_reduce(
            stats[:, 37:38], wqleaf[:, 0:m], AX.X, ALU.add))
        b.link("vector")
        b.emit("vector", lambda E: E.tensor_scalar(
            stats[:, 38:39], stats[:, 36:37], float(2 ** -9), MC, ALU.mult,
            ALU.add))
        b.emit("vector", lambda E: E.tensor_scalar(
            stats[:, 40:41], stats[:, 37:38], float(2 ** -5), MC, ALU.mult,
            ALU.add))
        b.link("vector")
        b.emit("vector", lambda E: E.tensor_scalar(
            stats[:, 38:39], stats[:, 38:39], -MC, None, ALU.add))
        b.emit("vector", lambda E: E.tensor_scalar(
            stats[:, 40:41], stats[:, 40:41], -MC, None, ALU.add))
        b.link("vector")
        b.emit("vector", lambda E: E.scalar_tensor_tensor(
            stats[:, 39:40], stats[:, 38:39], float(-(2 ** 9)),
            stats[:, 36:37], ALU.mult, ALU.add))
        b.emit("vector", lambda E: E.scalar_tensor_tensor(
            stats[:, 41:42], stats[:, 40:41], float(-(2 ** 5)),
            stats[:, 37:38], ALU.mult, ALU.add), tick="wq_v")
        b.wait("tensor", "wq_v", b.semval("wq_v"))
        mm_group([lambda E: E.matmul(ps[0][0:1, 0:6], ones_c[:, 0:1],
                                     stats[:, 38:44], start=True,
                                     stop=True)], 0)
        evac("vector", lambda E: E.tensor_copy(stats[0:1, 38:44],
                                               ps[0][0:1, 0:6]), 0)
        b.link("vector")
        b.emit("vector", lambda E: E.scalar_tensor_tensor(
            stats[0:1, 38:39], stats[0:1, 38:39], float(2 ** 9),
            stats[0:1, 39:40], ALU.mult, ALU.add))
        b.emit("vector", lambda E: E.scalar_tensor_tensor(
            stats[0:1, 40:41], stats[0:1, 40:41], float(2 ** 5),
            stats[0:1, 41:42], ALU.mult, ALU.add))
        b.link("vector")
        b.emit("vector", lambda E: E.scalar_tensor_tensor(
            stats[0:1, 38:39], stats[0:1, 38:39], float(2 ** 10),
            stats[0:1, 40:41], ALU.mult, ALU.add))
        b.link("vector")
        b.emit("vector", lambda E: E.tensor_add(
            stats[0:1, 38:39], stats[0:1, 38:39], stats[0:1, 43:44]))
        b.link("vector")
        b.emit("vector", lambda E, n=nrow * ncol: E.tensor_scalar(
            stats[0:1, 33:34], stats[0:1, 38:39], float(2.0 ** -20 / n), EPS,
            ALU.mult, ALU.max), tick="wqv")
        b.wait("tensor", "wqv", b.semval("wqv"))
        b.wait("tensor", "bev0", bank_ev[0])
        mm_group([lambda E: E.matmul(ps[1][:, 0:1], ones_r[0:1, :],
                                     stats[0:1, 33:34], start=True,
                                     stop=True)], 1)
        evac("vector", lambda E, mc=mucol: E.tensor_copy(
            mu[:, mc:mc + 1], ps[1][:, 0:1]), 1)
        b.link("vector")
        b.emit("vector", lambda E, mc=mucol: E.reciprocal(
            stats[:, 34:35], mu[:, mc:mc + 1]))
        b.link("vector")
        b.emit("vector", lambda E, mc=mucol: E.scalar_tensor_tensor(
            stats[:, 35:36], stats[:, 34:35], -1.0, mu[:, mc:mc + 1],
            ALU.mult, ALU.mult))
        b.link("vector")
        b.emit("vector", lambda E: E.tensor_scalar(
            stats[:, 35:36], stats[:, 35:36], 2.0, None, ALU.add))
        b.link("vector")
        b.emit("vector", lambda E: E.tensor_mul(
            stats[:, 34:35], stats[:, 34:35], stats[:, 35:36]))
        b.link("vector")
        for c in range(nch):
            b.emit("vector", lambda E, c=c, n=ncol: E.tensor_scalar(
                wqv[:, 0:n], wqin[:, c, 0:n], stats[:, 34:35], None,
                ALU.mult), tick="wq_v")
            b.wait("scalar", "wq_v", b.semval("wq_v"))
            b.emit("scalar", lambda E, n=ncol: E.activation(
                wqq0[:, 0:n], wqv[:, 0:n], AF.Copy, bias=256.75, scale=1.0),
                tick="wq_s")
            b.wait("vector", "wq_s", b.semval("wq_s"))
            b.emit("vector", lambda E, n=ncol: E.scalar_tensor_tensor(
                wqv[:, 0:n], wqq0[:, 0:n], -256.0, wqv[:, 0:n], ALU.add,
                ALU.subtract))
            b.link("vector")
            b.emit("vector", lambda E, n=ncol: E.scalar_tensor_tensor(
                wqv[:, 0:n], wqv[:, 0:n], 0.5, wqq0[:, 0:n], ALU.is_le,
                ALU.add))
            b.link("vector")
            if st_vals[c % 2] is not None:
                b.wait("vector", f"wq_st{c % 2}", st_vals[c % 2])
            b.emit("vector", lambda E, c=c, n=ncol: E.tensor_scalar(
                wqr[:, c % 2, 0:n], wqv[:, 0:n], -257.0, 1.0, ALU.add,
                ALU.min))
            b.link("vector")
            b.emit("vector", lambda E, c=c, n=ncol: E.tensor_scalar(
                wqr[:, c % 2, 0:n], wqr[:, c % 2, 0:n], -1.0, None,
                ALU.max), tick="wq_v")
            b.wait("sync", "wq_v", b.semval("wq_v"))
            st_vals[c % 2] = b.dma(w_dst[c * 128:(c + 1) * 128, 0:ncol],
                                   wqr[:, c % 2, 0:ncol], f"wq_st{c % 2}")
            if c == hch - 1 or c == nch - 1:
                wq_free[c // hch] = ("wq_v", b.semval("wq_v"))

    wquant(wgT, wg_q, D, INNER, 0)
    wquant(wuT, wu_q, D, INNER, 1)
    wquant(wdT, wd_q, INNER, D, 2)
    barrier(done=[("wq_st0", b.semval("wq_st0")),
                  ("wq_st1", b.semval("wq_st1")),
                  ("awst", b.semval("awst"))])
    wq_scope.close()

    def ffn_bufs(scope):
        Bf = SimpleNamespace()
        Bf.xt = [alloc(scope, [TC, D], F32), alloc(scope, [TC, D], F32)]
        Bf.vql = alloc(scope, [TC, D], F32)
        Bf.q0x = alloc(scope, [D], I16)
        Bf.xq_fm = alloc(scope, [8, TT], BF16)
        Bf.gu = alloc(scope, [TC, INNER], F32)
        Bf.hv = alloc(scope, [INNER], F32)
        Bf.hq_fm = alloc(scope, [32, TT], BF16)
        Bf.wbuf = alloc(scope, [2, 8, 512], BF16)
        Bf.wdbuf = alloc(scope, [2, D], BF16)
        return Bf

    def ffn_sublayer(Bf, x_src, x_dst, tag):
        xt, vql, q0x, xq_fm = Bf.xt, Bf.vql, Bf.q0x, Bf.xq_fm
        gu, hv, hq_fm, wbuf, wdbuf = (Bf.gu, Bf.hv, Bf.hq_fm, Bf.wbuf,
                                      Bf.wdbuf)
        wfree_state = [None, None]
        lad_guard = [None]
        xqT_done = [None]

        def stage1(it):
            p = it % 2
            so = 16 * p
            qo = 32 + 16 * p
            X = xt[p]
            if it >= 2:
                b.wait("sync", f"{tag}stx{p}", (it // 2) * 16)
            if lad_guard[0] is not None:
                b.wait("sync", f"{tag}v", lad_guard[0])
            src = x_src[it * TT:(it + 1) * TT, :].rearrange(
                "(c p) f -> p c f", p=128)
            lv = b.dma(X, src, f"{tag}ldx{p}")
            b.wait("vector", f"{tag}ldx{p}", lv)
            b.wait("scalar", f"{tag}ldx{p}", lv)
            b.wait("scalar", f"{tag}v", b.semval(f"{tag}v"))
            for tc in range(TC):
                b.emit("vector", lambda E, tc=tc, X=X, so=so: E.tensor_reduce(
                    stats[:, so + tc:so + tc + 1], X[:, tc], AX.X, ALU.add))
                b.emit("vector", lambda E, tc=tc, X=X, so=so: E.tensor_reduce(
                    stats[:, so + 8 + tc:so + 9 + tc], X[:, tc], AX.X, ALU.max))
                b.emit("vector", lambda E, tc=tc, X=X, so=so: E.tensor_reduce(
                    stats[:, so + 12 + tc:so + 13 + tc], X[:, tc], AX.X,
                    ALU.min))
            b.link("vector")
            for tc in range(TC):
                if tc > 0:
                    b.link("scalar")
                b.emit("scalar", lambda E, tc=tc, X=X, so=so: E.activation(
                    dump[:, :], X[:, tc], AF.Square,
                    accum_out=stats[:, so + 4 + tc:so + 5 + tc]),
                    tick=f"{tag}s")
            ssq = b.semval(f"{tag}s")
            mcol = stats[:, so:so + 4]
            vcol = stats[:, so + 4:so + 8]
            mxc = stats[:, so + 8:so + 12]
            mnc = stats[:, so + 12:so + 16]
            shc = stats[:, qo:qo + 4]
            dqg = stats[:, qo + 4:qo + 8]
            dqu = stats[:, qo + 8:qo + 12]
            dqd = stats[:, 64 + 4 * p:68 + 4 * p]
            b.wait("vector", f"{tag}s", ssq)
            b.emit("vector", lambda E, m=mcol: E.tensor_scalar(
                m, m, 1.0 / D, None, ALU.mult))
            b.link("vector")
            b.emit("vector", lambda E, v=vcol: E.tensor_scalar(
                v, v, 1.0 / D, None, ALU.mult))
            b.link("vector")
            b.emit("vector", lambda E, m=mcol: E.scalar_tensor_tensor(
                dump[:, 0:4], m, 1.0, m, ALU.bypass, ALU.mult))
            b.link("vector")
            b.emit("vector", lambda E, v=vcol: E.tensor_sub(
                v, v, dump[:, 0:4]), tick=f"{tag}v")
            b.wait("scalar", f"{tag}v", b.semval(f"{tag}v"))
            b.emit("scalar", lambda E, v=vcol: E.activation(
                dump[:, 4:8], v, AF.Sqrt, bias=EPS, scale=1.0),
                tick=f"{tag}s")
            b.wait("vector", f"{tag}s", b.semval(f"{tag}s"))
            b.emit("vector", lambda E: E.reciprocal(dump[:, 8:12],
                                                    dump[:, 4:8]))
            b.link("vector")
            b.emit("vector", lambda E, v=vcol: E.tensor_scalar(
                dump[:, 12:16], v, 1.0, EPS, ALU.mult, ALU.add))
            b.link("vector")
            b.emit("vector", lambda E: E.scalar_tensor_tensor(
                dump[:, 16:20], dump[:, 8:12], 1.0, dump[:, 8:12],
                ALU.bypass, ALU.mult))
            b.link("vector")
            b.emit("vector", lambda E: E.tensor_mul(
                dump[:, 16:20], dump[:, 16:20], dump[:, 12:16]))
            b.link("vector")
            b.emit("vector", lambda E: E.tensor_scalar(
                dump[:, 16:20], dump[:, 16:20], -0.5, 1.5, ALU.mult,
                ALU.add))
            b.link("vector")
            b.emit("vector", lambda E, v=vcol: E.tensor_mul(
                v, dump[:, 8:12], dump[:, 16:20]))
            b.link("vector")
            b.emit("vector", lambda E, mx=mxc, m=mcol: E.tensor_sub(mx, mx, m))
            b.link("vector")
            b.emit("vector", lambda E, mn=mnc, m=mcol: E.tensor_sub(mn, m, mn))
            b.link("vector")
            b.emit("vector", lambda E, mx=mxc, mn=mnc: E.tensor_max(mx, mx, mn))
            b.link("vector")
            b.emit("vector", lambda E, mx=mxc, v=vcol: E.tensor_mul(mx, mx, v))
            b.link("vector")
            b.emit("vector", lambda E, mx=mxc: E.tensor_scalar(
                mx, mx, EPS, None, ALU.max))
            b.link("vector")
            b.emit("vector", lambda E, d=dqg, mx=mxc: E.scalar_tensor_tensor(
                d, mx, 1.0 / 127.0, mu[:, 0:1].broadcast_to([128, 4]),
                ALU.mult, ALU.mult))
            b.emit("vector", lambda E, d=dqu, mx=mxc: E.scalar_tensor_tensor(
                d, mx, 1.0 / 127.0, mu[:, 1:2].broadcast_to([128, 4]),
                ALU.mult, ALU.mult))
            b.link("vector")
            b.emit("vector", lambda E, mx=mxc: E.reciprocal(mx, mx))
            b.link("vector")
            b.emit("vector", lambda E, mx=mxc: E.tensor_scalar(
                mx, mx, 127.0, None, ALU.mult))
            b.link("vector")
            b.emit("vector", lambda E, mx=mxc, v=vcol: E.tensor_mul(
                mx, mx, v))
            b.link("vector")
            b.emit("vector", lambda E, mn=mnc, m=mcol, mx=mxc:
                   E.scalar_tensor_tensor(mn, m, -1.0, mx, ALU.mult,
                                          ALU.mult), tick=f"{tag}v")
            vAB = b.semval(f"{tag}v")
            if xqT_done[0] is not None:
                b.wait("vector", *xqT_done[0])
            for tc in range(TC):
                if tc == 0:
                    b.wait("vector", f"{tag}v", vAB)
                b.emit("vector", lambda E, tc=tc, X=X, mx=mxc, mn=mnc:
                       E.tensor_scalar(vql[:, tc], X[:, tc],
                                       mx[:, tc:tc + 1],
                                       mn[:, tc:tc + 1], ALU.mult, ALU.add),
                       tick=f"{tag}v")
                b.wait("scalar", f"{tag}v", b.semval(f"{tag}v"))
                b.emit("scalar", lambda E, tc=tc: E.activation(
                    q0x[:], vql[:, tc], AF.Copy, bias=256.75, scale=1.0),
                    tick=f"{tag}s")
                b.wait("vector", f"{tag}s", b.semval(f"{tag}s"))
                b.emit("vector", lambda E, tc=tc: E.scalar_tensor_tensor(
                    vql[:, tc], q0x[:], -256.0, vql[:, tc], ALU.add,
                    ALU.subtract))
                b.link("vector")
                b.emit("vector", lambda E, tc=tc: E.scalar_tensor_tensor(
                    vql[:, tc], vql[:, tc], 0.5, q0x[:], ALU.is_le,
                    ALU.add))
                b.link("vector")
                b.emit("vector", lambda E, tc=tc: E.tensor_scalar(
                    vql[:, tc], vql[:, tc], -257.0, None, ALU.add),
                    tick=f"{tag}v")
            return b.semval(f"{tag}v")

        def evac_cb(it, tdown):
            p = it % 2
            X = xt[p]
            dqd = stats[:, 64 + 4 * p:68 + 4 * p]
            b.wait("vector", f"{tag}dmm", tdown)
            for dc in range(2):
                for tc in range(TC):
                    bk = dc * 4 + tc
                    bank_ev[bk] = b.emit(
                        "vector", lambda E, dc=dc, tc=tc, bk=bk, X=X, d=dqd:
                        E.scalar_tensor_tensor(
                            X[:, tc, dc * 512:(dc + 1) * 512], ps[bk][:],
                            d[:, tc:tc + 1],
                            X[:, tc, dc * 512:(dc + 1) * 512],
                            ALU.mult, ALU.add), tick=f"bev{bk}")
            b.wait("sync", f"bev{7}", bank_ev[7])
            dstv = x_dst[it * TT:(it + 1) * TT, :].rearrange(
                "(c p) f -> p c f", p=128)
            b.dma(dstv, X, f"{tag}stx{p}")

        def stage2(it, chain_v):
            p = it % 2
            so = 16 * p
            qo = 32 + 16 * p
            X = xt[p]
            shc = stats[:, qo:qo + 4]
            dqg = stats[:, qo + 4:qo + 8]
            dqu = stats[:, qo + 8:qo + 12]
            dqd = stats[:, 64 + 4 * p:68 + 4 * p]
            xq_ev = []
            b.wait("tensor", f"{tag}v", chain_v)
            for tc in range(TC):
                for g2 in range(2):
                    tb = 4 + g2
                    b.wait("tensor", f"bev{tb}", bank_ev[tb])
                    for j in range(4):
                        ic = g2 * 4 + j
                        b.emit("tensor", lambda E, tc=tc, ic=ic, j=j, tb=tb:
                               E.transpose(
                                   ps[tb][:, j * 128:(j + 1) * 128],
                                   vql[:, tc, ic * 128:(ic + 1) * 128],
                                   idtf[:]))
                    bank_mm[tb] = b.tick_last("tensor", f"bmm{tb}")
                    ev = evac("scalar", lambda E, tc=tc, g2=g2, tb=tb: E.copy(
                        xq_fm[:, g2 * 4:(g2 + 1) * 4,
                              tc * 128:(tc + 1) * 128],
                        ps[tb][:, 0:512].rearrange("p (a t) -> p a t", a=4)),
                        tb)
                xq_ev.append((f"bmm{4 + 1}", bank_mm[5]))
                xq_ev.append((f"bev{tb}", ev))
            xqT_done[0] = (f"bmm{5}", bank_mm[5])
            wfree = wfree_state
            silu_last = None
            up_last = None
            for phase in range(2):
                wsrc = wg_q if phase == 0 else wu_q
                if phase == 1:
                    b.wait("vector", *silu_last)
                for oc in range(8):
                    wp = (phase * 8 + oc) % 2
                    if wfree[wp] is not None:
                        b.wait("sync", *wfree[wp])
                    wl = b.dma(wbuf[:, wp],
                               wsrc[:, oc * 512:(oc + 1) * 512].rearrange(
                                   "(c p) f -> p c f", p=128), f"{tag}wld{wp}")
                    b.wait("tensor", f"{tag}wld{wp}", wl)
                    if phase == 0 and oc == 0:
                        b.wait("tensor", *xq_ev[-1])
                        b.wait("tensor", *xq_ev[-2])
                    for tc in range(TC):
                        bk = ((phase * 8 + oc) % 2) * 2 + (tc % 2)
                        mms = [
                            (lambda E, ic=ic, tc=tc, wp=wp, bk=bk: E.matmul(
                                ps[bk][:],
                                xq_fm[:, ic, tc * 128:(tc + 1) * 128],
                                wbuf[:, wp, ic], start=(ic == 0),
                                stop=(ic == 7)))
                            for ic in range(8)]
                        mm_group(mms, bk)
                        if phase == 0:
                            evac("scalar",
                                 lambda E, tc=tc, oc=oc, bk=bk, d=dqg:
                                 E.activation(
                                     gu[:, tc, oc * 512:(oc + 1) * 512],
                                     ps[bk][:], AF.Silu,
                                     scale=d[:, tc:tc + 1]), bk)
                        else:
                            evac("vector",
                                 lambda E, tc=tc, oc=oc, bk=bk, d=dqu:
                                 E.scalar_tensor_tensor(
                                     gu[:, tc, oc * 512:(oc + 1) * 512],
                                     ps[bk][:], d[:, tc:tc + 1],
                                     gu[:, tc, oc * 512:(oc + 1) * 512],
                                     ALU.mult, ALU.mult), bk)
                    bklast = ((phase * 8 + oc) % 2) * 2 + 1
                    wfree[wp] = (f"bmm{bklast}", bank_mm[bklast])
                    if phase == 0:
                        silu_last = (f"bev{bklast}", bank_ev[bklast])
                    else:
                        up_last = (f"bev{bklast}", bank_ev[bklast])
            b.wait("vector", *silu_last)
            b.wait("vector", *up_last)
            for tc in range(TC):
                b.emit("vector", lambda E, tc=tc, s=shc: E.tensor_reduce(
                    s[:, tc:tc + 1], gu[:, tc], AX.X, ALU.max,
                    apply_absolute_value=True))
            b.link("vector")
            b.emit("vector", lambda E, s=shc: E.tensor_scalar(
                s, s, EPS, None, ALU.max))
            b.link("vector")
            b.emit("vector", lambda E, d=dqd, s=shc: E.scalar_tensor_tensor(
                d, s, 0.5 / 127.0, mu[:, 2:3].broadcast_to([128, 4]),
                ALU.mult, ALU.mult))
            b.link("vector")
            b.emit("vector", lambda E, s=shc: E.reciprocal(s, s))
            b.link("vector")
            b.emit("vector", lambda E, s=shc: E.tensor_scalar(
                s, s, 127.0, None, ALU.mult), tick=f"{tag}v")
            vsh = b.semval(f"{tag}v")
            scr = xt[1 - p].rearrange("p a d -> p (a d)")
            if it >= 1:
                b.wait("scalar", f"{tag}stx{1 - p}",
                       ((it - 1) // 2 + 1) * 16)
                b.wait("scalar", "bev7", b.semval("bev7"))
            hq_ev = []
            prevG = None
            for tc in range(TC):
                b.wait("scalar", f"{tag}v", vsh)
                if prevG is not None:
                    b.wait("scalar", f"{tag}v", prevG)
                b.emit("scalar", lambda E, tc=tc, s=shc, scr=scr:
                       E.activation(scr, gu[:, tc], AF.Copy, bias=500.0,
                                    scale=s[:, tc:tc + 1]), tick=f"{tag}s")
                bB = b.semval(f"{tag}s")
                b.wait("vector", f"{tag}v", vsh)
                b.emit("vector", lambda E, tc=tc, s=shc: E.tensor_scalar(
                    hv[:], gu[:, tc], s[:, tc:tc + 1], C32M, ALU.mult,
                    ALU.add))
                b.link("vector")
                b.wait("vector", f"{tag}s", bB)
                b.emit("vector", lambda E, tc=tc: E.tensor_scalar(
                    gu[:, tc], hv[:], -C32M, None, ALU.add), tick=f"{tag}v")
                vC = b.semval(f"{tag}v")
                b.wait("scalar", f"{tag}v", vC)
                b.emit("scalar", lambda E, scr=scr: E.activation(
                    hv[:], scr, AF.Copy, bias=C52M, scale=1.0))
                b.link("scalar")
                b.emit("scalar", lambda E, scr=scr: E.activation(
                    scr, hv[:], AF.Copy, bias=-C52M, scale=1.0),
                    tick=f"{tag}s")
                eE = b.semval(f"{tag}s")
                b.link("vector")
                b.emit("vector", lambda E, tc=tc: E.tensor_scalar(
                    gu[:, tc], gu[:, tc], cst[:, 5:6], cst[:, 6:7],
                    ALU.mult, ALU.add))
                b.link("vector")
                b.wait("vector", f"{tag}s", eE)
                b.emit("vector", lambda E, tc=tc, scr=scr:
                       E.scalar_tensor_tensor(
                           gu[:, tc], scr, cst[:, 4:5], gu[:, tc],
                           ALU.mult, ALU.add), tick=f"{tag}v")
                prevG = b.semval(f"{tag}v")
                b.wait("tensor", f"{tag}v", b.semval(f"{tag}v"))
                for g in range(8):
                    tb = 4 + (g % 2)
                    b.wait("tensor", f"bev{tb}", bank_ev[tb])
                    for j in range(4):
                        oc = g * 4 + j
                        b.emit("tensor", lambda E, tc=tc, oc=oc, j=j, tb=tb:
                               E.transpose(
                                   ps[tb][:, j * 128:(j + 1) * 128],
                                   gu[:, tc, oc * 128:(oc + 1) * 128],
                                   idtf[:]))
                    bank_mm[tb] = b.tick_last("tensor", f"bmm{tb}")
                    ev = evac("vector",
                              lambda E, tc=tc, g=g, tb=tb: E.tensor_copy(
                                  hq_fm[:, g * 4:(g + 1) * 4,
                                        tc * 128:(tc + 1) * 128],
                                  ps[tb][:, 0:512].rearrange(
                                      "p (a t) -> p a t", a=4)), tb)
                    hq_ev.append((f"bev{tb}", ev))
                hq_ev.append((f"bmm{4}", bank_mm[4]))
                hq_ev.append((f"bmm{5}", bank_mm[5]))
            lad_guard[0] = b.semval(f"{tag}v")
            b.wait("tensor", *hq_ev[-1])
            b.wait("tensor", *hq_ev[-2])
            for bk in range(8):
                b.wait("tensor", f"bev{bk}", bank_ev[bk])
            for oc in range(32):
                wp = oc % 2
                if oc >= 2:
                    b.wait("sync", f"{tag}dmm", b.semval(f"{tag}dmm") - 1)
                wl = b.dma(wdbuf[:, wp], wd_q[oc * 128:(oc + 1) * 128, :],
                           f"{tag}dld{wp}")
                b.wait("tensor", f"{tag}dld{wp}", wl)
                for dc in range(2):
                    for tc in range(TC):
                        bk = dc * 4 + tc
                        b.emit("tensor",
                               lambda E, oc=oc, dc=dc, tc=tc, wp=wp, bk=bk:
                               E.matmul(ps[bk][:],
                                        hq_fm[:, oc, tc * 128:(tc + 1) * 128],
                                        wdbuf[:, wp, dc * 512:(dc + 1) * 512],
                                        start=(oc == 0), stop=(oc == 31)))
                b.tick_last("tensor", f"{tag}dmm")
            return b.semval(f"{tag}dmm")

        pending = None
        for it in range(NTILE):
            chain_v = stage1(it)
            if pending is not None:
                evac_cb(*pending)
            tdown = stage2(it, chain_v)
            pending = (it, tdown)
        evac_cb(*pending)
        return [(f"{tag}stx0", b.semval(f"{tag}stx0")),
                (f"{tag}stx1", b.semval(f"{tag}stx1"))]


    def attn_core_bufs(scope):
        Ba = SimpleNamespace()
        Ba.o_fm = alloc(scope, [8, SEQ], BF16)
        Ba.v_tm = alloc(scope, [16, D], BF16)
        Ba.k_fm = alloc(scope, [8, SEQ], BF16)
        Ba.q_fm = alloc(scope, [8, SEQ], BF16)
        return Ba

    def attn_ln_bufs(scope):
        Bl = SimpleNamespace()
        Bl.xn_fm = alloc(scope, [8, SEQ], BF16)
        Bl.x1t_a = alloc(scope, [2, D], F32)
        Bl.xn_tm = alloc(scope, [2, D], F32)
        Bl.qkvw = alloc(scope, [2, 8, 128], BF16)
        Bl.vw_r = alloc(scope, [8, D], BF16)
        return Bl

    def attn_head_bufs(scope):
        Bh = SimpleNamespace()
        Bh.p0a = alloc(scope, [SEQ], F32)
        Bh.ppt = [alloc(scope, [SEQ], BF16), alloc(scope, [SEQ], BF16)]
        Bh.sc_B = alloc(scope, [16, NKV], F32)
        Bh.sc_E = alloc(scope, [16, NKV], F32)
        Bh.sc_z = alloc(scope, [16, NKV], F32)
        Bh.sc_m = alloc(scope, [16, NKV], F32)
        Bh.hi16 = alloc(scope, [16, NKV], BF16)
        Bh.lo16 = alloc(scope, [16, NKV], BF16)
        Bh.sc_Mq = alloc(scope, [16, 4], F32)
        Bh.sc_cf = alloc(scope, [16, 4], F32)
        Bh.aughl = alloc(scope, [128], F32)
        Bh.q_aug2 = [alloc(scope, [SEQ], BF16), alloc(scope, [SEQ], BF16)]
        Bh.k_aug2 = [alloc(scope, [SEQ], BF16), alloc(scope, [SEQ], BF16)]
        return Bh

    def attn_wo_bufs(scope):
        Bw = SimpleNamespace()
        Bw.x1t_b = alloc(scope, [2, D], F32)
        Bw.wo_r = alloc(scope, [8, D], BF16)
        return Bw

    def attn_ln_qkv(Ba, Bl, sq, tag):
        o_fm, v_tm, k_fm, q_fm = Ba.o_fm, Ba.v_tm, Ba.k_fm, Ba.q_fm
        xn_fm, x1t_a, xn_tm = Bl.xn_fm, Bl.x1t_a, Bl.xn_tm
        qkvw, vw_r = Bl.qkvw, Bl.vw_r
        row0 = sq * SEQ
        for tc in range(16):
            pp = tc % 2
            so = 48 + 4 * pp
            if tc >= 2:
                b.wait("sync", f"{tag}xnf{pp}", b.semval(f"{tag}xnf{pp}"))
            lv = b.dma(x1t_a[:, pp],
                       x1d[row0 + tc * 128:row0 + (tc + 1) * 128, :],
                       f"{tag}lx{pp}")
            b.wait("vector", f"{tag}lx{pp}", lv)
            b.wait("scalar", f"{tag}lx{pp}", lv)
            XA = x1t_a[:, pp]
            b.wait("scalar", f"{tag}v", b.semval(f"{tag}v"))
            b.emit("vector", lambda E, XA=XA, so=so: E.tensor_reduce(
                stats[:, so:so + 1], XA, AX.X, ALU.add), tick=f"{tag}v")
            b.emit("scalar", lambda E, XA=XA, so=so: E.activation(
                dump[:, :], XA, AF.Square,
                accum_out=stats[:, so + 1:so + 2]), tick=f"{tag}s")
            b.wait("vector", f"{tag}s", b.semval(f"{tag}s"))
            b.wait("vector", f"{tag}v", b.semval(f"{tag}v"))
            b.emit("vector", lambda E, so=so: E.tensor_scalar(
                stats[:, so:so + 2], stats[:, so:so + 2], 1.0 / D, None,
                ALU.mult))
            b.link("vector")
            b.emit("vector", lambda E, so=so: E.scalar_tensor_tensor(
                stats[:, so + 2:so + 3], stats[:, so:so + 1], 1.0,
                stats[:, so:so + 1], ALU.bypass, ALU.mult))
            b.link("vector")
            b.emit("vector", lambda E, so=so: E.tensor_sub(
                stats[:, so + 1:so + 2], stats[:, so + 1:so + 2],
                stats[:, so + 2:so + 3]), tick=f"{tag}v")
            b.wait("scalar", f"{tag}v", b.semval(f"{tag}v"))
            b.emit("scalar", lambda E, so=so: E.activation(
                stats[:, so + 2:so + 3], stats[:, so + 1:so + 2], AF.Sqrt,
                bias=EPS, scale=1.0), tick=f"{tag}s")
            b.wait("vector", f"{tag}s", b.semval(f"{tag}s"))
            b.emit("vector", lambda E, so=so: E.reciprocal(
                stats[:, so + 2:so + 3], stats[:, so + 2:so + 3]))
            b.link("vector")
            b.emit("vector", lambda E, so=so: E.tensor_scalar(
                stats[:, so + 3:so + 4], stats[:, so + 1:so + 2], 1.0, EPS,
                ALU.mult, ALU.add))
            b.link("vector")
            b.emit("vector", lambda E, so=so: E.scalar_tensor_tensor(
                dump[:, 20:21], stats[:, so + 2:so + 3], 1.0,
                stats[:, so + 2:so + 3], ALU.bypass, ALU.mult))
            b.link("vector")
            b.emit("vector", lambda E, so=so: E.tensor_mul(
                dump[:, 20:21], dump[:, 20:21], stats[:, so + 3:so + 4]))
            b.link("vector")
            b.emit("vector", lambda E: E.tensor_scalar(
                dump[:, 20:21], dump[:, 20:21], -0.5, 1.5, ALU.mult,
                ALU.add))
            b.link("vector")
            b.emit("vector", lambda E, so=so: E.tensor_mul(
                stats[:, so + 1:so + 2], stats[:, so + 2:so + 3],
                dump[:, 20:21]))
            b.link("vector")
            b.emit("vector", lambda E, so=so: E.scalar_tensor_tensor(
                stats[:, so:so + 1], stats[:, so:so + 1], -1.0,
                stats[:, so + 1:so + 2], ALU.mult, ALU.mult))
            b.link("vector")
            b.emit("vector", lambda E, XA=XA, pp=pp, so=so: E.tensor_scalar(
                xn_tm[:, pp], XA, stats[:, so + 1:so + 2],
                stats[:, so:so + 1], ALU.mult, ALU.add), tick=f"{tag}v")
            b.wait("tensor", f"{tag}v", b.semval(f"{tag}v"))
            for g2 in range(2):
                tb = 4 + g2
                b.wait("tensor", f"bev{tb}", bank_ev[tb])
                for j in range(4):
                    ic = g2 * 4 + j
                    b.emit("tensor", lambda E, ic=ic, j=j, pp=pp, tb=tb:
                           E.transpose(
                               ps[tb][:, j * 128:(j + 1) * 128],
                               xn_tm[:, pp, ic * 128:(ic + 1) * 128],
                               idtf[:]))
                bank_mm[tb] = b.tick_last("tensor", f"bmm{tb}")
                ev = evac("scalar", lambda E, tc=tc, g2=g2, tb=tb: E.copy(
                    xn_fm[:, g2 * 4:(g2 + 1) * 4, tc * 128:(tc + 1) * 128],
                    ps[tb][:, 0:512].rearrange("p (a t) -> p a t", a=4)), tb)
            b.wait("sync", f"bmm{4}", bank_mm[4])
            b.wait("sync", f"bmm{5}", bank_mm[5])
            b.emit("sync", lambda E: E.nop(), tick=f"{tag}xnf{pp}")
        xn_done = [(f"bev{4}", bank_ev[4]), (f"bev{5}", bank_ev[5])]
        vwl = b.dma(vw_r, wv_b[:, :].rearrange("(c p) f -> p c f", p=128),
                    f"{tag}vwl")
        qk_wfree = {}
        for (wsrcd, dstfm, scl, qt) in ((wq_b, q_fm, 0.125, "q"),
                                        (wk_b, k_fm, 1.0, "k")):
            for dc in range(8):
                wp = dc % 2
                if (qt, wp) in qk_wfree:
                    b.wait("sync", *qk_wfree[(qt, wp)])
                if qt == "k" and dc < 2:
                    b.wait("sync", *qk_wfree[("q", wp)])
                wl = b.dma(qkvw[:, wp],
                           wsrcd[:, dc * 128:(dc + 1) * 128].rearrange(
                               "(c p) f -> p c f", p=128), f"{tag}{qt}wl{wp}")
                b.wait("tensor", f"{tag}{qt}wl{wp}", wl)
                if dc == 0 and qt == "q":
                    for nm, v in xn_done:
                        b.wait("tensor", nm, v)
                for qq in range(4):
                    bk = (dc * 4 + qq) % 4
                    mms = [
                        (lambda E, ic=ic, qq=qq, wp=wp, bk=bk: E.matmul(
                            ps[bk][:], qkvw[:, wp, ic],
                            xn_fm[:, ic, qq * 512:(qq + 1) * 512],
                            start=(ic == 0), stop=(ic == 7)))
                        for ic in range(8)]
                    mm_group(mms, bk)
                    evac("scalar", lambda E, dc=dc, qq=qq, bk=bk, d=dstfm,
                         scl=scl: E.activation(
                             d[:, dc, qq * 512:(qq + 1) * 512], ps[bk][:],
                             AF.Copy, bias=0.0, scale=scl), bk)
                    if qq == 3:
                        qk_wfree[(qt, wp)] = (f"bmm{bk}", bank_mm[bk])
        b.wait("tensor", f"{tag}vwl", vwl)
        for tc in range(16):
            for half in range(2):
                bk = (tc * 2 + half) % 4
                mms = [
                    (lambda E, ic=ic, tc=tc, half=half, bk=bk: E.matmul(
                        ps[bk][:], xn_fm[:, ic, tc * 128:(tc + 1) * 128],
                        vw_r[:, ic, half * 512:(half + 1) * 512],
                        start=(ic == 0), stop=(ic == 7)))
                    for ic in range(8)]
                mm_group(mms, bk)
                evac("scalar", lambda E, tc=tc, half=half, bk=bk: E.copy(
                    v_tm[:, tc, half * 512:(half + 1) * 512], ps[bk][:]), bk)
        vdone = [(f"bev{i}", bank_ev[i]) for i in range(4)]
        return vdone

    def attn_heads(Ba, Bh, sq, tag, vdone):
        o_fm, v_tm, k_fm, q_fm = Ba.o_fm, Ba.v_tm, Ba.k_fm, Ba.q_fm
        p0a, ppt = Bh.p0a, Bh.ppt
        sc_B, sc_E, sc_z, sc_m = Bh.sc_B, Bh.sc_E, Bh.sc_z, Bh.sc_m
        sc_E3 = Bh.sc_E.rearrange("p a b -> p (a b)").rearrange(
            "p (a b c) -> p a b c", a=16, b=4)
        hi16, lo16 = Bh.hi16, Bh.lo16
        sc_Mq, sc_cf, aughl = Bh.sc_Mq, Bh.sc_cf, Bh.aughl
        q_aug2, k_aug2 = Bh.q_aug2, Bh.k_aug2
        row0 = sq * SEQ
        b.dma(k_aug2[0][64:128, :], indic[:], f"{tag}ind")
        b.dma(k_aug2[1][0:64, :], indic[:], f"{tag}ind")
        pB_vals = {}
        augv = {}
        augT_prev = [None]

        def prologue(h):
            pi = h % 2
            ch = h // 2
            qof = 64 * pi
            ka, qa = k_aug2[pi], q_aug2[pi]
            if h >= 2:
                b.wait("vector", f"{tag}pB", pB_vals[h - 2])
            if h == 0:
                for nm, v in vdone:
                    b.wait("vector", nm, v)
            b.emit("vector", lambda E, ch=ch, qof=qof, ka=ka: E.tensor_copy(
                ka[qof:qof + 64, :], k_fm[qof:qof + 64, ch, :]))
            b.emit("vector", lambda E, ch=ch, qof=qof, qa=qa: E.tensor_copy(
                qa[qof:qof + 64, :], q_fm[qof:qof + 64, ch, :]),
                tick=f"{tag}aug")
            augv[h] = b.semval(f"{tag}aug")

        def passA_mm_max(h, qc, kq):
            pi = h % 2
            ch = h // 2
            qof = 64 * pi
            bk = kq % 2
            mm_group([
                (lambda E, qc=qc, kq=kq, bk=bk, ch=ch, qof=qof:
                 E.matmul(ps[bk][:],
                          q_fm[qof:qof + 64, ch,
                               qc * 128:(qc + 1) * 128],
                          k_fm[qof:qof + 64, ch,
                               kq * 512:(kq + 1) * 512],
                          start=True, stop=True,
                          tile_position=(qof, 0)))], bk)
            b.wait("vector", f"bmm{bk}", bank_mm[bk])
            b.emit("vector", lambda E, qc=qc, kq=kq, bk=bk:
                   E.tensor_reduce(
                       sc_B[:, qc, kq * 8:(kq + 1) * 8],
                       ps[bk][:].rearrange("p (a x) -> p a x", a=8),
                       AX.X, ALU.max))
            b.link("vector")
            b.emit("vector", lambda E, qc=qc, kq=kq:
                   E.tensor_reduce(sc_Mq[:, qc, kq:kq + 1],
                                   sc_B[:, qc, 0:(kq + 1) * 8], AX.X,
                                   ALU.max), tick=f"{tag}v")
            nmcol = 56 + bk
            b.wait("scalar", f"{tag}v", b.semval(f"{tag}v"))
            b.emit("scalar", lambda E, qc=qc, kq=kq, nmcol=nmcol:
                   E.activation(stats[:, nmcol:nmcol + 1],
                                sc_Mq[:, qc, kq:kq + 1], AF.Copy,
                                bias=0.0, scale=-1.0))
            b.link("scalar")
            ev = evac("scalar", lambda E, kq=kq, bk=bk, nmcol=nmcol:
                      E.activation(p0a[:, kq * 512:(kq + 1) * 512],
                                   ps[bk][:], AF.Exp,
                                   bias=stats[:, nmcol:nmcol + 1],
                                   scale=1.0), bk)
            return ev

        def passA_sum(qc, kq, ev):
            bk = kq % 2
            b.wait("vector", f"bev{bk}", ev)
            b.emit("vector", lambda E, qc=qc, kq=kq: E.tensor_reduce(
                sc_E[:, qc, kq * 8:(kq + 1) * 8],
                p0a[:, kq * 512:(kq + 1) * 512].rearrange(
                    "p (a x) -> p a x", a=8), AX.X, ALU.add),
                tick=f"{tag}v")

        def smalls_augT(h, tail_cb=None):
            pi = h % 2
            aof = 64 * (1 - pi)
            qa = q_aug2[pi]
            if tail_cb is not None:
                tail_cb()
            vA = b.semval(f"{tag}v")
            b.wait("vector", f"{tag}v", vA)
            b.emit("vector", lambda E: E.tensor_sub(
                sc_cf[:], sc_Mq[:],
                sc_Mq[:, :, 3:4].broadcast_to([128, 16, 4])), tick=f"{tag}v")
            b.wait("scalar", f"{tag}v", b.semval(f"{tag}v"))
            b.emit("scalar", lambda E: E.activation(
                sc_cf[:], sc_cf[:], AF.Exp), tick=f"{tag}s")
            b.wait("vector", f"{tag}s", b.semval(f"{tag}s"))
            b.emit("vector", lambda E: E.tensor_mul(
                sc_E3[:], sc_E3[:],
                sc_cf[:].rearrange("p a x -> p a x 1" if False else
                                   "p a x -> p a x").unsqueeze(3)
                .broadcast_to([128, 16, 4, 8])))
            b.link("vector")
            for qc in range(16):
                b.emit("vector", lambda E, qc=qc: E.tensor_tensor_scan(
                    sc_z[:, qc, :], sc_E[:, qc, :], sc_E[:, qc, :], 0.0,
                    ALU.add, ALU.bypass))
                b.emit("vector", lambda E, qc=qc: E.tensor_tensor_scan(
                    sc_m[:, qc, :], sc_B[:, qc, :], sc_B[:, qc, :], -3.0e38,
                    ALU.max, ALU.bypass))
            b.link("vector")
            vz = b.tick_last("vector", f"{tag}v") if False else None
            b.emit("vector", lambda E: E.nop(), tick=f"{tag}v")
            b.wait("scalar", f"{tag}v", b.semval(f"{tag}v"))
            b.emit("scalar", lambda E: E.activation(
                sc_z[:], sc_z[:], AF.Ln), tick=f"{tag}s")
            b.wait("vector", f"{tag}s", b.semval(f"{tag}s"))
            b.emit("vector", lambda E: E.tensor_sub(
                sc_B[:], sc_Mq[:, :, 3:4].broadcast_to([128, 16, 32]),
                sc_m[:]))
            b.link("vector")
            b.emit("vector", lambda E: E.tensor_add(sc_z[:], sc_z[:],
                                                    sc_B[:]))
            b.link("vector")
            for qc in range(16):
                b.emit("vector", lambda E, qc=qc: E.tensor_tensor_scan(
                    sc_m[:, qc, :], sc_z[:, qc, :], sc_z[:, qc, :], 0.0,
                    ALU.add, ALU.bypass))
            b.link("vector")
            b.emit("vector", lambda E: E.tensor_sub(
                sc_B[:], sc_m[:, :, 31:32].broadcast_to([128, 16, 32]),
                sc_m[:]))
            b.link("vector")
            b.emit("vector", lambda E: E.tensor_add(sc_B[:], sc_B[:],
                                                    sc_z[:]))
            b.link("vector")
            b.emit("vector", lambda E: E.scalar_tensor_tensor(
                sc_E[:], sc_B[:], -1.0,
                sc_Mq[:, :, 3:4].broadcast_to([128, 16, 32]),
                ALU.mult, ALU.subtract))
            b.link("vector")
            b.emit("vector", lambda E: E.tensor_copy(hi16[:], sc_E[:]))
            b.link("vector")
            b.emit("vector", lambda E: E.tensor_sub(lo16[:], sc_E[:],
                                                    hi16[:]),
                   tick=f"{tag}v")
            vhl = b.semval(f"{tag}v")
            b.wait("vector", f"{tag}v", vhl)
            for qc in range(16):
                tb = 2 + qc % 2
                if augT_prev[0] is not None:
                    b.wait("vector", *augT_prev[0])
                b.emit("vector", lambda E, qc=qc, aof=aof: E.tensor_copy(
                    aughl[:, aof:aof + 32], hi16[:, qc, :]))
                b.link("vector")
                b.emit("vector", lambda E, qc=qc, aof=aof: E.tensor_copy(
                    aughl[:, aof + 32:aof + 64], lo16[:, qc, :]),
                    tick=f"{tag}v")
                b.wait("tensor", f"{tag}v", b.semval(f"{tag}v"))
                b.wait("tensor", f"bev{tb}", bank_ev[tb])
                b.emit("tensor", lambda E, tb=tb: E.transpose(
                    ps[tb][:, 0:128], aughl[:], idtf[:]))
                bank_mm[tb] = b.tick_last("tensor", f"bmm{tb}")
                augT_prev[0] = (f"bmm{tb}", bank_mm[tb])
                evac("scalar", lambda E, qc=qc, tb=tb, aof=aof, qa=qa: E.copy(
                    qa[aof:aof + 64, qc * 128:(qc + 1) * 128],
                    ps[tb][aof:aof + 64, 0:128]), tb)

        def passB_pre(h):
            b.wait("tensor", f"{tag}aug", augv[h])
            if h == 0:
                b.wait("tensor", f"{tag}ind", 32)
            for pvb in (4, 5, 6, 7):
                b.wait("tensor", f"bev{pvb}", bank_ev[pvb])

        pend_pv = [None]

        def emit_pv(h, kc, qq, ev):
            pi = h % 2
            sp = kc % 2
            bk = 2 + qq % 2
            pvb = 4 + qq
            b.wait("tensor", f"bev{bk}", ev)
            b.emit("tensor", lambda E, kc=kc, qq=qq, sp=sp,
                   h=h, pi=pi, pvb=pvb: E.matmul(
                       ps[pvb][64 * pi:64 * pi + 64, :],
                       v_tm[:, kc, h * 64:(h + 1) * 64],
                       ppt[sp][:, qq * 512:(qq + 1) * 512],
                       start=(kc == 0), stop=(kc == 15),
                       tile_position=(0, 64 * pi)))
            if qq == 3:
                b.tick_last("tensor", f"{tag}pv")

        def passB_sub(h, kc, qq):
            pi = h % 2
            ka, qa = k_aug2[pi], q_aug2[pi]
            sp = kc % 2
            bk = 2 + qq % 2
            mm_group([
                (lambda E, kc=kc, qq=qq, bk=bk, ka=ka, qa=qa: E.matmul(
                    ps[bk][:], ka[:, kc * 128:(kc + 1) * 128],
                    qa[:, qq * 512:(qq + 1) * 512],
                    start=True, stop=True))], bk)
            if pend_pv[0] is not None:
                emit_pv(*pend_pv[0])
                pend_pv[0] = None
            if qq == 0:
                if kc >= 2:
                    b.wait("scalar", f"{tag}pv", b.semval(f"{tag}pv") - 1)
                else:
                    b.wait("scalar", f"{tag}pv", b.semval(f"{tag}pv"))
            ev = evac("scalar", lambda E, qq=qq, bk=bk, sp=sp:
                      E.activation(
                          ppt[sp][:, qq * 512:(qq + 1) * 512],
                          ps[bk][:], AF.Exp), bk)
            pend_pv[0] = (h, kc, qq, ev)

        def passB_tail(h):
            pi = h % 2
            ch = h // 2
            pvv = b.semval(f"{tag}pv")
            for qq in range(4):
                pvb = 4 + qq
                b.wait("scalar", f"{tag}pv", pvv)
                v2 = b.emit("scalar", lambda E, qq=qq, ch=ch, pi=pi,
                            pvb=pvb: E.copy(
                                o_fm[64 * pi:64 * pi + 64, ch,
                                     qq * 512:(qq + 1) * 512],
                                ps[pvb][64 * pi:64 * pi + 64, :]),
                            tick=f"bev{pvb}")
                bank_ev[pvb] = v2
            b.wait("sync", f"{tag}pv", pvv)
            b.emit("sync", lambda E: E.nop(), tick=f"{tag}pB")
            pB_vals[h] = b.semval(f"{tag}pB")

        def passA_loop(h, hB=None):
            pend = []
            i = 0
            for qc in range(16):
                for kq in range(4):
                    ev = passA_mm_max(h, qc, kq)
                    if len(pend) >= 2:
                        passA_sum(*pend.pop(0))
                    pend.append((qc, kq, ev))
                    if hB is not None and i < 56:
                        passB_sub(hB, i // 4, i % 4)
                    i += 1
            for pp in pend:
                passA_sum(*pp)

        def passB_rest(h):
            for j in range(56, 64):
                passB_sub(h, j // 4, j % 4)
            emit_pv(*pend_pv[0])
            pend_pv[0] = None
            passB_tail(h)

        for h in range(H):
            prologue(h)
            if h == 0:
                passA_loop(h)
            else:
                passB_pre(h - 1)
                passA_loop(h, hB=h - 1)
            smalls_augT(h, tail_cb=(
                (lambda hh=h - 1: passB_rest(hh)) if h >= 1 else None))
        passB_pre(H - 1)
        for kc in range(16):
            for qq in range(4):
                passB_sub(H - 1, kc, qq)
        emit_pv(*pend_pv[0])
        pend_pv[0] = None
        passB_tail(H - 1)
        heads_done = [(f"bev{i}", bank_ev[i]) for i in (4, 5, 6, 7)]
        return heads_done

    def attn_wo(Ba, Bw, sq, tag, heads_done):
        o_fm = Ba.o_fm
        x1t_b, wo_r = Bw.x1t_b, Bw.wo_r
        row0 = sq * SEQ
        wol = b.dma(wo_r, wo_b[:, :].rearrange("(c p) f -> p c f", p=128),
                    f"{tag}wol")
        b.wait("tensor", f"{tag}wol", wol)
        for nm, v in heads_done:
            b.wait("tensor", nm, v)
        for qc in range(16):
            pp = qc % 2
            if qc >= 2:
                b.wait("sync", f"{tag}x2{pp}", b.semval(f"{tag}x2{pp}"))
            lv = b.dma(x1t_b[:, pp],
                       x1d[row0 + qc * 128:row0 + (qc + 1) * 128, :],
                       f"{tag}lb{pp}")
            for half in range(2):
                bk = (qc * 2 + half) % 4
                mms = [
                    (lambda E, mc=mc, qc=qc, half=half, bk=bk: E.matmul(
                        ps[bk][:], o_fm[:, mc, qc * 128:(qc + 1) * 128],
                        wo_r[:, mc, half * 512:(half + 1) * 512],
                        start=(mc == 0), stop=(mc == 7)))
                    for mc in range(8)]
                mm_group(mms, bk)
            b.wait("vector", f"{tag}lb{pp}", lv)
            for half in range(2):
                bk = (qc * 2 + half) % 4
                evac("vector", lambda E, pp=pp, half=half, bk=bk:
                     E.tensor_add(x1t_b[:, pp, half * 512:(half + 1) * 512],
                                  ps[bk][:],
                                  x1t_b[:, pp, half * 512:(half + 1) * 512]),
                     bk)
            b.link("vector")
            b.emit("vector", lambda E: E.nop(), tick=f"{tag}ev")
            b.wait("sync", f"{tag}ev", b.semval(f"{tag}ev"))
            b.dma(x2d[row0 + qc * 128:row0 + (qc + 1) * 128, :],
                  x1t_b[:, pp], f"{tag}x2{pp}")
        return [(f"{tag}x2{0}", b.semval(f"{tag}x2{0}")),
                (f"{tag}x2{1}", b.semval(f"{tag}x2{1}"))]

    b.wait("vector", "bev1", bank_ev[1])

    f1_scope = mkscope()
    Bf1 = ffn_bufs(f1_scope)
    st1 = ffn_sublayer(Bf1, x0, x1d, "f1")
    barrier(done=st1)
    f1_scope.close()

    at_scope = mkscope()
    Ba = attn_core_bufs(at_scope)
    for sq, tag in ((0, "a0"), (1, "a1")):
        ln_scope = mkscope()
        Bl = attn_ln_bufs(ln_scope)
        vdone = attn_ln_qkv(Ba, Bl, sq, tag)
        barrier()
        ln_scope.close()
        hd_scope = mkscope()
        Bh = attn_head_bufs(hd_scope)
        hdone = attn_heads(Ba, Bh, sq, tag, vdone)
        barrier()
        hd_scope.close()
        wo_scope = mkscope()
        Bw = attn_wo_bufs(wo_scope)
        sta = attn_wo(Ba, Bw, sq, tag, hdone)
        barrier(done=sta)
        wo_scope.close()
    at_scope.close()

    f3_scope = mkscope()
    Bf3 = ffn_bufs(f3_scope)
    st3 = ffn_sublayer(Bf3, x2d, y, "f3")
    for nm, v in st3:
        b.wait("sync", nm, v)
    f3_scope.close()

    b.run_block()
    b.ctx.close()
    return nc


def _host_inputs(L0_attn_x, Wq, Wk, Wv, Wo, w_gate, w_up, w_down):
    import ml_dtypes
    consts = np.zeros((128, 8), dtype=np.float32)
    consts[:, 0] = 256.0
    consts[:, 1] = EPS
    consts[:, 2] = 0.125
    ind = np.zeros((64, SEQ), dtype=np.float32)
    for t in range(NKV):
        ind[t, t * 64:(t + 1) * 64] = 1.0
        ind[32 + t, t * 64:(t + 1) * 64] = 1.0
    ind = ind.astype(ml_dtypes.bfloat16)
    ident = np.eye(128, dtype=np.float32).astype(ml_dtypes.bfloat16)
    com = {
        "wq": np.ascontiguousarray(Wq), "wk": np.ascontiguousarray(Wk),
        "wv": np.ascontiguousarray(Wv), "wo": np.ascontiguousarray(Wo),
        "wgT": np.ascontiguousarray(w_gate.T),
        "wuT": np.ascontiguousarray(w_up.T),
        "wdT": np.ascontiguousarray(w_down.T),
        "consts": consts, "indic": ind, "ident": ident,
        "identf": np.eye(128, dtype=np.float32),
    }
    xs = np.ascontiguousarray(L0_attn_x).reshape(8, NTOK, D)
    return [dict(com, x0=np.ascontiguousarray(xs[c])) for c in range(8)]


def kernel(L0_attn_x, L0_attn_Wq, L0_attn_Wk, L0_attn_Wv, L0_attn_Wo,
           L0_ffn_w_gate, L0_ffn_w_up, L0_ffn_w_down):
    in_maps = _host_inputs(
        np.asarray(L0_attn_x, dtype=np.float32),
        np.asarray(L0_attn_Wq, dtype=np.float32),
        np.asarray(L0_attn_Wk, dtype=np.float32),
        np.asarray(L0_attn_Wv, dtype=np.float32),
        np.asarray(L0_attn_Wo, dtype=np.float32),
        np.asarray(L0_ffn_w_gate, dtype=np.float32),
        np.asarray(L0_ffn_w_up, dtype=np.float32),
        np.asarray(L0_ffn_w_down, dtype=np.float32))
    nc = build_program()
    res = run_bass_kernel_spmd(nc, in_maps, core_ids=list(range(8)))
    out = np.stack([r["y"] for r in res.results], axis=0)
    return out.reshape(16, SEQ, D)


# version 18
# speedup vs baseline: 1.0318x; 1.0039x over previous
from contextlib import ExitStack

import numpy as np
import concourse.bass as bass
import concourse.mybir as mybir
from concourse.bass_utils import run_bass_kernel_spmd

F32 = mybir.dt.float32
BF16 = mybir.dt.bfloat16
I16 = mybir.dt.int16
U8 = mybir.dt.uint8
AF = mybir.ActivationFunctionType
ALU = mybir.AluOpType
AX = mybir.AxisListType

D = 1024
INNER = 4096
H = 16
DH = 64
SEQ = 2048
NSEQ = 2
NTOK = NSEQ * SEQ
TT = 512
NTILE = NTOK // TT
TC = TT // 128
EPS = 1e-5
NKV = 32
C32M = 12582912.0
C52M = 6755399441055744.0

ENGS = ("sync", "vector", "scalar", "tensor", "gpsimd")

STAGE_FFN1 = True
STAGE_ATTN = True
STAGE_FFN2 = True


class Builder:
    def __init__(self):
        self.nc = bass.Bass("TRN2", target_bir_lowering=False)
        self.ctx = ExitStack()
        self.q = {e: [] for e in ENGS}
        self.sems = {}
        self.semctr = {}
        self.last_tick = {e: None for e in ENGS}

    def sem(self, name):
        if name not in self.sems:
            self.sems[name] = self.ctx.enter_context(self.nc.semaphore(name))
            self.semctr[name] = 0
        return self.sems[name]

    def semval(self, name):
        self.sem(name)
        return self.semctr[name]

    def emit(self, eng, fn, tick=None, tickn=1):
        if tick is None:
            self.q[eng].append(fn)
            self.last_tick[eng] = None
            return None
        h = self.sem(tick)
        self.semctr[tick] += tickn
        val = self.semctr[tick]
        self.q[eng].append(lambda E, f=fn, h=h, n=tickn: f(E).then_inc(h, n))
        self.last_tick[eng] = (tick, val)
        return val

    def tick_last(self, eng, name, n=1):
        if self.last_tick[eng] is not None:
            return self.emit(eng, lambda E: E.nop(), tick=name, tickn=n)
        h = self.sem(name)
        self.semctr[name] += n
        val = self.semctr[name]
        last = self.q[eng][-1]
        assert not isinstance(last, tuple)
        self.q[eng][-1] = (lambda E, f=last, h=h, m=n: f(E).then_inc(h, m))
        self.last_tick[eng] = (name, val)
        return val

    def wait(self, eng, name, val):
        if val <= 0:
            self.sem(name)
            return
        h = self.sem(name)
        self.q[eng].append(("w", h, val))
        self.last_tick[eng] = None

    def link(self, eng):
        if self.last_tick[eng] is not None:
            name, v = self.last_tick[eng]
            self.wait(eng, name, v)
            return
        name = f"_lk_{eng}"
        v = self.tick_last(eng, name)
        self.wait(eng, name, v)

    def dma(self, out, in_, tick, eng="sync"):
        return self.emit(eng, lambda E, o=out, i=in_: E.dma_start(out=o, in_=i),
                         tick=tick, tickn=16)

    def run_block(self):
        block = self.ctx.enter_context(self.nc.Block())

        def mk(engname):
            def body(E):
                for item in self.q[engname]:
                    if isinstance(item, tuple):
                        _, h, val = item
                        E.wait_ge(h, val)
                    else:
                        item(E)
            return body

        block.sync(mk("sync"))
        block.vector(mk("vector"))
        block.scalar(mk("scalar"))
        block.tensor(mk("tensor"))
        block.gpsimd(mk("gpsimd"))


def build_program():
    b = Builder()
    nc = b.nc
    e = b.ctx.enter_context

    x0 = nc.dram_tensor("x0", [NTOK, D], F32, kind="ExternalInput").ap()
    wq_f = nc.dram_tensor("wq", [D, D], F32, kind="ExternalInput").ap()
    wk_f = nc.dram_tensor("wk", [D, D], F32, kind="ExternalInput").ap()
    wv_f = nc.dram_tensor("wv", [D, D], F32, kind="ExternalInput").ap()
    wo_f = nc.dram_tensor("wo", [D, D], F32, kind="ExternalInput").ap()
    wgT = nc.dram_tensor("wgT", [D, INNER], F32, kind="ExternalInput").ap()
    wuT = nc.dram_tensor("wuT", [D, INNER], F32, kind="ExternalInput").ap()
    wdT = nc.dram_tensor("wdT", [INNER, D], F32, kind="ExternalInput").ap()
    consts = nc.dram_tensor("consts", [128, 8], F32, kind="ExternalInput").ap()
    indic = nc.dram_tensor("indic", [64, SEQ], BF16, kind="ExternalInput").ap()
    ident = nc.dram_tensor("ident", [128, 128], BF16, kind="ExternalInput").ap()
    identf = nc.dram_tensor("identf", [128, 128], F32, kind="ExternalInput").ap()
    y = nc.dram_tensor("y", [NTOK, D], F32, kind="ExternalOutput").ap()

    x1d = nc.dram_tensor("x1d", [NTOK, D], F32, kind="Internal").ap()
    x2d = nc.dram_tensor("x2d", [NTOK, D], F32, kind="Internal").ap()
    wg_q = nc.dram_tensor("wg_q", [D, INNER], BF16, kind="Internal").ap()
    wu_q = nc.dram_tensor("wu_q", [D, INNER], BF16, kind="Internal").ap()
    wd_q = nc.dram_tensor("wd_q", [INNER, D], BF16, kind="Internal").ap()
    wq_b = nc.dram_tensor("wq_b", [D, D], BF16, kind="Internal").ap()
    wk_b = nc.dram_tensor("wk_b", [D, D], BF16, kind="Internal").ap()
    wv_b = nc.dram_tensor("wv_b", [D, D], BF16, kind="Internal").ap()
    wo_b = nc.dram_tensor("wo_b", [D, D], BF16, kind="Internal").ap()

    cst = e(nc.sbuf_tensor("cst", [128, 8], F32))
    nc.const_aps.register(F32, 256.0, cst[:, 0:1])
    nc.const_aps.register(F32, EPS, cst[:, 1:2])
    nc.const_aps.register(F32, 0.125, cst[:, 2:3])
    ones_r = e(nc.sbuf_tensor("ones_r", [1, 128], F32))
    ones_c = e(nc.sbuf_tensor("ones_c", [128, 1], F32))
    mu = e(nc.sbuf_tensor("mu", [128, 4], F32))
    idt = e(nc.sbuf_tensor("idt", [128, 128], BF16))
    idtf = e(nc.sbuf_tensor("idtf", [128, 128], F32))
    stats = e(nc.sbuf_tensor("stats", [128, 72], F32))
    dump = e(nc.sbuf_tensor("dump", [128, 64], F32))

    from types import SimpleNamespace

    def mkscope():
        return ExitStack()

    _alloc_n = [0]

    def alloc(scope, shape, dt):
        _alloc_n[0] += 1
        t = scope.enter_context(
            nc.sbuf_tensor(f"t{_alloc_n[0]}", [128] + list(shape), dt))
        return t[:]

    def barrier(done=()):
        vals = {}
        for eng in ("sync", "vector", "scalar", "tensor", "gpsimd"):
            vals[eng] = b.emit(eng, lambda E: E.nop(), tick=f"ph_{eng}")
        for eng in ("sync", "vector", "scalar", "tensor", "gpsimd"):
            for oeng, v in vals.items():
                if oeng != eng:
                    b.wait(eng, f"ph_{oeng}", v)
            for nm, v in done:
                b.wait(eng, nm, v)

    ps = [e(nc.psum_tensor(f"psbank{i}", [128, 512], F32)) for i in range(8)]
    psb = [p[:].bitcast(BF16) for p in ps]

    bank_mm = [0] * 8
    bank_ev = [0] * 8

    def mm_group(matmuls, bank):
        b.wait("tensor", f"bev{bank}", bank_ev[bank])
        for m in matmuls[:-1]:
            b.emit("tensor", m)
        v = b.emit("tensor", matmuls[-1], tick=f"bmm{bank}")
        bank_mm[bank] = v
        return v

    def evac(eng, fn, bank):
        b.wait(eng, f"bmm{bank}", bank_mm[bank])
        v = b.emit(eng, fn, tick=f"bev{bank}")
        bank_ev[bank] = v
        return v

    b.dma(cst[:], consts[:], "cld")
    b.dma(idt[:], ident[:], "cld")
    b.dma(idtf[:], identf[:], "cld")
    for eng in ("vector", "scalar", "tensor", "gpsimd"):
        b.wait(eng, "cld", 48)
    b.emit("vector", lambda E: E.memset(ones_r[:], 1.0))
    vo = b.emit("vector", lambda E: E.memset(ones_c[:], 1.0), tick="vones")
    b.wait("tensor", "vones", vo)
    b.emit("vector", lambda E: E.memset(cst[:, 3:4], 0.75))
    b.link("vector")
    b.emit("vector", lambda E: E.tensor_scalar(
        cst[:, 4:5], cst[:, 3:4], C52M, None, ALU.add))
    b.link("vector")
    b.emit("vector", lambda E: E.tensor_scalar(
        cst[:, 4:5], cst[:, 4:5], -C52M, None, ALU.add))
    b.link("vector")
    b.emit("vector", lambda E: E.tensor_scalar(
        cst[:, 5:6], cst[:, 4:5], -1.0, 1.0, ALU.mult, ALU.add))
    b.link("vector")
    b.emit("vector", lambda E: E.tensor_scalar(
        cst[:, 6:7], cst[:, 4:5], -500.0, None, ALU.mult), tick="wcst")

    wq_scope = mkscope()
    awbuf = alloc(wq_scope, [4, D], BF16)
    for (src, dst) in ((wq_f, wq_b), (wk_f, wk_b), (wv_f, wv_b), (wo_f, wo_b)):
        for ch in range(2):
            seg = src[ch * 512:(ch + 1) * 512, :].rearrange("(c p) f -> p c f", p=128)
            lv = b.dma(awbuf, seg, "awld", eng="gpsimd")
            b.wait("sync", "awld", lv)
            dseg = dst[ch * 512:(ch + 1) * 512, :].rearrange("(c p) f -> p c f", p=128)
            sv = b.dma(dseg, awbuf, "awst")
            b.wait("gpsimd", "awst", sv)

    wqin8 = alloc(wq_scope, [8, INNER], F32)
    wqin32 = wqin8.rearrange("p a b -> p (a b)").rearrange(
        "p (a b) -> p a b", a=32)
    wqv = alloc(wq_scope, [INNER], F32)
    wqq0 = alloc(wq_scope, [INNER], I16)
    wqr = alloc(wq_scope, [2, INNER], BF16)
    wqhi = alloc(wq_scope, [INNER], F32)
    wqleaf = alloc(wq_scope, [1024], F32)

    wq_free = [None, None]
    st_vals = [None, None]

    def wquant(w_src, w_dst, nrow, ncol, mucol):
        nch = nrow // 128
        wqin = wqin8 if nch == 8 else wqin32
        hch = nch // 2
        b.wait("vector", "bmm0", bank_mm[0])
        b.wait("vector", "bmm1", bank_mm[1])
        b.wait("sync", "awst", b.semval("awst"))
        MC = float((1 << 23) + (1 << 22))
        nl = ncol // 128
        lvh = [0, 0]
        for hf in range(2):
            if wq_free[hf] is not None:
                b.wait("sync", *wq_free[hf])
            lvh[hf] = b.dma(
                wqin[:, hf * hch:(hf + 1) * hch, 0:ncol],
                w_src[hf * 128 * hch:(hf + 1) * 128 * hch, :].rearrange(
                    "(c p) f -> p c f", p=128), f"wqld{hf}")
        b.emit("vector", lambda E: E.memset(stats[:, 43:44], 0.0))
        b.link("vector")
        for c in range(nch):
            if c % hch == 0:
                b.wait("vector", f"wqld{c // hch}", lvh[c // hch])
                b.wait("scalar", f"wqld{c // hch}", lvh[c // hch])
            b.wait("scalar", "wq_v", b.semval("wq_v"))
            b.emit("scalar", lambda E, c=c, n=ncol: E.activation(
                wqv[:, 0:n], wqin[:, c, 0:n], AF.Abs), tick="wq_s")
            b.wait("vector", "wq_s", b.semval("wq_s"))
            b.emit("vector", lambda E, n=ncol: E.tensor_scalar(
                wqhi[:, 0:n], wqv[:, 0:n], float(2 ** 20), MC, ALU.mult,
                ALU.add))
            b.link("vector")
            b.emit("vector", lambda E, n=ncol: E.tensor_scalar(
                wqhi[:, 0:n], wqhi[:, 0:n], -MC, None, ALU.add))
            b.link("vector")
            b.emit("vector", lambda E, n=ncol: E.scalar_tensor_tensor(
                wqv[:, 0:n], wqv[:, 0:n], float(2 ** 20), wqhi[:, 0:n],
                ALU.mult, ALU.subtract))
            b.link("vector")
            b.emit("vector", lambda E, n=ncol: E.tensor_reduce(
                stats[:, 42:43], wqv[:, 0:n], AX.X, ALU.add))
            b.link("vector")
            b.emit("vector", lambda E: E.tensor_add(
                stats[:, 43:44], stats[:, 43:44], stats[:, 42:43]))
            b.link("vector")
            b.emit("vector", lambda E, c=c, n=ncol, nl=nl: E.tensor_reduce(
                wqleaf[:, c * nl:(c + 1) * nl],
                wqhi[:, 0:n].rearrange("p (a x) -> p a x", x=128),
                AX.X, ALU.add), tick="wq_v")
        nlf = nch * nl
        b.wait("vector", "wq_v", b.semval("wq_v"))
        b.emit("vector", lambda E, m=nlf: E.tensor_scalar(
            wqhi[:, 0:m], wqleaf[:, 0:m], float(2 ** -10), MC, ALU.mult,
            ALU.add))
        b.link("vector")
        b.emit("vector", lambda E, m=nlf: E.tensor_scalar(
            wqhi[:, 0:m], wqhi[:, 0:m], -MC, None, ALU.add))
        b.link("vector")
        b.emit("vector", lambda E, m=nlf: E.scalar_tensor_tensor(
            wqleaf[:, 0:m], wqhi[:, 0:m], float(-(2 ** 10)), wqleaf[:, 0:m],
            ALU.mult, ALU.add))
        b.link("vector")
        b.emit("vector", lambda E, m=nlf: E.tensor_reduce(
            stats[:, 36:37], wqhi[:, 0:m], AX.X, ALU.add))
        b.emit("vector", lambda E, m=nlf: E.tensor_reduce(
            stats[:, 37:38], wqleaf[:, 0:m], AX.X, ALU.add))
        b.link("vector")
        b.emit("vector", lambda E: E.tensor_scalar(
            stats[:, 38:39], stats[:, 36:37], float(2 ** -9), MC, ALU.mult,
            ALU.add))
        b.emit("vector", lambda E: E.tensor_scalar(
            stats[:, 40:41], stats[:, 37:38], float(2 ** -5), MC, ALU.mult,
            ALU.add))
        b.link("vector")
        b.emit("vector", lambda E: E.tensor_scalar(
            stats[:, 38:39], stats[:, 38:39], -MC, None, ALU.add))
        b.emit("vector", lambda E: E.tensor_scalar(
            stats[:, 40:41], stats[:, 40:41], -MC, None, ALU.add))
        b.link("vector")
        b.emit("vector", lambda E: E.scalar_tensor_tensor(
            stats[:, 39:40], stats[:, 38:39], float(-(2 ** 9)),
            stats[:, 36:37], ALU.mult, ALU.add))
        b.emit("vector", lambda E: E.scalar_tensor_tensor(
            stats[:, 41:42], stats[:, 40:41], float(-(2 ** 5)),
            stats[:, 37:38], ALU.mult, ALU.add), tick="wq_v")
        b.wait("tensor", "wq_v", b.semval("wq_v"))
        mm_group([lambda E: E.matmul(ps[0][0:1, 0:6], ones_c[:, 0:1],
                                     stats[:, 38:44], start=True,
                                     stop=True)], 0)
        evac("vector", lambda E: E.tensor_copy(stats[0:1, 38:44],
                                               ps[0][0:1, 0:6]), 0)
        b.link("vector")
        b.emit("vector", lambda E: E.scalar_tensor_tensor(
            stats[0:1, 38:39], stats[0:1, 38:39], float(2 ** 9),
            stats[0:1, 39:40], ALU.mult, ALU.add))
        b.emit("vector", lambda E: E.scalar_tensor_tensor(
            stats[0:1, 40:41], stats[0:1, 40:41], float(2 ** 5),
            stats[0:1, 41:42], ALU.mult, ALU.add))
        b.link("vector")
        b.emit("vector", lambda E: E.scalar_tensor_tensor(
            stats[0:1, 38:39], stats[0:1, 38:39], float(2 ** 10),
            stats[0:1, 40:41], ALU.mult, ALU.add))
        b.link("vector")
        b.emit("vector", lambda E: E.tensor_add(
            stats[0:1, 38:39], stats[0:1, 38:39], stats[0:1, 43:44]))
        b.link("vector")
        b.emit("vector", lambda E, n=nrow * ncol: E.tensor_scalar(
            stats[0:1, 33:34], stats[0:1, 38:39], float(2.0 ** -20 / n), EPS,
            ALU.mult, ALU.max), tick="wqv")
        b.wait("tensor", "wqv", b.semval("wqv"))
        b.wait("tensor", "bev0", bank_ev[0])
        mm_group([lambda E: E.matmul(ps[1][:, 0:1], ones_r[0:1, :],
                                     stats[0:1, 33:34], start=True,
                                     stop=True)], 1)
        evac("vector", lambda E, mc=mucol: E.tensor_copy(
            mu[:, mc:mc + 1], ps[1][:, 0:1]), 1)
        b.link("vector")
        b.emit("vector", lambda E, mc=mucol: E.reciprocal(
            stats[:, 34:35], mu[:, mc:mc + 1]))
        b.link("vector")
        b.emit("vector", lambda E, mc=mucol: E.scalar_tensor_tensor(
            stats[:, 35:36], stats[:, 34:35], -1.0, mu[:, mc:mc + 1],
            ALU.mult, ALU.mult))
        b.link("vector")
        b.emit("vector", lambda E: E.tensor_scalar(
            stats[:, 35:36], stats[:, 35:36], 2.0, None, ALU.add))
        b.link("vector")
        b.emit("vector", lambda E: E.tensor_mul(
            stats[:, 34:35], stats[:, 34:35], stats[:, 35:36]))
        b.link("vector")
        for c in range(nch):
            b.emit("vector", lambda E, c=c, n=ncol: E.tensor_scalar(
                wqv[:, 0:n], wqin[:, c, 0:n], stats[:, 34:35], None,
                ALU.mult), tick="wq_v")
            b.wait("scalar", "wq_v", b.semval("wq_v"))
            b.emit("scalar", lambda E, n=ncol: E.activation(
                wqq0[:, 0:n], wqv[:, 0:n], AF.Copy, bias=256.75, scale=1.0),
                tick="wq_s")
            b.wait("vector", "wq_s", b.semval("wq_s"))
            b.emit("vector", lambda E, n=ncol: E.scalar_tensor_tensor(
                wqv[:, 0:n], wqq0[:, 0:n], -256.0, wqv[:, 0:n], ALU.add,
                ALU.subtract))
            b.link("vector")
            b.emit("vector", lambda E, n=ncol: E.scalar_tensor_tensor(
                wqv[:, 0:n], wqv[:, 0:n], 0.5, wqq0[:, 0:n], ALU.is_le,
                ALU.add))
            b.link("vector")
            if st_vals[c % 2] is not None:
                b.wait("vector", f"wq_st{c % 2}", st_vals[c % 2])
            b.emit("vector", lambda E, c=c, n=ncol: E.tensor_scalar(
                wqr[:, c % 2, 0:n], wqv[:, 0:n], -257.0, 1.0, ALU.add,
                ALU.min))
            b.link("vector")
            b.emit("vector", lambda E, c=c, n=ncol: E.tensor_scalar(
                wqr[:, c % 2, 0:n], wqr[:, c % 2, 0:n], -1.0, None,
                ALU.max), tick="wq_v")
            b.wait("sync", "wq_v", b.semval("wq_v"))
            st_vals[c % 2] = b.dma(w_dst[c * 128:(c + 1) * 128, 0:ncol],
                                   wqr[:, c % 2, 0:ncol], f"wq_st{c % 2}")
            if c == hch - 1 or c == nch - 1:
                wq_free[c // hch] = ("wq_v", b.semval("wq_v"))

    wquant(wgT, wg_q, D, INNER, 0)
    wquant(wuT, wu_q, D, INNER, 1)
    wquant(wdT, wd_q, INNER, D, 2)
    barrier(done=[("wq_st0", b.semval("wq_st0")),
                  ("wq_st1", b.semval("wq_st1")),
                  ("awst", b.semval("awst"))])
    wq_scope.close()

    def ffn_bufs(scope):
        Bf = SimpleNamespace()
        Bf.xt = [alloc(scope, [TC, D], F32), alloc(scope, [TC, D], F32)]
        Bf.vql = alloc(scope, [TC, D], F32)
        Bf.q0x = alloc(scope, [D], I16)
        Bf.xq_fm = alloc(scope, [8, TT], BF16)
        Bf.gu = alloc(scope, [TC, INNER], F32)
        Bf.hv = alloc(scope, [INNER], F32)
        Bf.hq_fm = alloc(scope, [32, TT], BF16)
        Bf.wbuf = alloc(scope, [2, 8, 512], BF16)
        Bf.wdbuf = alloc(scope, [2, D], BF16)
        return Bf

    def ffn_sublayer(Bf, x_src, x_dst, tag):
        xt, vql, q0x, xq_fm = Bf.xt, Bf.vql, Bf.q0x, Bf.xq_fm
        gu, hv, hq_fm, wbuf, wdbuf = (Bf.gu, Bf.hv, Bf.hq_fm, Bf.wbuf,
                                      Bf.wdbuf)
        wfree_state = [None, None]
        lad_guard = [None]
        xqT_done = [None]

        def stage1(it):
            p = it % 2
            so = 16 * p
            qo = 32 + 16 * p
            X = xt[p]
            if it >= 2:
                b.wait("sync", f"{tag}stx{p}", (it // 2) * 16)
            if lad_guard[0] is not None:
                b.wait("sync", f"{tag}v", lad_guard[0])
            src = x_src[it * TT:(it + 1) * TT, :].rearrange(
                "(c p) f -> p c f", p=128)
            lv = b.dma(X, src, f"{tag}ldx{p}")
            b.wait("vector", f"{tag}ldx{p}", lv)
            b.wait("scalar", f"{tag}ldx{p}", lv)
            b.wait("scalar", f"{tag}v", b.semval(f"{tag}v"))
            for tc in range(TC):
                b.emit("vector", lambda E, tc=tc, X=X, so=so: E.tensor_reduce(
                    stats[:, so + tc:so + tc + 1], X[:, tc], AX.X, ALU.add))
                b.emit("vector", lambda E, tc=tc, X=X, so=so: E.tensor_reduce(
                    stats[:, so + 8 + tc:so + 9 + tc], X[:, tc], AX.X, ALU.max))
                b.emit("vector", lambda E, tc=tc, X=X, so=so: E.tensor_reduce(
                    stats[:, so + 12 + tc:so + 13 + tc], X[:, tc], AX.X,
                    ALU.min))
            b.link("vector")
            for tc in range(TC):
                if tc > 0:
                    b.link("scalar")
                b.emit("scalar", lambda E, tc=tc, X=X, so=so: E.activation(
                    vql[:, tc], X[:, tc], AF.Square,
                    accum_out=stats[:, so + 4 + tc:so + 5 + tc]),
                    tick=f"{tag}s")
            ssq = b.semval(f"{tag}s")
            mcol = stats[:, so:so + 4]
            vcol = stats[:, so + 4:so + 8]
            mxc = stats[:, so + 8:so + 12]
            mnc = stats[:, so + 12:so + 16]
            shc = stats[:, qo:qo + 4]
            dqg = stats[:, qo + 4:qo + 8]
            dqu = stats[:, qo + 8:qo + 12]
            dqd = stats[:, 64 + 4 * p:68 + 4 * p]
            b.wait("vector", f"{tag}s", ssq)
            b.emit("vector", lambda E, m=mcol: E.tensor_scalar(
                m, m, 1.0 / D, None, ALU.mult))
            b.link("vector")
            b.emit("vector", lambda E, v=vcol: E.tensor_scalar(
                v, v, 1.0 / D, None, ALU.mult))
            b.link("vector")
            b.emit("vector", lambda E, m=mcol: E.scalar_tensor_tensor(
                dump[:, 0:4], m, 1.0, m, ALU.bypass, ALU.mult))
            b.link("vector")
            b.emit("vector", lambda E, v=vcol: E.tensor_sub(
                v, v, dump[:, 0:4]), tick=f"{tag}v")
            b.wait("scalar", f"{tag}v", b.semval(f"{tag}v"))
            b.emit("scalar", lambda E, v=vcol: E.activation(
                dump[:, 4:8], v, AF.Sqrt, bias=EPS, scale=1.0),
                tick=f"{tag}s")
            b.wait("vector", f"{tag}s", b.semval(f"{tag}s"))
            b.emit("vector", lambda E: E.reciprocal(dump[:, 8:12],
                                                    dump[:, 4:8]))
            b.link("vector")
            b.emit("vector", lambda E, v=vcol: E.tensor_scalar(
                dump[:, 12:16], v, 1.0, EPS, ALU.mult, ALU.add))
            b.link("vector")
            b.emit("vector", lambda E: E.scalar_tensor_tensor(
                dump[:, 16:20], dump[:, 8:12], 1.0, dump[:, 8:12],
                ALU.bypass, ALU.mult))
            b.link("vector")
            b.emit("vector", lambda E: E.tensor_mul(
                dump[:, 16:20], dump[:, 16:20], dump[:, 12:16]))
            b.link("vector")
            b.emit("vector", lambda E: E.tensor_scalar(
                dump[:, 16:20], dump[:, 16:20], -0.5, 1.5, ALU.mult,
                ALU.add))
            b.link("vector")
            b.emit("vector", lambda E, v=vcol: E.tensor_mul(
                v, dump[:, 8:12], dump[:, 16:20]))
            b.link("vector")
            b.emit("vector", lambda E, mx=mxc, m=mcol: E.tensor_sub(mx, mx, m))
            b.link("vector")
            b.emit("vector", lambda E, mn=mnc, m=mcol: E.tensor_sub(mn, m, mn))
            b.link("vector")
            b.emit("vector", lambda E, mx=mxc, mn=mnc: E.tensor_max(mx, mx, mn))
            b.link("vector")
            b.emit("vector", lambda E, mx=mxc, v=vcol: E.tensor_mul(mx, mx, v))
            b.link("vector")
            b.emit("vector", lambda E, mx=mxc: E.tensor_scalar(
                mx, mx, EPS, None, ALU.max))
            b.link("vector")
            b.emit("vector", lambda E, d=dqg, mx=mxc: E.scalar_tensor_tensor(
                d, mx, 1.0 / 127.0, mu[:, 0:1].broadcast_to([128, 4]),
                ALU.mult, ALU.mult))
            b.emit("vector", lambda E, d=dqu, mx=mxc: E.scalar_tensor_tensor(
                d, mx, 1.0 / 127.0, mu[:, 1:2].broadcast_to([128, 4]),
                ALU.mult, ALU.mult))
            b.link("vector")
            b.emit("vector", lambda E, mx=mxc: E.reciprocal(mx, mx))
            b.link("vector")
            b.emit("vector", lambda E, mx=mxc: E.tensor_scalar(
                mx, mx, 127.0, None, ALU.mult))
            b.link("vector")
            b.emit("vector", lambda E, mx=mxc, v=vcol: E.tensor_mul(
                mx, mx, v))
            b.link("vector")
            b.emit("vector", lambda E, mn=mnc, m=mcol, mx=mxc:
                   E.scalar_tensor_tensor(mn, m, -1.0, mx, ALU.mult,
                                          ALU.mult), tick=f"{tag}v")
            vAB = b.semval(f"{tag}v")
            if xqT_done[0] is not None:
                b.wait("vector", *xqT_done[0])
            for tc in range(TC):
                if tc == 0:
                    b.wait("vector", f"{tag}v", vAB)
                b.emit("vector", lambda E, tc=tc, X=X, mx=mxc, mn=mnc:
                       E.tensor_scalar(vql[:, tc], X[:, tc],
                                       mx[:, tc:tc + 1],
                                       mn[:, tc:tc + 1], ALU.mult, ALU.add),
                       tick=f"{tag}v")
                b.wait("scalar", f"{tag}v", b.semval(f"{tag}v"))
                b.emit("scalar", lambda E, tc=tc: E.activation(
                    q0x[:], vql[:, tc], AF.Copy, bias=256.75, scale=1.0),
                    tick=f"{tag}s")
                b.wait("vector", f"{tag}s", b.semval(f"{tag}s"))
                b.emit("vector", lambda E, tc=tc: E.scalar_tensor_tensor(
                    vql[:, tc], q0x[:], -256.0, vql[:, tc], ALU.add,
                    ALU.subtract))
                b.link("vector")
                b.emit("vector", lambda E, tc=tc: E.scalar_tensor_tensor(
                    vql[:, tc], vql[:, tc], 0.5, q0x[:], ALU.is_le,
                    ALU.add))
                b.link("vector")
                b.emit("vector", lambda E, tc=tc: E.tensor_scalar(
                    vql[:, tc], vql[:, tc], -257.0, None, ALU.add),
                    tick=f"{tag}v")
            return b.semval(f"{tag}v")

        def evac_cb(it, tdown):
            p = it % 2
            X = xt[p]
            dqd = stats[:, 64 + 4 * p:68 + 4 * p]
            b.wait("vector", f"{tag}dmm", tdown)
            for dc in range(2):
                for tc in range(TC):
                    bk = dc * 4 + tc
                    bank_ev[bk] = b.emit(
                        "vector", lambda E, dc=dc, tc=tc, bk=bk, X=X, d=dqd:
                        E.scalar_tensor_tensor(
                            X[:, tc, dc * 512:(dc + 1) * 512], ps[bk][:],
                            d[:, tc:tc + 1],
                            X[:, tc, dc * 512:(dc + 1) * 512],
                            ALU.mult, ALU.add), tick=f"bev{bk}")
            b.wait("sync", f"bev{7}", bank_ev[7])
            dstv = x_dst[it * TT:(it + 1) * TT, :].rearrange(
                "(c p) f -> p c f", p=128)
            b.dma(dstv, X, f"{tag}stx{p}")

        def stage2(it, chain_v):
            p = it % 2
            so = 16 * p
            qo = 32 + 16 * p
            X = xt[p]
            shc = stats[:, qo:qo + 4]
            dqg = stats[:, qo + 4:qo + 8]
            dqu = stats[:, qo + 8:qo + 12]
            dqd = stats[:, 64 + 4 * p:68 + 4 * p]
            xq_ev = []
            b.wait("tensor", f"{tag}v", chain_v)
            for tc in range(TC):
                for g2 in range(2):
                    tb = 4 + g2
                    b.wait("tensor", f"bev{tb}", bank_ev[tb])
                    for j in range(4):
                        ic = g2 * 4 + j
                        b.emit("tensor", lambda E, tc=tc, ic=ic, j=j, tb=tb:
                               E.transpose(
                                   ps[tb][:, j * 128:(j + 1) * 128],
                                   vql[:, tc, ic * 128:(ic + 1) * 128],
                                   idtf[:]))
                    bank_mm[tb] = b.tick_last("tensor", f"bmm{tb}")
                    ev = evac("scalar", lambda E, tc=tc, g2=g2, tb=tb: E.copy(
                        xq_fm[:, g2 * 4:(g2 + 1) * 4,
                              tc * 128:(tc + 1) * 128],
                        ps[tb][:, 0:512].rearrange("p (a t) -> p a t", a=4)),
                        tb)
                xq_ev.append((f"bmm{4 + 1}", bank_mm[5]))
                xq_ev.append((f"bev{tb}", ev))
            xqT_done[0] = (f"bmm{5}", bank_mm[5])
            wfree = wfree_state
            silu_last = None
            up_last = None
            for phase in range(2):
                wsrc = wg_q if phase == 0 else wu_q
                if phase == 1:
                    b.wait("vector", *silu_last)
                for oc in range(8):
                    wp = (phase * 8 + oc) % 2
                    if wfree[wp] is not None:
                        b.wait("sync", *wfree[wp])
                    wl = b.dma(wbuf[:, wp],
                               wsrc[:, oc * 512:(oc + 1) * 512].rearrange(
                                   "(c p) f -> p c f", p=128), f"{tag}wld{wp}")
                    b.wait("tensor", f"{tag}wld{wp}", wl)
                    if phase == 0 and oc == 0:
                        b.wait("tensor", *xq_ev[-1])
                        b.wait("tensor", *xq_ev[-2])
                    for tc in range(TC):
                        bk = ((phase * 8 + oc) % 2) * 2 + (tc % 2)
                        mms = [
                            (lambda E, ic=ic, tc=tc, wp=wp, bk=bk: E.matmul(
                                ps[bk][:],
                                xq_fm[:, ic, tc * 128:(tc + 1) * 128],
                                wbuf[:, wp, ic], start=(ic == 0),
                                stop=(ic == 7)))
                            for ic in range(8)]
                        mm_group(mms, bk)
                        if phase == 0:
                            evac("scalar",
                                 lambda E, tc=tc, oc=oc, bk=bk, d=dqg:
                                 E.activation(
                                     gu[:, tc, oc * 512:(oc + 1) * 512],
                                     ps[bk][:], AF.Silu,
                                     scale=d[:, tc:tc + 1]), bk)
                        else:
                            evac("vector",
                                 lambda E, tc=tc, oc=oc, bk=bk, d=dqu:
                                 E.scalar_tensor_tensor(
                                     gu[:, tc, oc * 512:(oc + 1) * 512],
                                     ps[bk][:], d[:, tc:tc + 1],
                                     gu[:, tc, oc * 512:(oc + 1) * 512],
                                     ALU.mult, ALU.mult), bk)
                    bklast = ((phase * 8 + oc) % 2) * 2 + 1
                    wfree[wp] = (f"bmm{bklast}", bank_mm[bklast])
                    if phase == 0:
                        silu_last = (f"bev{bklast}", bank_ev[bklast])
                    else:
                        up_last = (f"bev{bklast}", bank_ev[bklast])
            b.wait("vector", *silu_last)
            b.wait("vector", *up_last)
            for tc in range(TC):
                b.emit("vector", lambda E, tc=tc, s=shc: E.tensor_reduce(
                    s[:, tc:tc + 1], gu[:, tc], AX.X, ALU.max,
                    apply_absolute_value=True))
            b.link("vector")
            b.emit("vector", lambda E, s=shc: E.tensor_scalar(
                s, s, EPS, None, ALU.max))
            b.link("vector")
            b.emit("vector", lambda E, d=dqd, s=shc: E.scalar_tensor_tensor(
                d, s, 0.5 / 127.0, mu[:, 2:3].broadcast_to([128, 4]),
                ALU.mult, ALU.mult))
            b.link("vector")
            b.emit("vector", lambda E, s=shc: E.reciprocal(s, s))
            b.link("vector")
            b.emit("vector", lambda E, s=shc: E.tensor_scalar(
                s, s, 127.0, None, ALU.mult), tick=f"{tag}v")
            vsh = b.semval(f"{tag}v")
            scr = xt[1 - p].rearrange("p a d -> p (a d)")
            if it >= 1:
                b.wait("scalar", f"{tag}stx{1 - p}",
                       ((it - 1) // 2 + 1) * 16)
                b.wait("scalar", "bev7", b.semval("bev7"))
            hq_ev = []
            prevG = None
            for tc in range(TC):
                b.wait("scalar", f"{tag}v", vsh)
                if prevG is not None:
                    b.wait("scalar", f"{tag}v", prevG)
                b.emit("scalar", lambda E, tc=tc, s=shc, scr=scr:
                       E.activation(scr, gu[:, tc], AF.Copy, bias=500.0,
                                    scale=s[:, tc:tc + 1]), tick=f"{tag}s")
                bB = b.semval(f"{tag}s")
                b.wait("vector", f"{tag}v", vsh)
                b.emit("vector", lambda E, tc=tc, s=shc: E.tensor_scalar(
                    hv[:], gu[:, tc], s[:, tc:tc + 1], C32M, ALU.mult,
                    ALU.add))
                b.link("vector")
                b.wait("vector", f"{tag}s", bB)
                b.emit("vector", lambda E, tc=tc: E.tensor_scalar(
                    gu[:, tc], hv[:], -C32M, None, ALU.add), tick=f"{tag}v")
                vC = b.semval(f"{tag}v")
                b.wait("scalar", f"{tag}v", vC)
                b.emit("scalar", lambda E, scr=scr: E.activation(
                    hv[:], scr, AF.Copy, bias=C52M, scale=1.0))
                b.link("scalar")
                b.emit("scalar", lambda E, scr=scr: E.activation(
                    scr, hv[:], AF.Copy, bias=-C52M, scale=1.0),
                    tick=f"{tag}s")
                eE = b.semval(f"{tag}s")
                b.link("vector")
                b.emit("vector", lambda E, tc=tc: E.tensor_scalar(
                    gu[:, tc], gu[:, tc], cst[:, 5:6], cst[:, 6:7],
                    ALU.mult, ALU.add))
                b.link("vector")
                b.wait("vector", f"{tag}s", eE)
                b.emit("vector", lambda E, tc=tc, scr=scr:
                       E.scalar_tensor_tensor(
                           gu[:, tc], scr, cst[:, 4:5], gu[:, tc],
                           ALU.mult, ALU.add), tick=f"{tag}v")
                prevG = b.semval(f"{tag}v")
                b.wait("tensor", f"{tag}v", b.semval(f"{tag}v"))
                for g in range(8):
                    tb = 4 + (g % 2)
                    b.wait("tensor", f"bev{tb}", bank_ev[tb])
                    for j in range(4):
                        oc = g * 4 + j
                        b.emit("tensor", lambda E, tc=tc, oc=oc, j=j, tb=tb:
                               E.transpose(
                                   ps[tb][:, j * 128:(j + 1) * 128],
                                   gu[:, tc, oc * 128:(oc + 1) * 128],
                                   idtf[:]))
                    bank_mm[tb] = b.tick_last("tensor", f"bmm{tb}")
                    ev = evac("vector",
                              lambda E, tc=tc, g=g, tb=tb: E.tensor_copy(
                                  hq_fm[:, g * 4:(g + 1) * 4,
                                        tc * 128:(tc + 1) * 128],
                                  ps[tb][:, 0:512].rearrange(
                                      "p (a t) -> p a t", a=4)), tb)
                    hq_ev.append((f"bev{tb}", ev))
                hq_ev.append((f"bmm{4}", bank_mm[4]))
                hq_ev.append((f"bmm{5}", bank_mm[5]))
            lad_guard[0] = b.semval(f"{tag}v")
            b.wait("tensor", *hq_ev[-1])
            b.wait("tensor", *hq_ev[-2])
            for bk in range(8):
                b.wait("tensor", f"bev{bk}", bank_ev[bk])
            for oc in range(32):
                wp = oc % 2
                if oc >= 2:
                    b.wait("sync", f"{tag}dmm", b.semval(f"{tag}dmm") - 1)
                wl = b.dma(wdbuf[:, wp], wd_q[oc * 128:(oc + 1) * 128, :],
                           f"{tag}dld{wp}")
                b.wait("tensor", f"{tag}dld{wp}", wl)
                for dc in range(2):
                    for tc in range(TC):
                        bk = dc * 4 + tc
                        b.emit("tensor",
                               lambda E, oc=oc, dc=dc, tc=tc, wp=wp, bk=bk:
                               E.matmul(ps[bk][:],
                                        hq_fm[:, oc, tc * 128:(tc + 1) * 128],
                                        wdbuf[:, wp, dc * 512:(dc + 1) * 512],
                                        start=(oc == 0), stop=(oc == 31)))
                b.tick_last("tensor", f"{tag}dmm")
            return b.semval(f"{tag}dmm")

        pending = None
        for it in range(NTILE):
            chain_v = stage1(it)
            if pending is not None:
                evac_cb(*pending)
            tdown = stage2(it, chain_v)
            pending = (it, tdown)
        evac_cb(*pending)
        return [(f"{tag}stx0", b.semval(f"{tag}stx0")),
                (f"{tag}stx1", b.semval(f"{tag}stx1"))]


    def attn_core_bufs(scope):
        Ba = SimpleNamespace()
        Ba.o_fm = alloc(scope, [8, SEQ], BF16)
        Ba.v_tm = alloc(scope, [16, D], BF16)
        Ba.k_fm = alloc(scope, [8, SEQ], BF16)
        Ba.q_fm = alloc(scope, [8, SEQ], BF16)
        return Ba

    def attn_ln_bufs(scope, vw=None):
        Bl = SimpleNamespace()
        Bl.xn_fm = alloc(scope, [8, SEQ], BF16)
        Bl.x1t_a = alloc(scope, [2, D], F32)
        Bl.xn_tm = alloc(scope, [2, D], F32)
        Bl.qkvw = alloc(scope, [2, 8, 128], BF16)
        Bl.vw_r = alloc(scope, [8, D], BF16) if vw is None else vw
        return Bl

    def attn_head_bufs(scope):
        Bh = SimpleNamespace()
        Bh.p0a = alloc(scope, [SEQ], F32)
        Bh.ppt = [alloc(scope, [SEQ], BF16), alloc(scope, [SEQ], BF16)]
        Bh.sc_B = alloc(scope, [16, NKV], F32)
        Bh.sc_E = alloc(scope, [16, NKV], F32)
        Bh.sc_z = alloc(scope, [16, NKV], F32)
        Bh.sc_m = alloc(scope, [16, NKV], F32)
        Bh.hi16 = alloc(scope, [16, NKV], BF16)
        Bh.lo16 = alloc(scope, [16, NKV], BF16)
        Bh.sc_Mq = alloc(scope, [16, 4], F32)
        Bh.sc_cf = alloc(scope, [16, 4], F32)
        Bh.aughl = alloc(scope, [128], F32)
        Bh.q_aug2 = [alloc(scope, [SEQ], BF16), alloc(scope, [SEQ], BF16)]
        Bh.k_aug2 = [alloc(scope, [SEQ], BF16), alloc(scope, [SEQ], BF16)]
        return Bh

    def attn_wo_bufs(scope, wo=None):
        Bw = SimpleNamespace()
        Bw.x1t_b = alloc(scope, [2, D], F32)
        Bw.wo_r = alloc(scope, [8, D], BF16) if wo is None else wo
        return Bw

    def attn_ln_qkv(Ba, Bl, sq, tag, mix_cb=None):
        o_fm, v_tm, k_fm, q_fm = Ba.o_fm, Ba.v_tm, Ba.k_fm, Ba.q_fm
        xn_fm, x1t_a, xn_tm = Bl.xn_fm, Bl.x1t_a, Bl.xn_tm
        qkvw, vw_r = Bl.qkvw, Bl.vw_r
        row0 = sq * SEQ
        for tc in range(16):
            if mix_cb is not None:
                mix_cb(tc)
            pp = tc % 2
            so = 48 + 4 * pp
            if tc >= 2:
                b.wait("sync", f"{tag}xnf{pp}", b.semval(f"{tag}xnf{pp}"))
            lv = b.dma(x1t_a[:, pp],
                       x1d[row0 + tc * 128:row0 + (tc + 1) * 128, :],
                       f"{tag}lx{pp}")
            b.wait("vector", f"{tag}lx{pp}", lv)
            b.wait("scalar", f"{tag}lx{pp}", lv)
            XA = x1t_a[:, pp]
            b.wait("scalar", f"{tag}v", b.semval(f"{tag}v"))
            b.emit("vector", lambda E, XA=XA, so=so: E.tensor_reduce(
                stats[:, so:so + 1], XA, AX.X, ALU.add), tick=f"{tag}v")
            b.emit("scalar", lambda E, XA=XA, so=so, pp=pp: E.activation(
                xn_tm[:, pp], XA, AF.Square,
                accum_out=stats[:, so + 1:so + 2]), tick=f"{tag}s")
            b.wait("vector", f"{tag}s", b.semval(f"{tag}s"))
            b.wait("vector", f"{tag}v", b.semval(f"{tag}v"))
            b.emit("vector", lambda E, so=so: E.tensor_scalar(
                stats[:, so:so + 2], stats[:, so:so + 2], 1.0 / D, None,
                ALU.mult))
            b.link("vector")
            b.emit("vector", lambda E, so=so: E.scalar_tensor_tensor(
                stats[:, so + 2:so + 3], stats[:, so:so + 1], 1.0,
                stats[:, so:so + 1], ALU.bypass, ALU.mult))
            b.link("vector")
            b.emit("vector", lambda E, so=so: E.tensor_sub(
                stats[:, so + 1:so + 2], stats[:, so + 1:so + 2],
                stats[:, so + 2:so + 3]), tick=f"{tag}v")
            b.wait("scalar", f"{tag}v", b.semval(f"{tag}v"))
            b.emit("scalar", lambda E, so=so: E.activation(
                stats[:, so + 2:so + 3], stats[:, so + 1:so + 2], AF.Sqrt,
                bias=EPS, scale=1.0), tick=f"{tag}s")
            b.wait("vector", f"{tag}s", b.semval(f"{tag}s"))
            b.emit("vector", lambda E, so=so: E.reciprocal(
                stats[:, so + 2:so + 3], stats[:, so + 2:so + 3]))
            b.link("vector")
            b.emit("vector", lambda E, so=so: E.tensor_scalar(
                stats[:, so + 3:so + 4], stats[:, so + 1:so + 2], 1.0, EPS,
                ALU.mult, ALU.add))
            b.link("vector")
            b.emit("vector", lambda E, so=so: E.scalar_tensor_tensor(
                dump[:, 20:21], stats[:, so + 2:so + 3], 1.0,
                stats[:, so + 2:so + 3], ALU.bypass, ALU.mult))
            b.link("vector")
            b.emit("vector", lambda E, so=so: E.tensor_mul(
                dump[:, 20:21], dump[:, 20:21], stats[:, so + 3:so + 4]))
            b.link("vector")
            b.emit("vector", lambda E: E.tensor_scalar(
                dump[:, 20:21], dump[:, 20:21], -0.5, 1.5, ALU.mult,
                ALU.add))
            b.link("vector")
            b.emit("vector", lambda E, so=so: E.tensor_mul(
                stats[:, so + 1:so + 2], stats[:, so + 2:so + 3],
                dump[:, 20:21]))
            b.link("vector")
            b.emit("vector", lambda E, so=so: E.scalar_tensor_tensor(
                stats[:, so:so + 1], stats[:, so:so + 1], -1.0,
                stats[:, so + 1:so + 2], ALU.mult, ALU.mult))
            b.link("vector")
            b.emit("vector", lambda E, XA=XA, pp=pp, so=so: E.tensor_scalar(
                xn_tm[:, pp], XA, stats[:, so + 1:so + 2],
                stats[:, so:so + 1], ALU.mult, ALU.add), tick=f"{tag}v")
            b.wait("tensor", f"{tag}v", b.semval(f"{tag}v"))
            for g2 in range(2):
                tb = 4 + g2
                b.wait("tensor", f"bev{tb}", bank_ev[tb])
                for j in range(4):
                    ic = g2 * 4 + j
                    b.emit("tensor", lambda E, ic=ic, j=j, pp=pp, tb=tb:
                           E.transpose(
                               ps[tb][:, j * 128:(j + 1) * 128],
                               xn_tm[:, pp, ic * 128:(ic + 1) * 128],
                               idtf[:]))
                bank_mm[tb] = b.tick_last("tensor", f"bmm{tb}")
                ev = evac("scalar", lambda E, tc=tc, g2=g2, tb=tb: E.copy(
                    xn_fm[:, g2 * 4:(g2 + 1) * 4, tc * 128:(tc + 1) * 128],
                    ps[tb][:, 0:512].rearrange("p (a t) -> p a t", a=4)), tb)
            b.wait("sync", f"bmm{4}", bank_mm[4])
            b.wait("sync", f"bmm{5}", bank_mm[5])
            b.emit("sync", lambda E: E.nop(), tick=f"{tag}xnf{pp}")
        xn_done = [(f"bev{4}", bank_ev[4]), (f"bev{5}", bank_ev[5])]
        for bk in range(4):
            b.wait("sync", f"bmm{bk}", bank_mm[bk])
        vwl = b.dma(vw_r, wv_b[:, :].rearrange("(c p) f -> p c f", p=128),
                    f"{tag}vwl")
        qk_wfree = {}
        for (wsrcd, dstfm, scl, qt) in ((wq_b, q_fm, 0.125, "q"),
                                        (wk_b, k_fm, 1.0, "k")):
            for dc in range(8):
                wp = dc % 2
                if (qt, wp) in qk_wfree:
                    b.wait("sync", *qk_wfree[(qt, wp)])
                if qt == "k" and dc < 2:
                    b.wait("sync", *qk_wfree[("q", wp)])
                wl = b.dma(qkvw[:, wp],
                           wsrcd[:, dc * 128:(dc + 1) * 128].rearrange(
                               "(c p) f -> p c f", p=128), f"{tag}{qt}wl{wp}")
                b.wait("tensor", f"{tag}{qt}wl{wp}", wl)
                if dc == 0 and qt == "q":
                    for nm, v in xn_done:
                        b.wait("tensor", nm, v)
                for qq in range(4):
                    bk = (dc * 4 + qq) % 4
                    mms = [
                        (lambda E, ic=ic, qq=qq, wp=wp, bk=bk: E.matmul(
                            ps[bk][:], qkvw[:, wp, ic],
                            xn_fm[:, ic, qq * 512:(qq + 1) * 512],
                            start=(ic == 0), stop=(ic == 7)))
                        for ic in range(8)]
                    mm_group(mms, bk)
                    evac("scalar", lambda E, dc=dc, qq=qq, bk=bk, d=dstfm,
                         scl=scl: E.activation(
                             d[:, dc, qq * 512:(qq + 1) * 512], ps[bk][:],
                             AF.Copy, bias=0.0, scale=scl), bk)
                    if qq == 3:
                        qk_wfree[(qt, wp)] = (f"bmm{bk}", bank_mm[bk])
        b.wait("tensor", f"{tag}vwl", vwl)
        for tc in range(16):
            for half in range(2):
                bk = (tc * 2 + half) % 4
                mms = [
                    (lambda E, ic=ic, tc=tc, half=half, bk=bk: E.matmul(
                        ps[bk][:], xn_fm[:, ic, tc * 128:(tc + 1) * 128],
                        vw_r[:, ic, half * 512:(half + 1) * 512],
                        start=(ic == 0), stop=(ic == 7)))
                    for ic in range(8)]
                mm_group(mms, bk)
                evac("scalar", lambda E, tc=tc, half=half, bk=bk: E.copy(
                    v_tm[:, tc, half * 512:(half + 1) * 512], ps[bk][:]), bk)
        vdone = [(f"bev{i}", bank_ev[i]) for i in range(4)]
        return vdone

    def attn_heads(Ba, Bh, sq, tag, vdone):
        o_fm, v_tm, k_fm, q_fm = Ba.o_fm, Ba.v_tm, Ba.k_fm, Ba.q_fm
        p0a, ppt = Bh.p0a, Bh.ppt
        sc_B, sc_E, sc_z, sc_m = Bh.sc_B, Bh.sc_E, Bh.sc_z, Bh.sc_m
        sc_E3 = Bh.sc_E.rearrange("p a b -> p (a b)").rearrange(
            "p (a b c) -> p a b c", a=16, b=4)
        hi16, lo16 = Bh.hi16, Bh.lo16
        sc_Mq, sc_cf, aughl = Bh.sc_Mq, Bh.sc_cf, Bh.aughl
        q_aug2, k_aug2 = Bh.q_aug2, Bh.k_aug2
        row0 = sq * SEQ
        b.dma(k_aug2[0][64:128, :], indic[:], f"{tag}ind")
        b.dma(k_aug2[1][0:64, :], indic[:], f"{tag}ind")
        pB_vals = {}
        augv = {}
        augT_prev = [None]

        def prologue(h):
            pi = h % 2
            ch = h // 2
            qof = 64 * pi
            ka, qa = k_aug2[pi], q_aug2[pi]
            if h >= 2:
                b.wait("vector", f"{tag}pB", pB_vals[h - 2])
            if h == 0:
                for nm, v in vdone:
                    b.wait("vector", nm, v)
            b.emit("vector", lambda E, ch=ch, qof=qof, ka=ka: E.tensor_copy(
                ka[qof:qof + 64, :], k_fm[qof:qof + 64, ch, :]))
            b.emit("vector", lambda E, ch=ch, qof=qof, qa=qa: E.tensor_copy(
                qa[qof:qof + 64, :], q_fm[qof:qof + 64, ch, :]),
                tick=f"{tag}aug")
            augv[h] = b.semval(f"{tag}aug")

        def passA_mm_max(h, qc, kq):
            pi = h % 2
            ch = h // 2
            qof = 64 * pi
            bk = kq % 2
            mm_group([
                (lambda E, qc=qc, kq=kq, bk=bk, ch=ch, qof=qof:
                 E.matmul(ps[bk][:],
                          q_fm[qof:qof + 64, ch,
                               qc * 128:(qc + 1) * 128],
                          k_fm[qof:qof + 64, ch,
                               kq * 512:(kq + 1) * 512],
                          start=True, stop=True,
                          tile_position=(qof, 0)))], bk)
            b.wait("vector", f"bmm{bk}", bank_mm[bk])
            b.emit("vector", lambda E, qc=qc, kq=kq, bk=bk:
                   E.tensor_reduce(
                       sc_B[:, qc, kq * 8:(kq + 1) * 8],
                       ps[bk][:].rearrange("p (a x) -> p a x", a=8),
                       AX.X, ALU.max))
            b.link("vector")
            b.emit("vector", lambda E, qc=qc, kq=kq:
                   E.tensor_reduce(sc_Mq[:, qc, kq:kq + 1],
                                   sc_B[:, qc, 0:(kq + 1) * 8], AX.X,
                                   ALU.max), tick=f"{tag}v")
            nmcol = 56 + bk
            b.wait("scalar", f"{tag}v", b.semval(f"{tag}v"))
            b.emit("scalar", lambda E, qc=qc, kq=kq, nmcol=nmcol:
                   E.activation(stats[:, nmcol:nmcol + 1],
                                sc_Mq[:, qc, kq:kq + 1], AF.Copy,
                                bias=0.0, scale=-1.0))
            b.link("scalar")
            ev = evac("scalar", lambda E, kq=kq, bk=bk, nmcol=nmcol:
                      E.activation(p0a[:, kq * 512:(kq + 1) * 512],
                                   ps[bk][:], AF.Exp,
                                   bias=stats[:, nmcol:nmcol + 1],
                                   scale=1.0), bk)
            return ev

        def passA_sum(qc, kq, ev):
            bk = kq % 2
            b.wait("vector", f"bev{bk}", ev)
            b.emit("vector", lambda E, qc=qc, kq=kq: E.tensor_reduce(
                sc_E[:, qc, kq * 8:(kq + 1) * 8],
                p0a[:, kq * 512:(kq + 1) * 512].rearrange(
                    "p (a x) -> p a x", a=8), AX.X, ALU.add),
                tick=f"{tag}v")

        def smalls_augT(h, tail_cb=None):
            pi = h % 2
            aof = 64 * (1 - pi)
            qa = q_aug2[pi]
            if tail_cb is not None:
                tail_cb()
            vA = b.semval(f"{tag}v")
            b.wait("vector", f"{tag}v", vA)
            b.emit("vector", lambda E: E.tensor_sub(
                sc_cf[:], sc_Mq[:],
                sc_Mq[:, :, 3:4].broadcast_to([128, 16, 4])), tick=f"{tag}v")
            b.wait("scalar", f"{tag}v", b.semval(f"{tag}v"))
            b.emit("scalar", lambda E: E.activation(
                sc_cf[:], sc_cf[:], AF.Exp), tick=f"{tag}s")
            b.wait("vector", f"{tag}s", b.semval(f"{tag}s"))
            b.emit("vector", lambda E: E.tensor_mul(
                sc_E3[:], sc_E3[:],
                sc_cf[:].rearrange("p a x -> p a x 1" if False else
                                   "p a x -> p a x").unsqueeze(3)
                .broadcast_to([128, 16, 4, 8])))
            b.link("vector")
            for qc in range(16):
                b.emit("vector", lambda E, qc=qc: E.tensor_tensor_scan(
                    sc_z[:, qc, :], sc_E[:, qc, :], sc_E[:, qc, :], 0.0,
                    ALU.add, ALU.bypass))
                b.emit("vector", lambda E, qc=qc: E.tensor_tensor_scan(
                    sc_m[:, qc, :], sc_B[:, qc, :], sc_B[:, qc, :], -3.0e38,
                    ALU.max, ALU.bypass))
            b.link("vector")
            vz = b.tick_last("vector", f"{tag}v") if False else None
            b.emit("vector", lambda E: E.nop(), tick=f"{tag}v")
            b.wait("scalar", f"{tag}v", b.semval(f"{tag}v"))
            b.emit("scalar", lambda E: E.activation(
                sc_z[:], sc_z[:], AF.Ln), tick=f"{tag}s")
            b.wait("vector", f"{tag}s", b.semval(f"{tag}s"))
            b.emit("vector", lambda E: E.tensor_sub(
                sc_B[:], sc_Mq[:, :, 3:4].broadcast_to([128, 16, 32]),
                sc_m[:]))
            b.link("vector")
            b.emit("vector", lambda E: E.tensor_add(sc_z[:], sc_z[:],
                                                    sc_B[:]))
            b.link("vector")
            for qc in range(16):
                b.emit("vector", lambda E, qc=qc: E.tensor_tensor_scan(
                    sc_m[:, qc, :], sc_z[:, qc, :], sc_z[:, qc, :], 0.0,
                    ALU.add, ALU.bypass))
            b.link("vector")
            b.emit("vector", lambda E: E.tensor_sub(
                sc_B[:], sc_m[:, :, 31:32].broadcast_to([128, 16, 32]),
                sc_m[:]))
            b.link("vector")
            b.emit("vector", lambda E: E.tensor_add(sc_B[:], sc_B[:],
                                                    sc_z[:]))
            b.link("vector")
            b.emit("vector", lambda E: E.scalar_tensor_tensor(
                sc_E[:], sc_B[:], -1.0,
                sc_Mq[:, :, 3:4].broadcast_to([128, 16, 32]),
                ALU.mult, ALU.subtract))
            b.link("vector")
            b.emit("vector", lambda E: E.tensor_copy(hi16[:], sc_E[:]))
            b.link("vector")
            b.emit("vector", lambda E: E.tensor_sub(lo16[:], sc_E[:],
                                                    hi16[:]),
                   tick=f"{tag}v")
            vhl = b.semval(f"{tag}v")
            b.wait("vector", f"{tag}v", vhl)
            for qc in range(16):
                tb = 2 + qc % 2
                if augT_prev[0] is not None:
                    b.wait("vector", *augT_prev[0])
                b.emit("vector", lambda E, qc=qc, aof=aof: E.tensor_copy(
                    aughl[:, aof:aof + 32], hi16[:, qc, :]))
                b.link("vector")
                b.emit("vector", lambda E, qc=qc, aof=aof: E.tensor_copy(
                    aughl[:, aof + 32:aof + 64], lo16[:, qc, :]),
                    tick=f"{tag}v")
                b.wait("tensor", f"{tag}v", b.semval(f"{tag}v"))
                b.wait("tensor", f"bev{tb}", bank_ev[tb])
                b.emit("tensor", lambda E, tb=tb: E.transpose(
                    ps[tb][:, 0:128], aughl[:], idtf[:]))
                bank_mm[tb] = b.tick_last("tensor", f"bmm{tb}")
                augT_prev[0] = (f"bmm{tb}", bank_mm[tb])
                evac("scalar", lambda E, qc=qc, tb=tb, aof=aof, qa=qa: E.copy(
                    qa[aof:aof + 64, qc * 128:(qc + 1) * 128],
                    ps[tb][aof:aof + 64, 0:128]), tb)

        def passB_pre(h):
            b.wait("tensor", f"{tag}aug", augv[h])
            if h == 0:
                b.wait("tensor", f"{tag}ind", 32)
            for pvb in (4, 5, 6, 7):
                b.wait("tensor", f"bev{pvb}", bank_ev[pvb])

        pend_pv = [None]

        def emit_pv(h, kc, qq, ev):
            pi = h % 2
            sp = kc % 2
            bk = 2 + qq % 2
            pvb = 4 + qq
            b.wait("tensor", f"bev{bk}", ev)
            b.emit("tensor", lambda E, kc=kc, qq=qq, sp=sp,
                   h=h, pi=pi, pvb=pvb: E.matmul(
                       ps[pvb][64 * pi:64 * pi + 64, :],
                       v_tm[:, kc, h * 64:(h + 1) * 64],
                       ppt[sp][:, qq * 512:(qq + 1) * 512],
                       start=(kc == 0), stop=(kc == 15),
                       tile_position=(0, 64 * pi)))
            if qq == 3:
                b.tick_last("tensor", f"{tag}pv")

        def passB_sub(h, kc, qq):
            pi = h % 2
            ka, qa = k_aug2[pi], q_aug2[pi]
            sp = kc % 2
            bk = 2 + qq % 2
            mm_group([
                (lambda E, kc=kc, qq=qq, bk=bk, ka=ka, qa=qa: E.matmul(
                    ps[bk][:], ka[:, kc * 128:(kc + 1) * 128],
                    qa[:, qq * 512:(qq + 1) * 512],
                    start=True, stop=True))], bk)
            if pend_pv[0] is not None:
                emit_pv(*pend_pv[0])
                pend_pv[0] = None
            if qq == 0:
                if kc >= 2:
                    b.wait("scalar", f"{tag}pv", b.semval(f"{tag}pv") - 1)
                else:
                    b.wait("scalar", f"{tag}pv", b.semval(f"{tag}pv"))
            ev = evac("scalar", lambda E, qq=qq, bk=bk, sp=sp:
                      E.activation(
                          ppt[sp][:, qq * 512:(qq + 1) * 512],
                          ps[bk][:], AF.Exp), bk)
            pend_pv[0] = (h, kc, qq, ev)

        def passB_tail(h):
            pi = h % 2
            ch = h // 2
            pvv = b.semval(f"{tag}pv")
            for qq in range(4):
                pvb = 4 + qq
                b.wait("scalar", f"{tag}pv", pvv)
                v2 = b.emit("scalar", lambda E, qq=qq, ch=ch, pi=pi,
                            pvb=pvb: E.copy(
                                o_fm[64 * pi:64 * pi + 64, ch,
                                     qq * 512:(qq + 1) * 512],
                                ps[pvb][64 * pi:64 * pi + 64, :]),
                            tick=f"bev{pvb}")
                bank_ev[pvb] = v2
            b.wait("sync", f"{tag}pv", pvv)
            b.emit("sync", lambda E: E.nop(), tick=f"{tag}pB")
            pB_vals[h] = b.semval(f"{tag}pB")

        def passA_loop(h, hB=None):
            pend = []
            i = 0
            for qc in range(16):
                for kq in range(4):
                    ev = passA_mm_max(h, qc, kq)
                    if len(pend) >= 2:
                        passA_sum(*pend.pop(0))
                    pend.append((qc, kq, ev))
                    if hB is not None and i < 56:
                        passB_sub(hB, i // 4, i % 4)
                    i += 1
            for pp in pend:
                passA_sum(*pp)

        def passB_rest(h):
            for j in range(56, 64):
                passB_sub(h, j // 4, j % 4)
            emit_pv(*pend_pv[0])
            pend_pv[0] = None
            passB_tail(h)

        for h in range(H):
            prologue(h)
            if h == 0:
                passA_loop(h)
            else:
                passB_pre(h - 1)
                passA_loop(h, hB=h - 1)
            smalls_augT(h, tail_cb=(
                (lambda hh=h - 1: passB_rest(hh)) if h >= 1 else None))
        passB_pre(H - 1)
        for kc in range(16):
            for qq in range(4):
                passB_sub(H - 1, kc, qq)
        emit_pv(*pend_pv[0])
        pend_pv[0] = None
        passB_tail(H - 1)
        heads_done = [(f"bev{i}", bank_ev[i]) for i in (4, 5, 6, 7)]
        return heads_done

    def attn_wo_make(Ba, Bw, sq, tag, heads_done):
        o_fm = Ba.o_fm
        x1t_b, wo_r = Bw.x1t_b, Bw.wo_r
        row0 = sq * SEQ
        started = [False]

        def pre():
            wol = b.dma(wo_r, wo_b[:, :].rearrange("(c p) f -> p c f",
                                                   p=128), f"{tag}wol")
            b.wait("tensor", f"{tag}wol", wol)
            for nm, v in heads_done:
                b.wait("tensor", nm, v)

        def step(qc):
            if not started[0]:
                pre()
                started[0] = True
            pp = qc % 2
            if qc >= 2:
                b.wait("sync", f"{tag}x2{pp}", b.semval(f"{tag}x2{pp}"))
            lv = b.dma(x1t_b[:, pp],
                       x1d[row0 + qc * 128:row0 + (qc + 1) * 128, :],
                       f"{tag}lb{pp}")
            for half in range(2):
                bk = (qc * 2 + half) % 4
                mms = [
                    (lambda E, mc=mc, qc=qc, half=half, bk=bk: E.matmul(
                        ps[bk][:], o_fm[:, mc, qc * 128:(qc + 1) * 128],
                        wo_r[:, mc, half * 512:(half + 1) * 512],
                        start=(mc == 0), stop=(mc == 7)))
                    for mc in range(8)]
                mm_group(mms, bk)
            b.wait("vector", f"{tag}lb{pp}", lv)
            for half in range(2):
                bk = (qc * 2 + half) % 4
                evac("vector", lambda E, pp=pp, half=half, bk=bk:
                     E.tensor_add(x1t_b[:, pp, half * 512:(half + 1) * 512],
                                  ps[bk][:],
                                  x1t_b[:, pp, half * 512:(half + 1) * 512]),
                     bk)
            b.link("vector")
            b.emit("vector", lambda E: E.nop(), tick=f"{tag}ev")
            b.wait("sync", f"{tag}ev", b.semval(f"{tag}ev"))
            b.dma(x2d[row0 + qc * 128:row0 + (qc + 1) * 128, :],
                  x1t_b[:, pp], f"{tag}x2{pp}")

        def finish():
            return [(f"{tag}x2{0}", b.semval(f"{tag}x2{0}")),
                    (f"{tag}x2{1}", b.semval(f"{tag}x2{1}"))]

        return step, finish

    b.wait("vector", "bev1", bank_ev[1])

    f1_scope = mkscope()
    Bf1 = ffn_bufs(f1_scope)
    st1 = ffn_sublayer(Bf1, x0, x1d, "f1")
    barrier(done=st1)
    f1_scope.close()

    at_scope = mkscope()
    Ba = attn_core_bufs(at_scope)
    ln0_scope = mkscope()
    Bl = attn_ln_bufs(ln0_scope)
    vdone = attn_ln_qkv(Ba, Bl, 0, "a0")
    barrier()
    ln0_scope.close()
    hd0_scope = mkscope()
    Bh = attn_head_bufs(hd0_scope)
    hdone = attn_heads(Ba, Bh, 0, "a0", vdone)
    barrier()
    hd0_scope.close()
    mix_scope = mkscope()
    wovw = alloc(mix_scope, [8, D], BF16)
    Bw = attn_wo_bufs(mix_scope, wo=wovw)
    Bl1 = attn_ln_bufs(mix_scope, vw=wovw)
    wo_step, wo_fin = attn_wo_make(Ba, Bw, 0, "a0", hdone)
    vdone1 = attn_ln_qkv(Ba, Bl1, 1, "a1", mix_cb=wo_step)
    sta0 = wo_fin()
    barrier(done=sta0)
    mix_scope.close()
    hd1_scope = mkscope()
    Bh1 = attn_head_bufs(hd1_scope)
    hdone1 = attn_heads(Ba, Bh1, 1, "a1", vdone1)
    barrier()
    hd1_scope.close()
    wo1_scope = mkscope()
    Bw1 = attn_wo_bufs(wo1_scope)
    wo1_step, wo1_fin = attn_wo_make(Ba, Bw1, 1, "a1", hdone1)
    for qc in range(16):
        wo1_step(qc)
    sta1 = wo1_fin()
    barrier(done=sta1)
    wo1_scope.close()
    at_scope.close()

    f3_scope = mkscope()
    Bf3 = ffn_bufs(f3_scope)
    st3 = ffn_sublayer(Bf3, x2d, y, "f3")
    for nm, v in st3:
        b.wait("sync", nm, v)
    f3_scope.close()

    b.run_block()
    b.ctx.close()
    return nc


def _host_inputs(L0_attn_x, Wq, Wk, Wv, Wo, w_gate, w_up, w_down):
    import ml_dtypes
    consts = np.zeros((128, 8), dtype=np.float32)
    consts[:, 0] = 256.0
    consts[:, 1] = EPS
    consts[:, 2] = 0.125
    ind = np.zeros((64, SEQ), dtype=np.float32)
    for t in range(NKV):
        ind[t, t * 64:(t + 1) * 64] = 1.0
        ind[32 + t, t * 64:(t + 1) * 64] = 1.0
    ind = ind.astype(ml_dtypes.bfloat16)
    ident = np.eye(128, dtype=np.float32).astype(ml_dtypes.bfloat16)
    com = {
        "wq": np.ascontiguousarray(Wq), "wk": np.ascontiguousarray(Wk),
        "wv": np.ascontiguousarray(Wv), "wo": np.ascontiguousarray(Wo),
        "wgT": np.ascontiguousarray(w_gate.T),
        "wuT": np.ascontiguousarray(w_up.T),
        "wdT": np.ascontiguousarray(w_down.T),
        "consts": consts, "indic": ind, "ident": ident,
        "identf": np.eye(128, dtype=np.float32),
    }
    xs = np.ascontiguousarray(L0_attn_x).reshape(8, NTOK, D)
    return [dict(com, x0=np.ascontiguousarray(xs[c])) for c in range(8)]


def kernel(L0_attn_x, L0_attn_Wq, L0_attn_Wk, L0_attn_Wv, L0_attn_Wo,
           L0_ffn_w_gate, L0_ffn_w_up, L0_ffn_w_down):
    in_maps = _host_inputs(
        np.asarray(L0_attn_x, dtype=np.float32),
        np.asarray(L0_attn_Wq, dtype=np.float32),
        np.asarray(L0_attn_Wk, dtype=np.float32),
        np.asarray(L0_attn_Wv, dtype=np.float32),
        np.asarray(L0_attn_Wo, dtype=np.float32),
        np.asarray(L0_ffn_w_gate, dtype=np.float32),
        np.asarray(L0_ffn_w_up, dtype=np.float32),
        np.asarray(L0_ffn_w_down, dtype=np.float32))
    nc = build_program()
    res = run_bass_kernel_spmd(nc, in_maps, core_ids=list(range(8)))
    out = np.stack([r["y"] for r in res.results], axis=0)
    return out.reshape(16, SEQ, D)
